# version 07
# baseline (speedup 1.0000x reference)
import math
import contextlib
import numpy as np
import ml_dtypes
import concourse.bass as bass
import concourse.mybir as mybir
from concourse.bass_utils import run_bass_kernel_spmd

F32 = mybir.dt.float32
BF16 = mybir.dt.bfloat16
AF = mybir.ActivationFunctionType
ALU = mybir.AluOpType
AX = mybir.AxisListType

B, T, C = 4, 4096, 2048
H, HD = 16, 128
NT = 2048
NQT = 16
E_, KTOP, I_ = 16, 6, 1408
ISL = I_ // 128
SI = 2816
SISL = SI // 128
NCH = C // 128
SCALE = 1.0 / math.sqrt(HD)
MASKVAL = -1e9
MTOK = 512
NMACRO = NT // MTOK
BLOB_ELEMS = 189300736


def _perm_half():
    return np.concatenate([np.arange(0, HD, 2), np.arange(1, HD, 2)])

def _slabify(w):
    ci, no = w.shape
    k, o = ci // 128, no // 128
    return np.ascontiguousarray(
        w.reshape(k, 128, o, 128).transpose(2, 1, 0, 3).reshape(o, 128, k * 128))

def host_blobs(ins):
    pe = _perm_half()
    W1 = np.ascontiguousarray(ins["L0_attn_w_qkv"].T).reshape(C, 3, H, HD)
    W1q = W1.copy()
    W1q[:, 0] = W1[:, 0][:, :, pe]
    W1q[:, 1] = W1[:, 1][:, :, pe]
    W1 = W1q.reshape(C, 3 * C)
    parts = []
    parts.append(("qkv", _slabify(W1)))
    parts.append(("woutT", np.ascontiguousarray(ins["L0_attn_w_out"].T)))
    parts.append(("wq", _slabify(ins["L0_ffn_wq"])))
    parts.append(("wk", _slabify(ins["L0_ffn_wk"])))
    parts.append(("wv", _slabify(ins["L0_ffn_wv"])))
    parts.append(("wo", np.ascontiguousarray(ins["L0_ffn_wo"])))
    gt = np.ascontiguousarray(ins["L0_ffn_gate_w"].T)
    parts.append(("gate", np.ascontiguousarray(
        gt.reshape(NCH, 128, 16).transpose(1, 0, 2).reshape(128, NCH * 16))))
    parts.append(("eg", np.ascontiguousarray(
        ins["L0_ffn_eg_w"].reshape(E_, ISL, 128, NCH, 128)
        .transpose(0, 1, 4, 3, 2).reshape(E_, ISL, 128, NCH * 128))))
    parts.append(("eu", np.ascontiguousarray(
        ins["L0_ffn_eu_w"].reshape(E_, ISL, 128, NCH, 128)
        .transpose(0, 1, 4, 3, 2).reshape(E_, ISL, 128, NCH * 128))))
    edT = np.ascontiguousarray(ins["L0_ffn_ed_w"].transpose(0, 2, 1))
    parts.append(("ed4", np.ascontiguousarray(
        edT.reshape(E_, ISL, 128, 4, 512).transpose(0, 2, 3, 1, 4)
        .reshape(E_, 128, 4 * ISL * 512))))
    parts.append(("sg", _slabify(np.ascontiguousarray(ins["L0_ffn_sg_w"].T))))
    parts.append(("su", _slabify(np.ascontiguousarray(ins["L0_ffn_su_w"].T))))
    sdT = np.ascontiguousarray(ins["L0_ffn_sd_w"].T)
    parts.append(("sd4", np.ascontiguousarray(
        sdT.reshape(SISL, 128, 4, 512).transpose(1, 2, 0, 3)
        .reshape(128, 4 * SISL * 512))))
    d = dict(parts)
    groups = {}
    groups["A1"] = d["qkv"].astype(np.float32, copy=False).reshape(-1)
    ga = np.concatenate([d[n].astype(np.float32, copy=False).reshape(-1)
                         for n in ["woutT", "wq", "wk", "wv", "wo",
                                   "gate", "sg", "su", "sd4"]])
    pad = (-ga.size) % (8 * 2048)
    if pad:
        ga = np.concatenate([ga, np.zeros(pad, np.float32)])
    groups["A2"] = ga
    groups["G"] = d["eg"].astype(np.float32, copy=False).reshape(-1)
    groups["U"] = d["eu"].astype(np.float32, copy=False).reshape(-1)
    groups["D"] = d["ed4"].astype(np.float32, copy=False).reshape(-1)
    for k in ("A1", "G", "U", "D"):
        assert groups[k].size % (8 * 2048) == 0
    return groups

def blob_offsets():
    offs = {}
    offs["qkv"] = ("A1", 0, (48, 128, NCH * 128))
    cur = 0
    for name, shape in [
            ("woutT", (C, C)),
            ("wq", (16, 128, NCH * 128)), ("wk", (16, 128, NCH * 128)),
            ("wv", (16, 128, NCH * 128)), ("wo", (C, C)),
            ("gate", (128, NCH * 16)),
            ("sg", (SISL, 128, NCH * 128)), ("su", (SISL, 128, NCH * 128)),
            ("sd4", (128, 4 * SISL * 512))]:
        offs[name] = ("A2", cur, shape)
        cur += int(np.prod(shape))
    ga_elems = cur + ((-cur) % (8 * 2048))
    offs["eg"] = ("G", 0, (E_, ISL, 128, NCH * 128))
    offs["eu"] = ("U", 0, (E_, ISL, 128, NCH * 128))
    offs["ed4"] = ("D", 0, (E_, 128, 4 * ISL * 512))
    return offs, ga_elems

def host_percore(ins):
    x = ins["L0_attn_x"].astype(np.float32, copy=False)
    pe = _perm_half()
    half = HD // 2
    thetas = 10000.0 ** (-np.arange(half, dtype=np.float64) / half)
    ang = np.arange(T, dtype=np.float64)[:, None] * thetas[None, :]
    sinT = np.sin(ang).astype(np.float32)
    cosT = np.cos(ang).astype(np.float32)

    bq = ins["L0_attn_b_qkv"].astype(np.float32).reshape(3, H, HD).copy()
    bq[0] = bq[0][:, pe]
    bq[1] = bq[1][:, pe]
    bqkv_cols = np.ascontiguousarray(bq.reshape(48, 128).T)
    bout_row = ins["L0_attn_b_out"].astype(np.float32).reshape(1, C)
    ln1_tile = np.ascontiguousarray(np.broadcast_to(ins["L0_ffn_ln1_w"].astype(np.float32), (128, C)))
    ln2_tile = np.ascontiguousarray(np.broadcast_to(ins["L0_ffn_ln2_w"].astype(np.float32), (128, C)))

    cores = []
    for core in range(8):
        b, p = core // 2, core % 2
        rows = np.concatenate([np.arange(256 * i + 128 * p, 256 * i + 128 * p + 128)
                               for i in range(NQT)])
        cos_l = cosT[rows].T
        sin_l = sinT[rows].T
        cm = np.zeros((NQT, 128, 512), np.float32)
        for i in range(NQT):
            base = 512 * (i // 2)
            tg = 256 * i + 128 * p + np.arange(128)
            kt = base + np.arange(512)
            cm[i] = np.where(kt[None, :] <= tg[:, None], 0.0, MASKVAL)
        cores.append(dict(
            x=np.ascontiguousarray(x[b, rows]),
            ropeA=np.ascontiguousarray(np.concatenate([cos_l, sin_l], axis=0)),
            ropeB=np.ascontiguousarray(np.concatenate([sin_l, cos_l], axis=0)),
            cmask=np.ascontiguousarray(cm), rows=rows))
    shared = dict(bqkv_cols=bqkv_cols, bout_row=bout_row,
                  ln1_tile=ln1_tile, ln2_tile=ln2_tile)
    return cores, shared


class Ctr:
    def __init__(self, stack, nc, name):
        self.h = stack.enter_context(nc.semaphore(name))
        self.name = name

class Tk:
    def __init__(self, ctr, per=1):
        self.ctr = ctr
        self.map = {}
        self.next = 0
        self.per = per
    def idx(self, key):
        if key not in self.map:
            self.map[key] = self.next
            self.next += 1
        return self.map[key]
    def val(self, key):
        return (self.idx(key) + 1) * self.per
    def inc(self, instr, key):
        i = self.idx(key)
        instr.then_inc(self.ctr.h, self.per)
        return (i + 1) * self.per

class Pipe:
    def __init__(self, banks, fill: Tk, dones):
        self.banks, self.fill, self.dones = banks, fill, dones
        self.done = fill
    def prod(self, eng, key):
        i = self.fill.idx(key)
        n = len(self.banks)
        if i >= n:
            eng.wait_ge(self.dones[i % n].h, i // n)
        return self.banks[i % n]
    def prod_inc(self, instr, key):
        self.fill.inc(instr, key)
    def cons(self, eng, key):
        i = self.fill.idx(key)
        eng.wait_ge(self.fill.ctr.h, i + 1)
        return self.banks[i % len(self.banks)]
    def cons_inc(self, instr, key):
        i = self.fill.idx(key)
        instr.then_inc(self.dones[i % len(self.banks)].h, 1)
    def done_val(self, key):
        i = self.fill.idx(key)
        return (self.dones[i % len(self.banks)], i // len(self.banks) + 1)

class Stream:
    def __init__(self, bufs, lds, fr, ndma=1):
        self.bufs, self.lds, self.fr, self.ndma = bufs, lds, fr, ndma
        self.nfill = 0
        self.ncons = 0
        self.nfr = 0
    def load(self, eng, fill):
        s = self.nfill % len(self.bufs)
        if self.nfill >= len(self.bufs):
            eng.wait_ge(self.fr.h, self.nfill - len(self.bufs) + 1)
        fill(eng, self.bufs[s], self.lds[s])
        self.nfill += 1
    def buf(self, i):
        return self.bufs[i % len(self.bufs)]
    def wait(self, eng, i):
        n = len(self.bufs)
        eng.wait_ge(self.lds[i % n].h, 16 * self.ndma * (i // n + 1))
    def wait_all(self, eng, nfills):
        n = len(self.bufs)
        for s in range(n):
            cnt = len([i for i in range(nfills) if i % n == s])
            if cnt:
                eng.wait_ge(self.lds[s].h, 16 * self.ndma * cnt)
    def release_on(self, instr):
        instr.then_inc(self.fr.h, 1)
        self.nfr += 1


def build(upto="E", debug=()):
    nc = bass.Bass("TRN2", target_bir_lowering=False)
    upto_i = "ABCDE".index(upto)

    OFFS, GA_ELEMS = blob_offsets()
    GKEYS = ["A1", "A2", "G", "U", "D"]
    GSIZES = {"A1": 48 * 128 * NCH * 128, "A2": GA_ELEMS,
              "G": 46137344, "U": 46137344, "D": 46137344}
    SHARDS = {k: v // 8 for k, v in GSIZES.items()}

    x_in = nc.dram_tensor("x", [NT, C], F32, kind="ExternalInput").ap()
    wshard = {k: nc.dram_tensor(f"wshard{k}", [SHARDS[k]], F32, kind="ExternalInput").ap()
              for k in GKEYS}
    ropeA_in = nc.dram_tensor("ropeA", [128, NT], F32, kind="ExternalInput").ap()
    ropeB_in = nc.dram_tensor("ropeB", [128, NT], F32, kind="ExternalInput").ap()
    cm_in = nc.dram_tensor("cmask", [NQT, 128, 512], F32, kind="ExternalInput").ap()
    bqkv_in = nc.dram_tensor("bqkv_cols", [128, 48], F32, kind="ExternalInput").ap()
    bout_in = nc.dram_tensor("bout_row", [1, C], F32, kind="ExternalInput").ap()
    ln1_in = nc.dram_tensor("ln1_tile", [128, C], F32, kind="ExternalInput").ap()
    ln2_in = nc.dram_tensor("ln2_tile", [128, C], F32, kind="ExternalInput").ap()

    shard_bf = {k: nc.dram_tensor(f"shard_bf{k}", [SHARDS[k]], BF16, kind="Internal").ap()
                for k in GKEYS}
    blob_t = {k: nc.dram_tensor(f"blob{k}", [GSIZES[k] // 2048, 2048], BF16, kind="Internal")
              for k in GKEYS}
    blob = {k: blob_t[k].ap().rearrange("a b -> (a b)") for k in GKEYS}
    agin = nc.dram_tensor("agin", [2 * C, NT], BF16, kind="Internal").ap()
    agout = nc.dram_tensor("agout", [4 * C, NT], BF16, kind="Internal").ap()
    qbuf = nc.dram_tensor("qbuf", [C, NT], BF16, kind="Internal").ap()
    attv = nc.dram_tensor("attv", [C, NT], BF16, kind="Internal").ap()
    x1_hbm = nc.dram_tensor("x1_hbm", [NT, C], F32, kind="Internal").ap()
    xf_hbm = nc.dram_tensor("xf_hbm", [NT, C], F32, kind="Internal").ap()
    cw_dram = nc.dram_tensor("cw_dram", [E_, MTOK], F32, kind="Internal").ap()
    out_hbm = nc.dram_tensor("y", [NT, C], F32, kind="ExternalOutput").ap()

    dbg = {}
    for name, shape, dt in [("agin", [2 * C, NT], BF16), ("q", [C, NT], BF16),
                            ("attv", [C, NT], BF16), ("x1", [NT, C], F32),
                            ("xf", [NT, C], F32), ("agout", [4 * C, NT], BF16),
                            ("cw", [E_, MTOK], F32)]:
        if name in debug:
            dbg[name] = nc.dram_tensor("dbg_" + name, shape, dt, kind="ExternalOutput").ap()

    id_bf = nc.inline_tensor(np.eye(128, dtype=ml_dtypes.bfloat16), name="id_bf").ap()
    id_f = nc.inline_tensor(np.eye(128, dtype=np.float32), name="id_f").ap()
    ones_col = nc.inline_tensor(np.ones((128, 1), dtype=ml_dtypes.bfloat16), name="ones_col").ap()
    ones_row_f = nc.inline_tensor(np.ones((1, 128), dtype=np.float32), name="ones_row_f").ap()
    ones_row_bf = nc.inline_tensor(np.ones((1, 128), dtype=ml_dtypes.bfloat16), name="ones_row_bf").ap()
    eps_np = np.zeros((128, 2), np.float32)
    eps_np[:, 0] = 1e-5
    eps_np[:, 1] = 1e-6
    eps_in = nc.inline_tensor(eps_np, name="epsc").ap()

    st = contextlib.ExitStack()
    with st:
        SB = lambda n, s, d: st.enter_context(nc.sbuf_tensor(n, s, d))
        PS = lambda n, s, d: st.enter_context(nc.psum_tensor(n, s, d))
        id_bf_s = SB("id_bf_s", [128, 128], BF16)
        id_f_s = SB("id_f_s", [128, 128], F32)
        ones_col_s = SB("ones_col_s", [128, 1], BF16)
        ones_row_f_s = SB("ones_row_f_s", [1, 128], F32)
        ones_row_bf_s = SB("ones_row_bf_s", [1, 128], BF16)
        eps_s = SB("eps_s", [128, 2], F32)
        bqkv_s = SB("bqkv_s", [128, 48], F32)
        bout_bf = SB("bout_bf", [1, C], BF16)
        ln1_bf = SB("ln1_bf", [128, C], BF16)
        ln2_bf = SB("ln2_bf", [128, C], BF16)
        gate_sb = SB("gate_sb", [128, NCH * 16], BF16)
        stat = SB("stat", [128, 16], F32)
        stage = [SB(f"stage{i}", [128, C], BF16) for i in range(3)]

        psA = PS("psA", [128, 512], F32)
        psB = PS("psB", [128, 512], F32)
        psC = PS("psC", [128, 512], F32)
        psD = PS("psD", [128, 512], F32)
        psT1 = PS("psT1", [128, 1024], BF16)
        psT2 = PS("psT2", [128, 1024], BF16)
        psO = PS("psO", [128, 512], F32)
        psM = PS("psM", [128, 512], F32)

        Cm = lambda name: Ctr(st, nc, name)
        v_self = Cm("v_self"); v_n = [0]
        a_self = Cm("a_self"); a_n = [0]
        g_self = Cm("g_self"); g_n = [0]

        def selfinc(ctr, nref, instr):
            instr.then_inc(ctr.h, 1)
            nref[0] += 1
            return nref[0]

        s_castA1 = Cm("s_castA1")
        s_castR = Cm("s_castR")
        s_cast = {k: (s_castA1 if k == "A1" else s_castR) for k in GKEYS}
        def cast_done(g, gk):
            if gk == "A1":
                g.wait_ge(s_castA1.h, 16 * 8)
            else:
                g.wait_ge(s_castR.h, 16 * 32)
        s_ag = Cm("s_ag")
        s_gate = Cm("s_gate")
        s_consts = Cm("s_consts")
        s_cconst = Cm("s_cconst")
        st_stp = [Cm(f"st_stp{k}") for k in range(4)]
        def stw_all(eng, n):
            for k in range(4):
                c = (n - k + 3) // 4
                if c:
                    eng.wait_ge(st_stp[k].h, 16 * c)
        def stw_one(eng, s):
            eng.wait_ge(st_stp[s % 4].h, 16 * (s // 4 + 1))
        def stw_pre(eng, s):
            if s >= 4:
                eng.wait_ge(st_stp[s % 4].h, 16 * (s // 4))
        def stw_inc(d, s):
            d.then_inc(st_stp[s % 4].h, 16)
        def stw_pairs(n):
            return [(st_stp[k], 16 * ((n - k + 3) // 4)) for k in range(4)
                    if (n - k + 3) // 4 > 0]
        st_attv = Cm("st_attv")
        st_x1 = Cm("st_x1")
        st_xf = Cm("st_xf")
        st_out = Cm("st_out")
        st_cw = Cm("st_cw")
        st_dbg = Cm("st_dbg")

        main = Pipe([psA, psB], Tk(Cm("main_f")), [Cm("main_d0"), Cm("main_d1")])
        alt = Pipe([psC, psD], Tk(Cm("alt_f")), [Cm("alt_d0"), Cm("alt_d1")])
        misc = Pipe([psO, psM], Tk(Cm("misc_f")), [Cm("misc_d0"), Cm("misc_d1")])
        tpp = Pipe([psT1, psT2], Tk(Cm("tp_f")), [Cm("tp_d0"), Cm("tp_d1")])

        M = {}
        for nm in ["la_sq", "la_stat", "la_sd", "la_xhat", "la_fm", "la_fmT",
                   "at_sq", "at_sqrt", "at_knorm", "at_mrow", "at_mhat",
                   "at_smask", "at_exp", "at_linv", "at_prowsc", "at_pt",
                   "at_ptmm", "at_pvmm", "at_pvp", "at_normmm", "at_av",
                   "c_stage",
                   "e_acc", "e_that", "e_tfm", "e_glog", "e_gexp", "e_gcw",
                   "e_gT", "e_cwb", "e_sg", "e_hh", "e_dn", "e_dnmm", "e_fin"]:
            M[nm] = Tk(Cm(nm))

        def mkstream(name, nslots, ndma=1):
            return Stream([], [Cm(f"{name}_ld{i}") for i in range(nslots)],
                          Cm(name + "_fr"), ndma)

        xin = mkstream("xin", 2)
        slab = mkstream("slab", 3)
        Kst = mkstream("Kst", 2, ndma=8)
        Vst = mkstream("Vst", 2, ndma=8)
        Qst = mkstream("Qst", 2)
        avst = mkstream("avst", 2)
        wrow = mkstream("wrow", 3)
        egst = mkstream("egst", 2)
        eust = mkstream("eust", 2)
        edst = mkstream("edst", 2)
        cwrow = mkstream("cwrow", 2)

        block = st.enter_context(nc.Block())

        def bview(name, idx=()):
            grp, off, shape = OFFS[name]
            for k2, i2 in enumerate(idx):
                off += i2 * int(np.prod(shape[k2 + 1:]))
            sub = shape[len(idx):]
            assert len(sub) == 2
            r, c = sub
            return blob[grp][off: off + r * c].rearrange("(a b) -> a b", b=c)

        @block.gpsimd
        def _(g):
            for gk in GKEYS:
                chunk = SHARDS[gk] // 8
                for i in range(8):
                    g.dma_start(out=shard_bf[gk][i * chunk:(i + 1) * chunk],
                                in_=wshard[gk][i * chunk:(i + 1) * chunk]).then_inc(s_cast[gk].h, 16)
            for gi, gk in enumerate(["A1", "A2"]):
                cast_done(g, gk)
                g.collective_compute(
                    "AllGather", ALU.bypass,
                    replica_groups=[[0, 1, 2, 3, 4, 5, 6, 7]],
                    ins=[shard_bf[gk].rearrange("(a b) -> a b", b=2048)],
                    outs=[blob_t[gk].ap()]).then_inc(s_ag.h, 1)
            g.dma_start(out=bout_bf[:], in_=bout_in[:]).then_inc(s_cconst.h, 16)
            g.dma_start(out=ln1_bf[:], in_=ln1_in[:]).then_inc(s_cconst.h, 16)
            g.dma_start(out=ln2_bf[:], in_=ln2_in[:]).then_inc(s_cconst.h, 16)

        @block.sync
        def _(sy):
            for dst, src in [(id_bf_s, id_bf), (id_f_s, id_f), (ones_col_s, ones_col),
                             (ones_row_f_s, ones_row_f), (ones_row_bf_s, ones_row_bf),
                             (eps_s, eps_in), (bqkv_s, bqkv_in)]:
                sy.dma_start(out=dst[:], in_=src[:]).then_inc(s_consts.h, 16)
        CONSTS = 7 * 16

        stA = contextlib.ExitStack()
        SA = lambda n, s, d: stA.enter_context(nc.sbuf_tensor(n, s, d))
        fmbuf = SA("fmbufA", [128, NCH * NT], BF16)
        fmv = fmbuf[:].rearrange("p (c n) -> p c n", c=NCH)
        sq_s = [SA(f"sq_sA{i}", [128, C], F32) for i in range(2)]
        xhat_s = [SA(f"xhat_sA{i}", [128, C], BF16) for i in range(2)]
        ropeA_s = SA("ropeA_s", [128, NT], F32)
        ropeB_s = SA("ropeB_s", [128, NT], F32)
        rope_tmpA = SA("rope_tmpA", [128, 512], F32)
        rope_tmpB = SA("rope_tmpB", [128, 512], F32)
        xin.bufs = [SA(f"xinA{i}", [128, C], F32) for i in range(2)]
        slab.bufs = [SA(f"slabA{i}", [128, NCH * 128], BF16) for i in range(3)]

        A_ORDER0 = [*range(16, 48), *range(0, 16)]
        for tt in range(NQT):
            for nm in ["la_sq", "la_stat", "la_sd", "la_xhat", "la_fmT", "la_fm"]:
                M[nm].idx(f"A{tt}")
        for oi in range(48):
            M["c_stage"].idx(f"A{oi}")

        def a_store(sy, oi):
            o = A_ORDER0[oi]
            sel, head = divmod(o, 16)
            stg = stage[oi % 3]
            sy.wait_ge(M["c_stage"].ctr.h, M["c_stage"].val(f"A{oi}"))
            stw_pre(sy, oi)
            if sel == 0:
                d = sy.dma_start(out=qbuf[head * 128:(head + 1) * 128, :], in_=stg[:])
            elif sel == 1:
                d = sy.dma_start(out=agin[head * 128:(head + 1) * 128, :], in_=stg[:])
            else:
                d = sy.dma_start(out=agin[C + head * 128:C + (head + 1) * 128, :], in_=stg[:])
            stw_inc(d, oi)

        @block.sync
        def _(sy):
            sy.wait_ge(s_consts.h, CONSTS)
            sy.dma_start(out=ropeA_s[:], in_=ropeA_in[:]).then_inc(s_consts.h, 16)
            sy.dma_start(out=ropeB_s[:], in_=ropeB_in[:]).then_inc(s_consts.h, 16)
            for tt in range(NQT):
                def fill(eng, buf, ld, tt=tt):
                    eng.dma_start(out=buf[:], in_=x_in[tt * 128:(tt + 1) * 128, :]
                                  ).then_inc(ld.h, 16)
                xin.load(sy, fill)
            sy.wait_ge(s_ag.h, 1)
            for oi in range(48):
                o = A_ORDER0[oi]
                def fill(eng, buf, ld, o=o):
                    eng.dma_start(out=buf[:], in_=bview("qkv", (o,))).then_inc(ld.h, 16)
                slab.load(sy, fill)
                if oi >= 2:
                    a_store(sy, oi - 2)
            a_store(sy, 46)
            a_store(sy, 47)
        CONSTS_A = CONSTS + 32

        @block.scalar
        def _(s):
            s.wait_ge(s_consts.h, CONSTS)
            for tt in range(NQT):
                xin.wait(s, tt)
                if tt >= 2:
                    s.wait_ge(M["la_stat"].ctr.h, M["la_stat"].val(f"A{tt - 2}"))
                a = s.activation(sq_s[tt % 2][:], xin.buf(tt)[:], AF.Square)
                M["la_sq"].inc(a, f"A{tt}")
                s.wait_ge(M["la_stat"].ctr.h, M["la_stat"].val(f"A{tt}"))
                a = s.activation(stat[:, 6:7], stat[:, 5:6], AF.Sqrt, bias=eps_s[:, 0:1])
                M["la_sd"].inc(a, f"A{tt}")

        @block.vector
        def _(v):
            for tt in range(NQT):
                xin.wait(v, tt)
                v.wait_ge(M["la_sq"].ctr.h, M["la_sq"].val(f"A{tt}"))
                v.reduce_sum(stat[:, 0:1], xin.buf(tt)[:], axis=AX.X)
                r2 = v.reduce_sum(stat[:, 1:2], sq_s[tt % 2][:], axis=AX.X)
                n = selfinc(v_self, v_n, r2)
                v.wait_ge(v_self.h, n)
                v.tensor_scalar(stat[:, 2:3], stat[:, 0:1], 1.0 / C, None, ALU.mult)
                t2 = v.tensor_scalar(stat[:, 3:4], stat[:, 1:2], 1.0 / C, None, ALU.mult)
                n = selfinc(v_self, v_n, t2)
                v.wait_ge(v_self.h, n)
                t3 = v.tensor_mul(stat[:, 4:5], stat[:, 2:3], stat[:, 2:3])
                n = selfinc(v_self, v_n, t3)
                v.wait_ge(v_self.h, n)
                t5 = v.tensor_sub(stat[:, 5:6], stat[:, 3:4], stat[:, 4:5])
                M["la_stat"].inc(t5, f"A{tt}")
                v.wait_ge(M["la_sd"].ctr.h, M["la_sd"].val(f"A{tt}"))
                rc = v.reciprocal(stat[:, 7:8], stat[:, 6:7])
                n = selfinc(v_self, v_n, rc)
                v.wait_ge(v_self.h, n)
                if tt >= 2:
                    v.wait_ge(M["la_fmT"].ctr.h, M["la_fmT"].val(f"A{tt - 2}"))
                tsx = v.tensor_scalar(xhat_s[tt % 2][:], xin.buf(tt)[:],
                                      stat[:, 2:3], stat[:, 7:8], ALU.subtract, ALU.mult)
                M["la_xhat"].inc(tsx, f"A{tt}")
                v.wait_ge(M["la_xhat"].ctr.h, M["la_xhat"].val(f"A{tt}"))
                xin.release_on(v.nop())
                xin.ncons += 1
                for ett in ([tt - 1] if tt >= 1 else []) + ([tt] if tt == NQT - 1 else []):
                    for half in range(2):
                        key = f"Afm{ett}_{half}"
                        bank = tpp.cons(v, key)
                        cp = v.tensor_copy(
                            fmv[:, half * 8:(half + 1) * 8, ett * 128:(ett + 1) * 128],
                            bank[:].rearrange("p (j n) -> p j n", j=8))
                        tpp.cons_inc(cp, key)
                    dc, dv_ = tpp.done_val(key)
                    v.wait_ge(dc.h, dv_)
                    M["la_fm"].inc(v.nop(), f"A{ett}")

        @block.tensor
        def _(t):
            t.wait_ge(s_consts.h, CONSTS_A)
            for tt in range(NQT):
                t.wait_ge(M["la_xhat"].ctr.h, M["la_xhat"].val(f"A{tt}"))
                for half in range(2):
                    key = f"Afm{tt}_{half}"
                    bank = tpp.prod(t, key)
                    for j in range(8):
                        c = half * 8 + j
                        mm = t.transpose(bank[:, j * 128:(j + 1) * 128],
                                         xhat_s[tt % 2][:, c * 128:(c + 1) * 128], id_bf_s[:])
                    tpp.prod_inc(mm, key)
                t.wait_ge(tpp.fill.ctr.h, tpp.fill.val(key))
                M["la_fmT"].inc(t.nop(), f"A{tt}")

        A_ORDER = [*range(16, 48), *range(0, 16)]

        @block.tensor
        def _(t):
            t.wait_ge(M["la_fm"].ctr.h, M["la_fm"].val(f"A{NQT - 1}"))
            for oi in range(48):
                slab.wait(t, oi)
                sb = slab.buf(oi)
                for mt in range(4):
                    key = f"Aqkv{oi}_{mt}"
                    bank = main.prod(t, key)
                    for k in range(NCH):
                        mm = t.matmul(bank[:], sb[:, k * 128:(k + 1) * 128],
                                      fmv[:, k, mt * 512:(mt + 1) * 512],
                                      start=(k == 0), stop=(k == NCH - 1))
                    main.prod_inc(mm, key)
                t.wait_ge(main.fill.ctr.h, main.fill.val(key))
                slab.release_on(t.nop())
                slab.ncons += 1

        @block.vector
        def _(v):
            v.wait_ge(s_consts.h, CONSTS_A)
            for oi in range(48):
                o = A_ORDER[oi]
                sel = o // 16
                stg = stage[oi % 3]
                if oi >= 3:
                    stw_one(v, oi - 3)
                for mt in range(4):
                    key = f"Aqkv{oi}_{mt}"
                    bank = main.cons(v, key)
                    ms = stg[:, mt * 512:(mt + 1) * 512]
                    bcol = bqkv_s[:, o:o + 1]
                    if sel == 2:
                        ev = v.tensor_scalar(ms, bank[:], bcol, None, ALU.add)
                        n = selfinc(v_self, v_n, ev)
                        v.wait_ge(v_self.h, n)
                        main.cons_inc(v.nop(), key)
                    else:
                        TAm = ropeA_s[:, mt * 512:(mt + 1) * 512]
                        TBm = ropeB_s[:, mt * 512:(mt + 1) * 512]
                        p1 = v.scalar_tensor_tensor(rope_tmpA[0:64, :], bank[0:64, :],
                                                    bcol[0:64, :], TAm[0:64, :], ALU.add, ALU.mult)
                        selfinc(v_self, v_n, p1)
                        p2 = v.scalar_tensor_tensor(rope_tmpB[0:64, :], bank[64:128, :],
                                                    bcol[64:128, :], TAm[64:128, :], ALU.add, ALU.mult)
                        selfinc(v_self, v_n, p2)
                        p3 = v.scalar_tensor_tensor(rope_tmpA[64:128, :], bank[0:64, :],
                                                    bcol[0:64, :], TBm[0:64, :], ALU.add, ALU.mult)
                        selfinc(v_self, v_n, p3)
                        p4 = v.scalar_tensor_tensor(rope_tmpB[64:128, :], bank[64:128, :],
                                                    bcol[64:128, :], TBm[64:128, :], ALU.add, ALU.mult)
                        n = selfinc(v_self, v_n, p4)
                        v.wait_ge(v_self.h, n)
                        main.cons_inc(v.nop(), key)
                        r1 = v.tensor_sub(ms[0:64, :], rope_tmpA[0:64, :], rope_tmpB[0:64, :])
                        selfinc(v_self, v_n, r1)
                        r2 = v.tensor_add(ms[64:128, :], rope_tmpA[64:128, :], rope_tmpB[64:128, :])
                        n = selfinc(v_self, v_n, r2)
                        v.wait_ge(v_self.h, n)
                M["c_stage"].inc(v.nop(), f"A{oi}")

        A_KV_ST = 32
        A_ST_TOTAL = 48

        @block.gpsimd
        def _(g):
            stw_all(g, A_KV_ST)
            g.collective_compute(
                "AllGather", ALU.bypass,
                replica_groups=[[0, 1], [2, 3], [4, 5], [6, 7]],
                ins=[agin[:]], outs=[agout[:]]).then_inc(s_ag.h, 1)
            for gk in ["G", "U", "D"]:
                cast_done(g, gk)
                g.collective_compute(
                    "AllGather", ALU.bypass,
                    replica_groups=[[0, 1, 2, 3, 4, 5, 6, 7]],
                    ins=[shard_bf[gk].rearrange("(a b) -> a b", b=2048)],
                    outs=[blob_t[gk].ap()]).then_inc(s_ag.h, 1)

        if upto_i <= 1 and ("agin" in dbg or "q" in dbg):
            @block.sync
            def _(sy):
                stw_all(sy, A_ST_TOTAL)
                if "agin" in dbg:
                    sy.dma_start(out=dbg["agin"][:], in_=agin[:]).then_inc(st_dbg.h, 16)
                if "q" in dbg:
                    sy.dma_start(out=dbg["q"][:], in_=qbuf[:]).then_inc(st_dbg.h, 16)
        if "agout" in dbg:
            @block.sync
            def _(sy):
                sy.wait_ge(s_ag.h, 3)
                sy.dma_start(out=dbg["agout"][:], in_=agout[:]).then_inc(st_dbg.h, 16)

        if upto_i == 0:
            stA.close()

        def emit_attention(tag, ag_val, qst_store_base, av_dst, extra_waits,
                           prev_avld_fills):
            stX = contextlib.ExitStack()
            SX = lambda n, s, d: stX.enter_context(nc.sbuf_tensor(n, s, d))
            Kst.bufs = [SX(f"K{tag}{i}", [128, T], BF16) for i in range(2)]
            Vst.bufs = [SX(f"V{tag}{i}", [128, T], BF16) for i in range(2)]
            Qst.bufs = [SX(f"Q{tag}{i}", [128, NT], BF16) for i in range(2)]
            VT_s = SX(f"VT{tag}", [128, T], BF16)
            Ksq_s = SX(f"Ksq{tag}", [128, T], BF16)
            Qsq_s = SX(f"Qsq{tag}", [128, NT], BF16)
            kn2row = SX(f"kn2row{tag}", [1, T], F32)
            qn2row = SX(f"qn2row{tag}", [1, NT], F32)
            knmax = SX(f"knmax{tag}", [1, 1], F32)
            mrow = SX(f"mrow{tag}", [1, NT], F32)
            mhat_cols = SX(f"mhat{tag}", [128, NQT], F32)
            cm_s = SX(f"cm{tag}", [128, NQT * 512], F32)
            Prow = [SX(f"Prow{tag}{i}", [128, T], BF16) for i in range(3)]
            Smask = [SX(f"Smask{tag}{i}", [128, 512], F32) for i in range(3)]
            PT_s = SX(f"PT{tag}", [128, 2 * T], BF16)
            lparts = SX(f"lparts{tag}", [128, NQT * 8], F32)
            l_cols = SX(f"lcols{tag}", [128, NQT], F32)
            linv_cols = SX(f"linv{tag}", [128, NQT], F32)
            av_big = [SX(f"avbig{tag}{i}", [128, C], BF16) for i in range(2)]

            kb0, vb0, qb0 = Kst.nfill, Vst.nfill, Qst.nfill
            for h in range(H):
                for nm in ["at_sq", "at_sqrt", "at_knorm", "at_mrow", "at_mhat",
                           "at_normmm", "at_pvmm"]:
                    M[nm].idx(f"{tag}{h}")
                for i in range(NQT):
                    for nm in ["at_smask", "at_exp", "at_linv", "at_prowsc",
                               "at_pt", "at_ptmm"]:
                        M[nm].idx(f"{tag}{h}_{i}")
                for jp in range(8):
                    M["at_pvp"].idx(f"{tag}{h}_{jp}")
                    M["at_av"].idx(f"{tag}{h}_{jp}")
            g0 = g_n[0]
            atv0 = ATV_BASE[tag]

            def att_store(sy, h):
                sy.wait_ge(M["at_av"].ctr.h, M["at_av"].val(f"{tag}{h}_7"))
                sy.wait_ge(st_attv.h, 16 * (atv0 + h))
                sy.dma_start(out=av_dst[h * 128:(h + 1) * 128, :],
                             in_=av_big[h % 2][:]).then_inc(st_attv.h, 16)

            @block.sync
            def _(sy):
                sy.wait_ge(s_ag.h, ag_val)
                for ctr, val in extra_waits:
                    sy.wait_ge(ctr.h, val)
                if prev_avld_fills:
                    avst.wait_all(sy, prev_avld_fills)
                sy.dma_start(out=cm_s[:].rearrange("p (a c) -> p a c", a=NQT),
                             in_=cm_in.rearrange("a b c -> b a c")
                             ).then_inc(s_consts.h, 16)
                for h in range(H):
                    if h >= 1:
                        att_store(sy, h - 1)
                    def fillK(eng, buf, ld, h=h):
                        for par in range(2):
                            for q4 in range(4):
                                eng.dma_start(
                                    out=buf[:].rearrange("p (g two n) -> p g two n", two=2, n=128)[:, q4 * 4:(q4 + 1) * 4, par, :],
                                    in_=agout[par * 2 * C + h * 128: par * 2 * C + (h + 1) * 128,
                                              q4 * 512:(q4 + 1) * 512]
                                    .rearrange("p (g n) -> p g n", n=128)
                                ).then_inc(ld.h, 16)
                    Kst.load(sy, fillK)
                    def fillV(eng, buf, ld, h=h):
                        for par in range(2):
                            for q4 in range(4):
                                eng.dma_start(
                                    out=buf[:].rearrange("p (g two n) -> p g two n", two=2, n=128)[:, q4 * 4:(q4 + 1) * 4, par, :],
                                    in_=agout[par * 2 * C + C + h * 128: par * 2 * C + C + (h + 1) * 128,
                                              q4 * 512:(q4 + 1) * 512]
                                    .rearrange("p (g n) -> p g n", n=128)
                                ).then_inc(ld.h, 16)
                    Vst.load(sy, fillV)
                    def fillQ(eng, buf, ld, h=h):
                        stw_one(eng, qst_store_base + h)
                        eng.dma_start(out=buf[:], in_=qbuf[h * 128:(h + 1) * 128, :]
                                      ).then_inc(ld.h, 16)
                    Qst.load(sy, fillQ)

            @block.tensor
            def _(t):
                for h in range(H):
                    K_ = Kst.buf(kb0 + h)
                    V_ = Vst.buf(vb0 + h)
                    Q_ = Qst.buf(qb0 + h)
                    Vst.wait(t, vb0 + h)
                    if h > 0:
                        t.wait_ge(M["at_pvmm"].ctr.h, M["at_pvmm"].val(f"{tag}{h - 1}"))
                    for grp in range(4):
                        key = f"{tag}vt{h}_{grp}"
                        bank = tpp.prod(t, key)
                        for j in range(8):
                            jb = grp * 8 + j
                            mm = t.transpose(bank[:, j * 128:(j + 1) * 128],
                                             V_[:, jb * 128:(jb + 1) * 128], id_bf_s[:])
                        tpp.prod_inc(mm, key)
                    Kst.wait(t, kb0 + h)
                    Qst.wait(t, qb0 + h)
                    t.wait_ge(M["at_sq"].ctr.h, M["at_sq"].val(f"{tag}{h}"))
                    for kb in range(8):
                        key = f"{tag}kn{h}_{kb}"
                        bank = misc.prod(t, key)
                        mm = t.matmul(bank[0:1, :], ones_col_s[:],
                                      Ksq_s[:, kb * 512:(kb + 1) * 512], start=True, stop=True)
                        misc.prod_inc(mm, key)
                    for qb in range(4):
                        key = f"{tag}qn{h}_{qb}"
                        bank = misc.prod(t, key)
                        mm = t.matmul(bank[0:1, :], ones_col_s[:],
                                      Qsq_s[:, qb * 512:(qb + 1) * 512], start=True, stop=True)
                        misc.prod_inc(mm, key)
                    t.wait_ge(misc.fill.ctr.h, misc.fill.val(f"{tag}qn{h}_3"))
                    M["at_normmm"].inc(t.nop(), f"{tag}{h}")
                    t.wait_ge(M["at_mrow"].ctr.h, M["at_mrow"].val(f"{tag}{h}"))
                    key = f"{tag}mh{h}"
                    bank = misc.prod(t, key)
                    for i in range(NQT):
                        mm = t.transpose(bank[:, i:i + 1], mrow[:, i * 128:(i + 1) * 128],
                                         id_f_s[0:1, 0:1])
                    misc.prod_inc(mm, key)
                    t.wait_ge(M["at_mhat"].ctr.h, M["at_mhat"].val(f"{tag}{h}"))
                    for i in range(NQT + 2):
                        if i < NQT:
                            Pi = 512 * (i // 2 + 1)
                            for kb in range(Pi // 512):
                                key = f"{tag}S{h}_{i}_{kb}"
                                bank = main.prod(t, key)
                                mm = t.matmul(bank[:], Q_[:, i * 128:(i + 1) * 128],
                                              K_[:, kb * 512:(kb + 1) * 512],
                                              start=True, stop=True)
                                main.prod_inc(mm, key)
                        if i >= 2:
                            ip = i - 2
                            Pp = 512 * (ip // 2 + 1)
                            t.wait_ge(M["at_prowsc"].ctr.h,
                                      M["at_prowsc"].val(f"{tag}{h}_{ip}"))
                            nb128 = Pp // 128
                            pr = Prow[ip % 3]
                            for grp in range((nb128 + 7) // 8):
                                key = f"{tag}pt{h}_{ip}_{grp}"
                                bank = tpp.prod(t, key)
                                nj = min(8, nb128 - grp * 8)
                                for j in range(nj):
                                    jb = grp * 8 + j
                                    mm = t.transpose(bank[:, j * 128:(j + 1) * 128],
                                                     pr[:, jb * 128:(jb + 1) * 128], id_bf_s[:])
                                tpp.prod_inc(mm, key)
                            t.wait_ge(tpp.fill.ctr.h, tpp.fill.val(key))
                            M["at_ptmm"].inc(t.nop(), f"{tag}{h}_{ip}")
                            if ip % 2 == 1:
                                jp = ip // 2
                                t.wait_ge(M["at_pt"].ctr.h, M["at_pt"].val(f"{tag}{h}_{ip}"))
                                nkb = 4 * (jp + 1)
                                key = f"{tag}pv{h}_{jp}"
                                bank = alt.prod(t, key)
                                for jb in range(nkb):
                                    mm = t.matmul(bank[:, 0:256],
                                                  VT_s[:, jb * 128:(jb + 1) * 128],
                                                  PT_s[:, jb * 256:(jb + 1) * 256],
                                                  start=(jb == 0), stop=(jb == nkb - 1))
                                alt.prod_inc(mm, key)
                                t.wait_ge(alt.fill.ctr.h, alt.fill.val(key))
                                M["at_pvp"].inc(t.nop(), f"{tag}{h}_{jp}")
                    M["at_pvmm"].inc(t.nop(), f"{tag}{h}")
                    Kst.release_on(t.nop()); Kst.ncons += 1
                    Vst.release_on(t.nop()); Vst.ncons += 1
                    Qst.release_on(t.nop()); Qst.ncons += 1

            @block.scalar
            def _(s):
                s.wait_ge(s_consts.h, consts_after_att[tag])
                for h in range(H):
                    K_ = Kst.buf(kb0 + h)
                    Q_ = Qst.buf(qb0 + h)
                    Kst.wait(s, kb0 + h)
                    Qst.wait(s, qb0 + h)
                    if h > 0:
                        s.wait_ge(M["at_normmm"].ctr.h, M["at_normmm"].val(f"{tag}{h - 1}"))
                    s.activation(Ksq_s[:], K_[:], AF.Square)
                    qsq = s.activation(Qsq_s[:], Q_[:], AF.Square)
                    M["at_sq"].inc(qsq, f"{tag}{h}")
                    s.wait_ge(M["at_knorm"].ctr.h, M["at_knorm"].val(f"{tag}{h}"))
                    s.activation(knmax[:], knmax[:], AF.Sqrt)
                    sq2 = s.activation(qn2row[:], qn2row[:], AF.Sqrt)
                    M["at_sqrt"].inc(sq2, f"{tag}{h}")
                    s.wait_ge(M["at_mhat"].ctr.h, M["at_mhat"].val(f"{tag}{h}"))
                    for i in range(NQT):
                        Pi = 512 * (i // 2 + 1)
                        nb = Pi // 512
                        pr = Prow[i % 3]
                        if i >= 3:
                            s.wait_ge(M["at_ptmm"].ctr.h, M["at_ptmm"].val(f"{tag}{h}_{i - 3}"))
                        elif h > 0:
                            s.wait_ge(M["at_ptmm"].ctr.h,
                                      M["at_ptmm"].val(f"{tag}{h - 1}_{NQT - 3 + i}"))
                        for kb in range(nb):
                            key = f"{tag}S{h}_{i}_{kb}"
                            if kb == nb - 1:
                                s.wait_ge(M["at_smask"].ctr.h, M["at_smask"].val(f"{tag}{h}_{i}"))
                                ex = s.activation(pr[:, kb * 512:(kb + 1) * 512],
                                                  Smask[i % 3][:], AF.Exp,
                                                  bias=mhat_cols[:, i:i + 1], scale=SCALE,
                                                  accum_out=lparts[:, i * 8 + kb:i * 8 + kb + 1])
                            else:
                                bank = main.cons(s, key)
                                ex = s.activation(pr[:, kb * 512:(kb + 1) * 512],
                                                  bank[:], AF.Exp,
                                                  bias=mhat_cols[:, i:i + 1], scale=SCALE,
                                                  accum_out=lparts[:, i * 8 + kb:i * 8 + kb + 1])
                                main.cons_inc(ex, key)
                        M["at_exp"].inc(ex, f"{tag}{h}_{i}")

            @block.vector
            def _(v):
                v.wait_ge(s_consts.h, consts_after_att[tag])
                for h in range(H):
                    if h > 0:
                        v.wait_ge(M["at_pvmm"].ctr.h, M["at_pvmm"].val(f"{tag}{h - 1}"))
                    for grp in range(4):
                        key = f"{tag}vt{h}_{grp}"
                        bank = tpp.cons(v, key)
                        cp = v.tensor_copy(VT_s[:, grp * 1024:(grp + 1) * 1024], bank[:])
                        tpp.cons_inc(cp, key)
                    for kb in range(8):
                        key = f"{tag}kn{h}_{kb}"
                        bank = misc.cons(v, key)
                        cp = v.tensor_copy(kn2row[:, kb * 512:(kb + 1) * 512], bank[0:1, :])
                        misc.cons_inc(cp, key)
                    for qb in range(4):
                        key = f"{tag}qn{h}_{qb}"
                        bank = misc.cons(v, key)
                        cp = v.tensor_copy(qn2row[:, qb * 512:(qb + 1) * 512], bank[0:1, :])
                        misc.cons_inc(cp, key)
                    n = selfinc(v_self, v_n, v.nop())
                    v.wait_ge(v_self.h, n)
                    rm = v.reduce_max(knmax[:], kn2row[:], axis=AX.X)
                    M["at_knorm"].inc(rm, f"{tag}{h}")
                    v.wait_ge(M["at_sqrt"].ctr.h, M["at_sqrt"].val(f"{tag}{h}"))
                    mr = v.tensor_scalar(mrow[:], qn2row[:], knmax[0:1, 0:1],
                                         -SCALE * 1.02, ALU.mult, ALU.mult)
                    M["at_mrow"].inc(mr, f"{tag}{h}")
                    key = f"{tag}mh{h}"
                    bank = misc.cons(v, key)
                    if h > 0:
                        v.wait_ge(M["at_exp"].ctr.h, M["at_exp"].val(f"{tag}{h - 1}_{NQT - 1}"))
                    cp = v.tensor_copy(mhat_cols[:], bank[:, 0:NQT])
                    M["at_mhat"].inc(cp, f"{tag}{h}")
                    v.wait_ge(M["at_mhat"].ctr.h, M["at_mhat"].val(f"{tag}{h}"))
                    misc.cons_inc(v.nop(), key)
                    for i in range(NQT + 2):
                        if i < NQT:
                            Pi = 512 * (i // 2 + 1)
                            nb = Pi // 512
                            key = f"{tag}S{h}_{i}_{nb - 1}"
                            bank = main.cons(v, key)
                            if i >= 3:
                                v.wait_ge(M["at_exp"].ctr.h, M["at_exp"].val(f"{tag}{h}_{i - 3}"))
                            elif h > 0:
                                v.wait_ge(M["at_exp"].ctr.h,
                                          M["at_exp"].val(f"{tag}{h - 1}_{NQT - 3 + i}"))
                            stt = v.scalar_tensor_tensor(Smask[i % 3][:], bank[:], 1.0,
                                                         cm_s[:, i * 512:(i + 1) * 512],
                                                         ALU.mult, ALU.add)
                            M["at_smask"].inc(stt, f"{tag}{h}_{i}")
                            v.wait_ge(M["at_smask"].ctr.h, M["at_smask"].val(f"{tag}{h}_{i}"))
                            main.cons_inc(v.nop(), key)
                            v.wait_ge(M["at_exp"].ctr.h, M["at_exp"].val(f"{tag}{h}_{i}"))
                            if h > 0:
                                v.wait_ge(M["at_prowsc"].ctr.h,
                                          M["at_prowsc"].val(f"{tag}{h - 1}_{i}"))
                            rs = v.reduce_sum(l_cols[:, i:i + 1], lparts[:, i * 8:i * 8 + nb], axis=AX.X)
                            n = selfinc(v_self, v_n, rs)
                            v.wait_ge(v_self.h, n)
                            rc = v.reciprocal(linv_cols[:, i:i + 1], l_cols[:, i:i + 1])
                            M["at_linv"].inc(rc, f"{tag}{h}_{i}")
                            v.wait_ge(M["at_linv"].ctr.h, M["at_linv"].val(f"{tag}{h}_{i}"))
                            pr = Prow[i % 3]
                            rsc = v.tensor_scalar(pr[:, 0:Pi], pr[:, 0:Pi],
                                                  linv_cols[:, i:i + 1], None, ALU.mult)
                            M["at_prowsc"].inc(rsc, f"{tag}{h}_{i}")
                        if i >= 2:
                            iq = i - 2
                            Pq = 512 * (iq // 2 + 1)
                            nb128 = Pq // 128
                            if iq >= 2:
                                v.wait_ge(M["at_pvp"].ctr.h,
                                          M["at_pvp"].val(f"{tag}{h}_{iq // 2 - 1}"))
                            for grp in range((nb128 + 7) // 8):
                                key = f"{tag}pt{h}_{iq}_{grp}"
                                bank = tpp.cons(v, key)
                                nj = min(8, nb128 - grp * 8)
                                cp = v.tensor_copy(
                                    PT_s[:].rearrange("p (g two n) -> p g two n", two=2, n=128)[
                                        :, grp * 8:grp * 8 + nj, iq % 2, :],
                                    bank[:, 0:nj * 128].rearrange("p (j n) -> p j n", n=128))
                                tpp.cons_inc(cp, key)
                            dc, dv_ = tpp.done_val(key)
                            v.wait_ge(dc.h, dv_)
                            M["at_pt"].inc(v.nop(), f"{tag}{h}_{iq}")
                            if iq % 2 == 1:
                                jp = iq // 2
                                key = f"{tag}pv{h}_{jp}"
                                bank = alt.cons(v, key)
                                if jp == 0 and h >= 2:
                                    v.wait_ge(st_attv.h, 16 * (atv0 + h - 1))
                                ev = v.tensor_copy(av_big[h % 2][:, jp * 256:(jp + 1) * 256],
                                                   bank[:, 0:256])
                                M["at_av"].inc(ev, f"{tag}{h}_{jp}")
                                v.wait_ge(M["at_av"].ctr.h, M["at_av"].val(f"{tag}{h}_{jp}"))
                                alt.cons_inc(v.nop(), key)

            @block.sync
            def _(sy):
                att_store(sy, H - 1)
            return stX

        consts_after_att = {"B": CONSTS_A + 16, "D": CONSTS_A + 32}
        ATV_BASE = {"B": 0, "D": 16}

        ATV_TOT = 16
        if upto_i >= 1:
            stA.close()
            stB = emit_attention("B", 3, 32, attv, stw_pairs(A_ST_TOTAL), 0)
            if "attv" in dbg and upto_i <= 2:
                @block.sync
                def _(sy):
                    sy.wait_ge(st_attv.h, 16 * ATV_TOT)
                    sy.dma_start(out=dbg["attv"][:], in_=attv[:]).then_inc(st_dbg.h, 16)

        if upto_i >= 2:
            stB.close()
            stC = contextlib.ExitStack()
            SC = lambda n, s, d: stC.enter_context(nc.sbuf_tensor(n, s, d))
            fmbufC = SC("fmbufC", [128, NCH * NT], BF16)
            fmvC = fmbufC[:].rearrange("p (c n) -> p c n", c=NCH)
            sqC = [SC(f"sqC{i}", [128, C], F32) for i in range(2)]
            xhatC = [SC(f"xhatC{i}", [128, C], BF16) for i in range(2)]
            x1t = [SC(f"x1t{i}", [128, C], F32) for i in range(2)]
            xft = [SC(f"xft{i}", [128, C], F32) for i in range(2)]
            xin.bufs = [SC(f"xinC{i}", [128, C], F32) for i in range(2)]
            slab.bufs = [SC(f"slabC{i}", [128, NCH * 128], BF16) for i in range(3)]
            wrow.bufs = [SC(f"wrowC{i}", [128, C], BF16) for i in range(3)]
            avst.bufs = [SC(f"avstC{i}", [128, C], BF16) for i in range(2)]

            xb0, avb0, wrb0, slb0 = xin.nfill, avst.nfill, wrow.nfill, slab.nfill
            for tt in range(NQT):
                M["e_acc"].idx(f"C{tt}")
                M["la_sq"].idx(f"C{tt}")
                M["la_stat"].idx(f"C{tt}")
                M["la_sd"].idx(f"C{tt}")
                M["c_stage"].idx(f"Cxf{tt}")
                M["la_sq"].idx(f"C{tt}b")
                M["la_stat"].idx(f"C{tt}b")
                M["la_sd"].idx(f"C{tt}b")
                M["la_xhat"].idx(f"C{tt}")
                M["la_fmT"].idx(f"C{tt}")
                M["la_fm"].idx(f"C{tt}")
            for oi in range(48):
                M["c_stage"].idx(f"Cq{oi}")
            C_ATT_B_TOTAL = 16

            def c_xstore(sy, tt):
                sy.wait_ge(M["e_acc"].ctr.h, M["e_acc"].val(f"C{tt}"))
                sy.wait_ge(st_x1.h, 16 * tt)
                sy.dma_start(out=x1_hbm[tt * 128:(tt + 1) * 128, :],
                             in_=x1t[tt % 2][:]).then_inc(st_x1.h, 16)
                sy.wait_ge(M["c_stage"].ctr.h, M["c_stage"].val(f"Cxf{tt}"))
                sy.wait_ge(st_xf.h, 16 * tt)
                sy.dma_start(out=xf_hbm[tt * 128:(tt + 1) * 128, :],
                             in_=xft[tt % 2][:]).then_inc(st_xf.h, 16)

            def c_qstore(sy, oi):
                sel, head = divmod(oi, 16)
                stg = stage[oi % 3]
                sy.wait_ge(M["c_stage"].ctr.h, M["c_stage"].val(f"Cq{oi}"))
                stw_pre(sy, 48 + oi)
                if sel == 0:
                    d = sy.dma_start(out=agin[head * 128:(head + 1) * 128, :], in_=stg[:])
                elif sel == 1:
                    d = sy.dma_start(out=agin[C + head * 128:C + (head + 1) * 128, :], in_=stg[:])
                else:
                    d = sy.dma_start(out=qbuf[head * 128:(head + 1) * 128, :], in_=stg[:])
                stw_inc(d, 48 + oi)

            n_dbgA = (sum(1 for k in ("agin", "q", "agout") if k in dbg)
                      if upto_i <= 1 else
                      sum(1 for k in ("agout",) if k in dbg))

            @block.sync
            def _(sy):
                sy.wait_ge(st_attv.h, 16 * C_ATT_B_TOTAL)
                sy.wait_ge(s_ag.h, 2)
                if n_dbgA:
                    sy.wait_ge(st_dbg.h, 16 * n_dbgA)
                avst.ndma = 4
                for tt in range(NQT):
                    def fillav(eng, buf, ld, tt=tt):
                        for q4 in range(4):
                            eng.dma_start(
                                out=buf[:].rearrange("p (c n) -> p c n", c=NCH)[:, q4 * 4:(q4 + 1) * 4, :],
                                in_=attv[q4 * 4 * 128:(q4 + 1) * 4 * 128, tt * 128:(tt + 1) * 128]
                                .rearrange("(c p) p2 -> p2 c p", p=128) if False else
                                attv[q4 * 4 * 128:(q4 + 1) * 4 * 128, tt * 128:(tt + 1) * 128]
                                .rearrange("(c p) n -> p c n", p=128)
                            ).then_inc(ld.h, 16)
                    avst.load(sy, fillav)
                    def fillx(eng, buf, ld, tt=tt):
                        eng.dma_start(out=buf[:], in_=x_in[tt * 128:(tt + 1) * 128, :]
                                      ).then_inc(ld.h, 16)
                    xin.load(sy, fillx)
                    for c in range(NCH):
                        def fillw(eng, buf, ld, c=c):
                            eng.dma_start(out=buf[:], in_=bview("woutT")[c * 128:(c + 1) * 128, :]
                                          ).then_inc(ld.h, 16)
                        wrow.load(sy, fillw)
                    if tt >= 1:
                        c_xstore(sy, tt - 1)
                c_xstore(sy, NQT - 1)
                sy.wait_ge(s_ag.h, 3)
                Qst.wait_all(sy, 16)
                for oi in range(48):
                    nm, o = [("wk", oi), ("wv", oi - 16), ("wq", oi - 32)][oi // 16]
                    def fill(eng, buf, ld, nm=nm, o=o):
                        eng.dma_start(out=buf[:], in_=bview(nm, (o,))).then_inc(ld.h, 16)
                    slab.load(sy, fill)
                    if oi >= 2:
                        c_qstore(sy, oi - 2)
                c_qstore(sy, 46)
                c_qstore(sy, 47)

            @block.tensor
            def _(t):
                t.wait_ge(s_cconst.h, 48)
                for tt in range(NQT):
                    avst.wait(t, avb0 + tt)
                    av = avst.buf(avb0 + tt)
                    avv = av[:].rearrange("p (c n) -> p c n", c=NCH)
                    keys = [f"Cop{tt}_{j}" for j in range(4)]
                    banks = [main.prod(t, keys[0]), main.prod(t, keys[1]),
                             alt.prod(t, keys[2]), alt.prod(t, keys[3])]
                    for c in range(NCH):
                        wrow.wait(t, wrb0 + tt * NCH + c)
                        wr = wrow.buf(wrb0 + tt * NCH + c)
                        for j in range(4):
                            mmw = t.matmul(banks[j][:], avv[:, c, :], wr[:, j * 512:(j + 1) * 512],
                                           start=(c == 0), stop=False)
                        wrow.release_on(mmw)
                        wrow.ncons += 1
                    for j in range(4):
                        mm = t.matmul(banks[j][:], ones_row_bf_s[:],
                                      bout_bf[0:1, j * 512:(j + 1) * 512],
                                      start=False, stop=True)
                        (main if j < 2 else alt).prod_inc(mm, keys[j])
                    t.wait_ge(alt.fill.ctr.h, alt.fill.val(keys[3]))
                    avst.release_on(t.nop())
                    avst.ncons += 1
                    t.wait_ge(M["la_xhat"].ctr.h, M["la_xhat"].val(f"C{tt}"))
                    for half in range(2):
                        key = f"Cfm{tt}_{half}"
                        bank = tpp.prod(t, key)
                        for j in range(8):
                            c = half * 8 + j
                            mm = t.transpose(bank[:, j * 128:(j + 1) * 128],
                                             xhatC[tt % 2][:, c * 128:(c + 1) * 128], id_bf_s[:])
                        tpp.prod_inc(mm, key)
                    t.wait_ge(tpp.fill.ctr.h, tpp.fill.val(key))
                    M["la_fmT"].inc(t.nop(), f"C{tt}")
                t.wait_ge(M["la_fm"].ctr.h, M["la_fm"].val(f"C{NQT - 1}"))
                for oi in range(48):
                    slab.wait(t, slb0 + oi)
                    sb = slab.buf(slb0 + oi)
                    for mt in range(4):
                        key = f"Cqkv{oi}_{mt}"
                        bank = main.prod(t, key)
                        for k in range(NCH):
                            mm = t.matmul(bank[:], sb[:, k * 128:(k + 1) * 128],
                                          fmvC[:, k, mt * 512:(mt + 1) * 512],
                                          start=(k == 0), stop=(k == NCH - 1))
                        main.prod_inc(mm, key)
                    t.wait_ge(main.fill.ctr.h, main.fill.val(key))
                    slab.release_on(t.nop())
                    slab.ncons += 1

            @block.scalar
            def _(s):
                for tt in range(NQT):
                    s.wait_ge(M["e_acc"].ctr.h, M["e_acc"].val(f"C{tt}"))
                    if tt >= 2:
                        s.wait_ge(M["la_stat"].ctr.h, M["la_stat"].val(f"C{tt - 2}b"))
                    a = s.activation(sqC[tt % 2][:], x1t[tt % 2][:], AF.Square)
                    M["la_sq"].inc(a, f"C{tt}")
                    s.wait_ge(M["la_stat"].ctr.h, M["la_stat"].val(f"C{tt}"))
                    a = s.activation(stat[:, 6:7], stat[:, 5:6], AF.Sqrt, bias=eps_s[:, 0:1])
                    M["la_sd"].inc(a, f"C{tt}")
                    s.wait_ge(M["c_stage"].ctr.h, M["c_stage"].val(f"Cxf{tt}"))
                    a = s.activation(sqC[tt % 2][:], xft[tt % 2][:], AF.Square)
                    M["la_sq"].inc(a, f"C{tt}b")
                    s.wait_ge(M["la_stat"].ctr.h, M["la_stat"].val(f"C{tt}b"))
                    a = s.activation(stat[:, 14:15], stat[:, 13:14], AF.Sqrt, bias=eps_s[:, 1:2])
                    M["la_sd"].inc(a, f"C{tt}b")

            @block.vector
            def _(v):
                v.wait_ge(s_cconst.h, 48)
                for tt in range(NQT):
                    xin.wait(v, xb0 + tt)
                    xtile = xin.buf(xb0 + tt)
                    if tt >= 2:
                        v.wait_ge(st_x1.h, 16 * (tt - 1))
                    keys = [f"Cop{tt}_{j}" for j in range(4)]
                    for j in range(4):
                        bank = (main if j < 2 else alt).cons(v, keys[j])
                        ev = v.scalar_tensor_tensor(x1t[tt % 2][:, j * 512:(j + 1) * 512],
                                                    bank[:], 1.0,
                                                    xtile[:, j * 512:(j + 1) * 512],
                                                    ALU.mult, ALU.add)
                        if j == 3:
                            M["e_acc"].inc(ev, f"C{tt}")
                            v.wait_ge(M["e_acc"].ctr.h, M["e_acc"].val(f"C{tt}"))
                            (main if j < 2 else alt).cons_inc(v.nop(), keys[j])
                            xin.release_on(v.nop())
                            xin.ncons += 1
                        else:
                            (main if j < 2 else alt).cons_inc(ev, keys[j])
                    v.wait_ge(M["la_sq"].ctr.h, M["la_sq"].val(f"C{tt}"))
                    v.reduce_sum(stat[:, 0:1], x1t[tt % 2][:], axis=AX.X)
                    r2 = v.reduce_sum(stat[:, 1:2], sqC[tt % 2][:], axis=AX.X)
                    n = selfinc(v_self, v_n, r2)
                    v.wait_ge(v_self.h, n)
                    v.tensor_scalar(stat[:, 2:3], stat[:, 0:1], 1.0 / C, None, ALU.mult)
                    t2 = v.tensor_scalar(stat[:, 3:4], stat[:, 1:2], 1.0 / C, None, ALU.mult)
                    n = selfinc(v_self, v_n, t2)
                    v.wait_ge(v_self.h, n)
                    t3 = v.tensor_mul(stat[:, 4:5], stat[:, 2:3], stat[:, 2:3])
                    n = selfinc(v_self, v_n, t3)
                    v.wait_ge(v_self.h, n)
                    t5 = v.tensor_sub(stat[:, 5:6], stat[:, 3:4], stat[:, 4:5])
                    M["la_stat"].inc(t5, f"C{tt}")
                    v.wait_ge(M["la_sd"].ctr.h, M["la_sd"].val(f"C{tt}"))
                    rc = v.reciprocal(stat[:, 7:8], stat[:, 6:7])
                    n = selfinc(v_self, v_n, rc)
                    v.wait_ge(v_self.h, n)
                    if tt >= 2:
                        v.wait_ge(st_xf.h, 16 * (tt - 1))
                    xfe = v.tensor_scalar(xft[tt % 2][:], x1t[tt % 2][:],
                                          stat[:, 2:3], stat[:, 7:8], ALU.subtract, ALU.mult)
                    M["c_stage"].inc(xfe, f"Cxf{tt}")
                    v.wait_ge(M["la_sq"].ctr.h, M["la_sq"].val(f"C{tt}b"))
                    r2 = v.reduce_sum(stat[:, 12:13], sqC[tt % 2][:], axis=AX.X)
                    n = selfinc(v_self, v_n, r2)
                    v.wait_ge(v_self.h, n)
                    t2 = v.tensor_scalar(stat[:, 13:14], stat[:, 12:13], 1.0 / C, None, ALU.mult)
                    M["la_stat"].inc(t2, f"C{tt}b")
                    v.wait_ge(M["la_sd"].ctr.h, M["la_sd"].val(f"C{tt}b"))
                    rc = v.reciprocal(stat[:, 15:16], stat[:, 14:15])
                    n = selfinc(v_self, v_n, rc)
                    v.wait_ge(v_self.h, n)
                    h2f = v.tensor_scalar(sqC[tt % 2][:], xft[tt % 2][:],
                                          stat[:, 15:16], None, ALU.mult)
                    n = selfinc(v_self, v_n, h2f)
                    v.wait_ge(v_self.h, n)
                    if tt >= 2:
                        v.wait_ge(M["la_fmT"].ctr.h, M["la_fmT"].val(f"C{tt - 2}"))
                    h2b = v.tensor_mul(xhatC[tt % 2][:], sqC[tt % 2][:], ln1_bf[:])
                    M["la_xhat"].inc(h2b, f"C{tt}")
                    for ett in ([tt - 1] if tt >= 1 else []) + ([tt] if tt == NQT - 1 else []):
                        for half in range(2):
                            key = f"Cfm{ett}_{half}"
                            bank = tpp.cons(v, key)
                            cp = v.tensor_copy(
                                fmvC[:, half * 8:(half + 1) * 8, ett * 128:(ett + 1) * 128],
                                bank[:].rearrange("p (j n) -> p j n", j=8))
                            tpp.cons_inc(cp, key)
                        dc, dv_ = tpp.done_val(key)
                        v.wait_ge(dc.h, dv_)
                        M["la_fm"].inc(v.nop(), f"C{ett}")
                for oi in range(48):
                    stg = stage[oi % 3]
                    if oi >= 3:
                        stw_one(v, 48 + oi - 3)
                    for mt in range(4):
                        key = f"Cqkv{oi}_{mt}"
                        bank = main.cons(v, key)
                        ev = v.tensor_copy(stg[:, mt * 512:(mt + 1) * 512], bank[:])
                        n = selfinc(v_self, v_n, ev)
                        v.wait_ge(v_self.h, n)
                        main.cons_inc(v.nop(), key)
                    M["c_stage"].inc(v.nop(), f"Cq{oi}")

            C_ST_TOTAL = 48 + 48

            @block.gpsimd
            def _(g):
                stw_all(g, 48 + 32)
                Kst.wait_all(g, 16)
                Vst.wait_all(g, 16)
                g.collective_compute(
                    "AllGather", ALU.bypass,
                    replica_groups=[[0, 1], [2, 3], [4, 5], [6, 7]],
                    ins=[agin[:]], outs=[agout[:]]).then_inc(s_ag.h, 1)

            if dbg:
                @block.sync
                def _(sy):
                    sy.wait_ge(st_x1.h, 16 * 16)
                    sy.wait_ge(st_xf.h, 16 * 16)
                    if "x1" in dbg:
                        sy.dma_start(out=dbg["x1"][:], in_=x1_hbm[:]).then_inc(st_dbg.h, 16)
                    if "xf" in dbg:
                        sy.dma_start(out=dbg["xf"][:], in_=xf_hbm[:]).then_inc(st_dbg.h, 16)
                    if upto_i == 2:
                        stw_all(sy, 96)
                        if "agin" in dbg:
                            sy.dma_start(out=dbg["agin"][:], in_=agin[:]).then_inc(st_dbg.h, 16)
                        if "q" in dbg:
                            sy.dma_start(out=dbg["q"][:], in_=qbuf[:]).then_inc(st_dbg.h, 16)

        if upto_i >= 3:
            ATV_TOT = 32
            stC.close()
            stD = emit_attention("D", 7, 80, attv,
                                 stw_pairs(C_ST_TOTAL)
                                 + [(st_x1, 16 * 16), (st_xf, 16 * 16)],
                                 avst.nfill)
            if "attv" in dbg:
                @block.sync
                def _(sy):
                    sy.wait_ge(st_attv.h, 16 * 32)
                    sy.dma_start(out=dbg["attv"][:], in_=attv[:]).then_inc(st_dbg.h, 16)

        if upto_i >= 4:
            stE = contextlib.ExitStack()
            SE = lambda n, s, d: stE.enter_context(nc.sbuf_tensor(n, s, d))
            stD.close()
            tfm = SE("tfm", [128, NCH * MTOK], BF16)
            tfv = tfm[:].rearrange("p (c n) -> p c n", c=NCH)
            accs = [SE(f"acc{i}", [128, C], F32) for i in range(4)]
            sqE = SE("sqE", [128, C], F32)
            thatE = SE("thatE", [128, C], BF16)
            hh_t = SE("hh_t", [128, 2 * ISL * 512], BF16)
            hh0 = [SE(f"hh0_{i}", [128, 512], BF16) for i in range(2)]
            sg_t = [SE(f"sg_t{i}", [128, 512], BF16) for i in range(2)]
            cwb = SE("cwb", [128, MTOK], F32)
            logits_s = SE("logits_s", [128, 16], F32)
            el_s = SE("el_s", [128, 16], F32)
            top8_s = SE("top8_s", [128, 8], F32)
            cw_s = SE("cw_s", [128, 16], F32)
            cwT_s = SE("cwT_s", [16, 128], F32)
            avst.bufs = [SE(f"avstE{i}", [128, 8 * MTOK], BF16) for i in range(2)]
            wrow.bufs = [SE(f"wrowE{i}", [128, C], BF16) for i in range(3)]
            xin.bufs = [SE(f"xinE{i}", [128, C], F32) for i in range(2)]
            egst.bufs = [SE(f"egstE{i}", [128, NCH * 128], BF16) for i in range(2)]
            eust.bufs = [SE(f"eustE{i}", [128, NCH * 128], BF16) for i in range(2)]
            edst.bufs = [SE(f"edstE{i}", [128, ISL * 512], BF16) for i in range(2)]
            cwrow.bufs = [SE(f"cwrowE{i}", [1, MTOK], F32) for i in range(2)]

            D_ATT_TOTAL = 256
            EXPERTS = [*range(E_), "sh"]
            GUSEQ = []
            for m_ in range(NMACRO):
                for ei_, e_ in enumerate(EXPERTS):
                    for isl_ in range(ISL if e_ != "sh" else SISL):
                        GUSEQ.append((m_, ei_, isl_))
            GUPOS = {k: i for i, k in enumerate(GUSEQ)}

            avst.ndma = 1
            def av_loads(sy, mm):
                for k in range(2):
                    def fillav(eng, buf, ld, k=k, mm=mm):
                        eng.dma_start(
                            out=buf[:].rearrange("p (c n) -> p c n", c=8),
                            in_=attv[k * 8 * 128:(k + 1) * 8 * 128,
                                     mm * MTOK:(mm + 1) * MTOK]
                            .rearrange("(c p) n -> p c n", p=128)
                        ).then_inc(ld.h, 16)
                    avst.load(sy, fillav)

            def wo_load(sy, c):
                def fillw(eng, buf, ld, c=c):
                    eng.dma_start(out=buf[:], in_=bview("wo")[c * 128:(c + 1) * 128, :]
                                  ).then_inc(ld.h, 16)
                wrow.load(sy, fillw)

            E_AV0, E_WR0 = avst.nfill, wrow.nfill
            for m in range(NMACRO):
                avb0, xb0, wrb0 = E_AV0 + 2 * m, xin.nfill, E_WR0 + 64 * m
                for tt in range(4):
                    for nm in ["e_acc", "la_sq", "la_stat", "la_sd"]:
                        M[nm].idx(f"E{m}_{tt}" if nm != "e_acc" else f"E{m}_{tt}")
                    for nm in ["e_that", "e_tfm", "e_glog", "e_gexp", "e_gcw", "e_gT"]:
                        M[nm].idx(f"{m}_{tt}")
                for ei, e in enumerate(EXPERTS):
                    nisl = ISL if e != "sh" else SISL
                    if e != "sh":
                        M["e_cwb"].idx(f"{m}_{e}")
                    for isl in range(nisl):
                        M["e_sg"].idx(f"{m}_{ei}_{isl}")
                        M["e_hh"].idx(f"{m}_{ei}_{isl}")
                    for dg in range(4):
                        for tt in range(4):
                            M["e_dn"].idx(f"{m}_{ei}_{dg}_{tt}")
                    M["e_dnmm"].idx(f"{m}_{ei}")
                for tt in range(4):
                    M["e_fin"].idx(f"{m}_{tt}")
                egb0, eub0, edb0, cwb0 = egst.nfill, eust.nfill, edst.nfill, cwrow.nfill

                @block.sync
                def _(sy):
                    if m == 0:
                        sy.wait_ge(st_attv.h, 16 * 32)
                        sy.wait_ge(s_ag.h, 6)
                        sy.dma_start(out=gate_sb[:], in_=bview("gate")
                                     ).then_inc(s_gate.h, 16)
                    def cw_store(tt):
                        sy.wait_ge(M["e_gT"].ctr.h, M["e_gT"].val(f"{m}_{tt}"))
                        if m > 0 and tt == 0:
                            cwrow.wait_all(sy, 16 * m)
                        sy.wait_ge(st_cw.h, 16 * (m * 4 + tt))
                        sy.dma_start(out=cw_dram[:, tt * 128:(tt + 1) * 128],
                                     in_=cwT_s[:]).then_inc(st_cw.h, 16)

                    if m == 0:
                        av_loads(sy, 0)
                    for tt in range(4):
                        gt = m * 4 + tt
                        def fillx(eng, buf, ld, gt=gt):
                            eng.wait_ge(st_xf.h, 16 * (gt + 1))
                            eng.dma_start(out=buf[:], in_=xf_hbm[gt * 128:(gt + 1) * 128, :]
                                          ).then_inc(ld.h, 16)
                        xin.load(sy, fillx)
                        for c in range(NCH):
                            if m >= 1 and tt == 0 and c < 3:
                                continue
                            wo_load(sy, c)
                        if tt >= 1:
                            cw_store(tt - 1)
                    cw_store(3)
                    for ei, e in enumerate(EXPERTS):
                        nisl = ISL if e != "sh" else SISL
                        if e != "sh":
                            def fillcw(eng, buf, ld, e=e):
                                eng.wait_ge(st_cw.h, 16 * 4 * (m + 1))
                                eng.dma_start(out=buf[:], in_=cw_dram[e:e + 1, :]
                                              ).then_inc(ld.h, 16)
                            cwrow.load(sy, fillcw)
                        for isl in range(nisl):
                            nm_g, nm_u = ("eg", "eu") if e != "sh" else ("sg", "su")
                            eidx = (e, isl) if e != "sh" else (isl,)
                            def fillg(eng, buf, ld, nm_g=nm_g, eidx=eidx):
                                eng.dma_start(out=buf[:], in_=bview(nm_g, eidx)
                                              ).then_inc(ld.h, 16)
                            egst.load(sy, fillg)
                            def fillu(eng, buf, ld, nm_u=nm_u, eidx=eidx):
                                eng.dma_start(out=buf[:], in_=bview(nm_u, eidx)
                                              ).then_inc(ld.h, 16)
                            eust.load(sy, fillu)
                        for dg in range(4):
                            nsets = 1 if e != "sh" else 2
                            for half in range(nsets):
                                def filld(eng, buf, ld, e=e, dg=dg, half=half):
                                    if e != "sh":
                                        src = bview("ed4", (e,))
                                        c0 = dg * ISL * 512
                                    else:
                                        src = bview("sd4")
                                        c0 = (dg * SISL + half * ISL) * 512
                                    eng.dma_start(
                                        out=buf[:],
                                        in_=src[:, c0:c0 + ISL * 512]
                                    ).then_inc(ld.h, 16)
                                edst.load(sy, filld)
                    for tt in range(4):
                        gt = m * 4 + tt
                        def fillx1(eng, buf, ld, gt=gt):
                            eng.wait_ge(st_x1.h, 16 * (gt + 1))
                            eng.dma_start(out=buf[:], in_=x1_hbm[gt * 128:(gt + 1) * 128, :]
                                          ).then_inc(ld.h, 16)
                        xin.load(sy, fillx1)
                    if m < NMACRO - 1:
                        av_loads(sy, m + 1)
                        for c in range(3):
                            wo_load(sy, c)
                    for tt in range(4):
                        gt = m * 4 + tt
                        sy.wait_ge(M["e_fin"].ctr.h, M["e_fin"].val(f"{m}_{tt}"))
                        sy.wait_ge(st_out.h, 16 * gt)
                        sy.dma_start(out=out_hbm[gt * 128:(gt + 1) * 128, :],
                                     in_=accs[tt][:]).then_inc(st_out.h, 16)

                @block.tensor
                def _(t):
                    for tt in range(4):
                        keys = [f"Eop{m}_{tt}_{j}" for j in range(4)]
                        banks = [main.prod(t, keys[0]), main.prod(t, keys[1]),
                                 alt.prod(t, keys[2]), alt.prod(t, keys[3])]
                        for c in range(NCH):
                            if tt == 0 and c % 8 == 0:
                                avst.wait(t, avb0 + c // 8)
                            av = avst.buf(avb0 + c // 8)
                            lhsT = av[:, (c % 8) * 512 + tt * 128:
                                      (c % 8) * 512 + tt * 128 + 128]
                            wrow.wait(t, wrb0 + tt * NCH + c)
                            wr = wrow.buf(wrb0 + tt * NCH + c)
                            for j in range(4):
                                mm = t.matmul(banks[j][:], lhsT,
                                              wr[:, j * 512:(j + 1) * 512],
                                              start=(c == 0), stop=(c == NCH - 1))
                            if c < NCH - 1:
                                wrow.release_on(mm)
                                wrow.ncons += 1
                        alt.prod_inc(mm, keys[2])
                        t.wait_ge(alt.fill.ctr.h, alt.fill.val(keys[2]))
                        main.prod_inc(t.nop(), keys[0])
                        main.prod_inc(t.nop(), keys[1])
                        alt.prod_inc(t.nop(), keys[3])
                        wrow.release_on(t.nop())
                        wrow.ncons += 1
                        if tt == 3:
                            avst.release_on(t.nop())
                            avst.release_on(t.nop())
                            avst.ncons += 2
                        t.wait_ge(M["e_that"].ctr.h, M["e_that"].val(f"{m}_{tt}"))
                        for half in range(2):
                            key = f"Efm{m}_{tt}_{half}"
                            bank = tpp.prod(t, key)
                            for j in range(8):
                                c = half * 8 + j
                                mm = t.transpose(bank[:, j * 128:(j + 1) * 128],
                                                 thatE[:, c * 128:(c + 1) * 128], id_bf_s[:])
                            tpp.prod_inc(mm, key)
                        M["la_fmT"].inc(t.nop(), f"E{m}_{tt}")
                        if m == 0 and tt == 0:
                            t.wait_ge(s_gate.h, 16)
                        t.wait_ge(M["e_tfm"].ctr.h, M["e_tfm"].val(f"{m}_{tt}"))
                        key = f"Eg{m}_{tt}"
                        bank = misc.prod(t, key)
                        for k in range(NCH):
                            mm = t.matmul(bank[:, 0:16], tfv[:, k, tt * 128:(tt + 1) * 128],
                                          gate_sb[:, k * 16:(k + 1) * 16],
                                          start=(k == 0), stop=(k == NCH - 1))
                        misc.prod_inc(mm, key)
                        t.wait_ge(M["e_gcw"].ctr.h, M["e_gcw"].val(f"{m}_{tt}"))
                        key = f"EgT{m}_{tt}"
                        bank = misc.prod(t, key)
                        mm = t.transpose(bank[0:16, 0:128], cw_s[:], id_f_s[:])
                        misc.prod_inc(mm, key)
                    for ei, e in enumerate(EXPERTS):
                        nisl = ISL if e != "sh" else SISL
                        if e != "sh":
                            key = f"Ecw{m}_{e}"
                            bank = misc.prod(t, key)
                            cwrow.wait(t, cwb0 + ei)
                            mm = t.matmul(bank[:], ones_row_f_s[:],
                                          cwrow.buf(cwb0 + ei)[:], start=True, stop=True)
                            misc.prod_inc(mm, key)
                            t.wait_ge(misc.fill.ctr.h, misc.fill.val(key))
                            cwrow.release_on(t.nop())
                            cwrow.ncons += 1
                        for isl in range(nisl):
                            gidx = egb0 + sum(ISL if x != "sh" else SISL for x in EXPERTS[:ei]) + isl
                            egst.wait(t, gidx)
                            eust.wait(t, gidx)
                            eg_ = egst.buf(gidx)
                            eu_ = eust.buf(gidx)
                            hhcol = (isl % (2 * ISL)) * 512 if e == "sh" else ((ei % 2) * ISL + isl) * 512
                            if ei >= 2 and isl == 0:
                                t.wait_ge(M["e_dnmm"].ctr.h, M["e_dnmm"].val(f"{m}_{ei - 2}"))
                            elif e == "sh" and isl == ISL and ei >= 1:
                                t.wait_ge(M["e_dnmm"].ctr.h, M["e_dnmm"].val(f"{m}_{ei - 1}"))
                            kg = f"Egu{m}_{ei}_{isl}g"
                            bank = main.prod(t, kg)
                            for k in range(NCH):
                                mm = t.matmul(bank[:], eg_[:, k * 128:(k + 1) * 128],
                                              tfv[:, k, :], start=(k == 0), stop=(k == NCH - 1))
                            main.prod_inc(mm, kg)
                            ku = f"Egu{m}_{ei}_{isl}u"
                            bank = alt.prod(t, ku)
                            for k in range(NCH):
                                mm = t.matmul(bank[:], eu_[:, k * 128:(k + 1) * 128],
                                              tfv[:, k, :], start=(k == 0), stop=(k == NCH - 1))
                            alt.prod_inc(mm, ku)
                            t.wait_ge(alt.fill.ctr.h, alt.fill.val(ku))
                            egst.release_on(t.nop()); egst.ncons += 1
                            eust.release_on(t.nop()); eust.ncons += 1
                        for dg in range(4):
                            nsets = 1 if e != "sh" else 2
                            dsets = []
                            for half in range(nsets):
                                didx = edb0 + sum((4 if x != "sh" else 8) for x in EXPERTS[:ei]) + dg * nsets + half
                                edst.wait(t, didx)
                                dsets.append(edst.buf(didx))
                            for tt in range(4):
                                key = f"Edn{m}_{ei}_{dg}_{tt}"
                                bank = misc.prod(t, key)
                                for isl in range(nisl):
                                    hcol = (isl * 512 if e == "sh"
                                            else ((ei % 2) * ISL + isl) * 512)
                                    t.wait_ge(M["e_hh"].ctr.h,
                                              M["e_hh"].val(f"{m}_{ei}_{isl}"))
                                    ds = dsets[isl // ISL] if e == "sh" else dsets[0]
                                    mm = t.matmul(bank[:],
                                                  hh_t[:, hcol + tt * 128: hcol + tt * 128 + 128],
                                                  ds[:, (isl % ISL) * 512:(isl % ISL + 1) * 512],
                                                  start=(isl == 0), stop=(isl == nisl - 1))
                                misc.prod_inc(mm, key)
                            t.wait_ge(misc.fill.ctr.h, misc.fill.val(key))
                            for half in range(nsets):
                                edst.release_on(t.nop()); edst.ncons += 1
                        M["e_dnmm"].inc(t.nop(), f"{m}_{ei}")

                @block.scalar
                def _(s):
                    for tt in range(4):
                        s.wait_ge(M["e_acc"].ctr.h, M["e_acc"].val(f"E{m}_{tt}"))
                        a = s.activation(sqE[:], accs[tt][:], AF.Square)
                        M["la_sq"].inc(a, f"E{m}_{tt}")
                        s.wait_ge(M["la_stat"].ctr.h, M["la_stat"].val(f"E{m}_{tt}"))
                        a = s.activation(stat[:, 6:7], stat[:, 5:6], AF.Sqrt, bias=eps_s[:, 1:2])
                        M["la_sd"].inc(a, f"E{m}_{tt}")
                        s.wait_ge(M["e_glog"].ctr.h, M["e_glog"].val(f"{m}_{tt}"))
                        a = s.activation(el_s[:], logits_s[:], AF.Exp, bias=stat[:, 8:9])
                        M["e_gexp"].inc(a, f"{m}_{tt}")
                    for ei, e in enumerate(EXPERTS):
                        nisl = ISL if e != "sh" else SISL
                        for isl in range(nisl):
                            kg = f"Egu{m}_{ei}_{isl}g"
                            bank = main.cons(s, kg)
                            if (ei * 100 + isl) >= 2:
                                pass
                            pos = GUPOS[(m, ei, isl)]
                            if pos >= 2:
                                pm, pei, pisl = GUSEQ[pos - 2]
                                s.wait_ge(M["e_hh"].ctr.h,
                                          M["e_hh"].val(f"{pm}_{pei}_{pisl}"))
                            sl = s.activation(sg_t[pos % 2][:], bank[:], AF.Silu)
                            M["e_sg"].inc(sl, f"{m}_{ei}_{isl}")
                            s.wait_ge(M["e_sg"].ctr.h, M["e_sg"].val(f"{m}_{ei}_{isl}"))
                            main.cons_inc(s.nop(), kg)

                @block.vector
                def _(v):
                    v.wait_ge(s_cconst.h, 48)
                    for tt in range(4):
                        gt = m * 4 + tt
                        xin.wait(v, xb0 + tt)
                        xtile = xin.buf(xb0 + tt)
                        if m > 0:
                            v.wait_ge(st_out.h, 16 * ((m - 1) * 4 + tt + 1))
                        keys = [f"Eop{m}_{tt}_{j}" for j in range(4)]
                        for j in range(4):
                            bank = (main if j < 2 else alt).cons(v, keys[j])
                            ev = v.scalar_tensor_tensor(accs[tt][:, j * 512:(j + 1) * 512],
                                                        bank[:], 1.0,
                                                        xtile[:, j * 512:(j + 1) * 512],
                                                        ALU.mult, ALU.add)
                            if j == 3:
                                M["e_acc"].inc(ev, f"E{m}_{tt}")
                                v.wait_ge(M["e_acc"].ctr.h, M["e_acc"].val(f"E{m}_{tt}"))
                                (main if j < 2 else alt).cons_inc(v.nop(), keys[j])
                                xin.release_on(v.nop())
                                xin.ncons += 1
                            else:
                                (main if j < 2 else alt).cons_inc(ev, keys[j])
                        v.wait_ge(M["la_sq"].ctr.h, M["la_sq"].val(f"E{m}_{tt}"))
                        r2 = v.reduce_sum(stat[:, 12:13], sqE[:], axis=AX.X)
                        n = selfinc(v_self, v_n, r2)
                        v.wait_ge(v_self.h, n)
                        t2 = v.tensor_scalar(stat[:, 5:6], stat[:, 12:13], 1.0 / C, None, ALU.mult)
                        M["la_stat"].inc(t2, f"E{m}_{tt}")
                        v.wait_ge(M["la_sd"].ctr.h, M["la_sd"].val(f"E{m}_{tt}"))
                        rc = v.reciprocal(stat[:, 7:8], stat[:, 6:7])
                        n = selfinc(v_self, v_n, rc)
                        v.wait_ge(v_self.h, n)
                        h2f = v.tensor_scalar(sqE[:], accs[tt][:], stat[:, 7:8], None, ALU.mult)
                        n = selfinc(v_self, v_n, h2f)
                        v.wait_ge(v_self.h, n)
                        thb = v.tensor_mul(thatE[:], sqE[:], ln2_bf[:])
                        M["e_that"].inc(thb, f"{m}_{tt}")
                        for half in range(2):
                            key = f"Efm{m}_{tt}_{half}"
                            bank = tpp.cons(v, key)
                            cp = v.tensor_copy(
                                tfv[:, half * 8:(half + 1) * 8, tt * 128:(tt + 1) * 128],
                                bank[:].rearrange("p (j n) -> p j n", j=8))
                            tpp.cons_inc(cp, key)
                        dc, dv_ = tpp.done_val(key)
                        v.wait_ge(dc.h, dv_)
                        M["e_tfm"].inc(v.nop(), f"{m}_{tt}")
                        key = f"Eg{m}_{tt}"
                        bank = misc.cons(v, key)
                        cp = v.tensor_copy(logits_s[:], bank[:, 0:16])
                        n = selfinc(v_self, v_n, cp)
                        v.wait_ge(v_self.h, n)
                        misc.cons_inc(v.nop(), key)
                        rm = v.reduce_max(stat[:, 9:10], logits_s[:], axis=AX.X)
                        n = selfinc(v_self, v_n, rm)
                        v.wait_ge(v_self.h, n)
                        ng = v.tensor_scalar(stat[:, 8:9], stat[:, 9:10], -1.0, None, ALU.mult)
                        M["e_glog"].inc(ng, f"{m}_{tt}")
                        v.wait_ge(M["e_gexp"].ctr.h, M["e_gexp"].val(f"{m}_{tt}"))
                        mx = v.max(top8_s[:], el_s[:])
                        n = selfinc(v_self, v_n, mx)
                        v.wait_ge(v_self.h, n)
                        rs = v.reduce_sum(stat[:, 10:11], top8_s[:, 0:KTOP], axis=AX.X)
                        n = selfinc(v_self, v_n, rs)
                        v.wait_ge(v_self.h, n)
                        rc = v.reciprocal(stat[:, 11:12], stat[:, 10:11])
                        ge = v.tensor_scalar(cw_s[:], el_s[:], top8_s[:, KTOP - 1:KTOP],
                                             None, ALU.is_ge)
                        n = selfinc(v_self, v_n, ge)
                        v.wait_ge(v_self.h, n)
                        cwf = v.scalar_tensor_tensor(cw_s[:], cw_s[:], stat[:, 11:12],
                                                     el_s[:], ALU.mult, ALU.mult)
                        M["e_gcw"].inc(cwf, f"{m}_{tt}")
                        key = f"EgT{m}_{tt}"
                        bank = misc.cons(v, key)
                        if m > 0 or tt > 0:
                            v.wait_ge(st_cw.h, 16 * (m * 4 + tt))
                        cp = v.tensor_copy(cwT_s[:], bank[0:16, 0:128])
                        M["e_gT"].inc(cp, f"{m}_{tt}")
                        v.wait_ge(M["e_gT"].ctr.h, M["e_gT"].val(f"{m}_{tt}"))
                        misc.cons_inc(v.nop(), key)
                    for ei, e in enumerate(EXPERTS):
                        nisl = ISL if e != "sh" else SISL
                        if e != "sh":
                            key = f"Ecw{m}_{e}"
                            bank = misc.cons(v, key)
                            cp = v.tensor_copy(cwb[:], bank[:])
                            M["e_cwb"].inc(cp, f"{m}_{e}")
                            v.wait_ge(M["e_cwb"].ctr.h, M["e_cwb"].val(f"{m}_{e}"))
                            misc.cons_inc(v.nop(), key)
                        for isl in range(nisl):
                            ku = f"Egu{m}_{ei}_{isl}u"
                            bank = alt.cons(v, ku)
                            v.wait_ge(M["e_sg"].ctr.h, M["e_sg"].val(f"{m}_{ei}_{isl}"))
                            pos = GUPOS[(m, ei, isl)]
                            hhx = v.scalar_tensor_tensor(hh0[pos % 2][:], bank[:], 1.0,
                                                         sg_t[pos % 2][:], ALU.mult, ALU.mult)
                            n = selfinc(v_self, v_n, hhx)
                            v.wait_ge(v_self.h, n)
                            alt.cons_inc(v.nop(), ku)
                            hcol = (isl * 512 if e == "sh"
                                    else ((ei % 2) * ISL + isl) * 512)
                            if e != "sh":
                                v.wait_ge(M["e_cwb"].ctr.h, M["e_cwb"].val(f"{m}_{e}"))
                                hw = v.tensor_mul(hh_t[:, hcol:hcol + 512], hh0[pos % 2][:],
                                                  cwb[:])
                            else:
                                hw = v.tensor_copy(hh_t[:, hcol:hcol + 512], hh0[pos % 2][:])
                            M["e_hh"].inc(hw, f"{m}_{ei}_{isl}")
                        for dg in range(4):
                            for tt in range(4):
                                key = f"Edn{m}_{ei}_{dg}_{tt}"
                                bank = misc.cons(v, key)
                                if ei == 0:
                                    v.wait_ge(M["e_that"].ctr.h, M["e_that"].val(f"{m}_{tt}"))
                                dn = v.tensor_add(accs[tt][:, dg * 512:(dg + 1) * 512],
                                                  accs[tt][:, dg * 512:(dg + 1) * 512],
                                                  bank[:])
                                M["e_dn"].inc(dn, f"{m}_{ei}_{dg}_{tt}")
                                v.wait_ge(M["e_dn"].ctr.h, M["e_dn"].val(f"{m}_{ei}_{dg}_{tt}"))
                                misc.cons_inc(v.nop(), key)
                    for tt in range(4):
                        v.wait_ge(M["e_dn"].ctr.h, M["e_dn"].val(f"{m}_{len(EXPERTS) - 1}_3_3"))
                        xin.wait(v, xb0 + 4 + tt)
                        ad = v.tensor_add(accs[tt][:], accs[tt][:], xin.buf(xb0 + 4 + tt)[:])
                        M["e_fin"].inc(ad, f"{m}_{tt}")
                        xin.ncons += 1
                        v.wait_ge(M["e_fin"].ctr.h, M["e_fin"].val(f"{m}_{tt}"))
                        xin.release_on(v.nop())

        @block.sync
        def _(sy):
            if upto_i == 0:
                stw_all(sy, 48)
            elif upto_i == 1:
                sy.wait_ge(st_attv.h, 16 * 16)
            elif upto_i == 2:
                stw_all(sy, 96)
                sy.wait_ge(st_x1.h, 16 * 16)
                sy.wait_ge(st_xf.h, 16 * 16)
            elif upto_i == 3:
                sy.wait_ge(st_attv.h, 16 * 32)
            else:
                sy.wait_ge(st_out.h, 16 * 16)
            if dbg:
                sy.wait_ge(st_dbg.h, 16 * len(dbg))
            if upto_i < 4:
                sy.dma_start(out=out_hbm[0:128, :], in_=x_in[0:128, :]).then_inc(st_out.h, 16)
                sy.wait_ge(st_out.h, 16)

        if upto_i == 0:
            pass
        elif upto_i == 1:
            stB.close()
        elif upto_i == 2:
            stC.close()
        elif upto_i == 3:
            stD.close()
        else:
            stE.close()

    return nc, dbg


def make_inmaps(ins):
    blobs = host_blobs(ins)
    cores, shared = host_percore(ins)
    in_maps = []
    for c in range(8):
        m = dict(
            x=cores[c]["x"],
            ropeA=cores[c]["ropeA"], ropeB=cores[c]["ropeB"],
            cmask=cores[c]["cmask"],
            bqkv_cols=shared["bqkv_cols"], bout_row=shared["bout_row"],
            ln1_tile=shared["ln1_tile"], ln2_tile=shared["ln2_tile"])
        for gk in ("A1", "A2", "G", "U", "D"):
            sh = blobs[gk].size // 8
            m[f"wshard{gk}"] = np.ascontiguousarray(blobs[gk][c * sh:(c + 1) * sh])
        in_maps.append(m)
    return in_maps, cores


_CACHE = {}

def kernel(**inputs):
    if "nc" not in _CACHE:
        _CACHE["nc"] = build(upto="E")[0]
    nc = _CACHE["nc"]
    in_maps, cores = make_inmaps(inputs)
    res = run_bass_kernel_spmd(nc, in_maps, core_ids=list(range(8)))
    full = np.zeros((B, T, C), np.float32)
    for core in range(8):
        full[core // 2][cores[core]["rows"]] = res.results[core]["y"]
    return full


# version 08
# speedup vs baseline: 1.0528x; 1.0528x over previous
import math
import contextlib
import numpy as np
import ml_dtypes
import concourse.bass as bass
import concourse.mybir as mybir
from concourse.bass_utils import run_bass_kernel_spmd

F32 = mybir.dt.float32
BF16 = mybir.dt.bfloat16
AF = mybir.ActivationFunctionType
ALU = mybir.AluOpType
AX = mybir.AxisListType

B, T, C = 4, 4096, 2048
H, HD = 16, 128
NT = 2048
NQT = 16
E_, KTOP, I_ = 16, 6, 1408
ISL = I_ // 128
SI = 2816
SISL = SI // 128
NCH = C // 128
SCALE = 1.0 / math.sqrt(HD)
MASKVAL = -1e9
MTOK = 512
NMACRO = NT // MTOK
BLOB_ELEMS = 189300736


def _perm_half():
    return np.concatenate([np.arange(0, HD, 2), np.arange(1, HD, 2)])

def _slabify(w):
    ci, no = w.shape
    k, o = ci // 128, no // 128
    return np.ascontiguousarray(
        w.reshape(k, 128, o, 128).transpose(2, 1, 0, 3).reshape(o, 128, k * 128))

def host_blobs(ins):
    pe = _perm_half()
    W1 = np.ascontiguousarray(ins["L0_attn_w_qkv"].T).reshape(C, 3, H, HD)
    W1q = W1.copy()
    W1q[:, 0] = W1[:, 0][:, :, pe]
    W1q[:, 1] = W1[:, 1][:, :, pe]
    W1 = W1q.reshape(C, 3 * C)
    parts = []
    parts.append(("qkv", _slabify(W1)))
    parts.append(("woutT", np.ascontiguousarray(ins["L0_attn_w_out"].T)))
    parts.append(("wq", _slabify(ins["L0_ffn_wq"])))
    parts.append(("wk", _slabify(ins["L0_ffn_wk"])))
    parts.append(("wv", _slabify(ins["L0_ffn_wv"])))
    parts.append(("wo", np.ascontiguousarray(ins["L0_ffn_wo"])))
    gt = np.ascontiguousarray(ins["L0_ffn_gate_w"].T)
    parts.append(("gate", np.ascontiguousarray(
        gt.reshape(NCH, 128, 16).transpose(1, 0, 2).reshape(128, NCH * 16))))
    parts.append(("eg", np.ascontiguousarray(
        ins["L0_ffn_eg_w"].reshape(E_, ISL, 128, NCH, 128)
        .transpose(0, 1, 4, 3, 2).reshape(E_, ISL, 128, NCH * 128))))
    parts.append(("eu", np.ascontiguousarray(
        ins["L0_ffn_eu_w"].reshape(E_, ISL, 128, NCH, 128)
        .transpose(0, 1, 4, 3, 2).reshape(E_, ISL, 128, NCH * 128))))
    edT = np.ascontiguousarray(ins["L0_ffn_ed_w"].transpose(0, 2, 1))
    parts.append(("ed4", np.ascontiguousarray(
        edT.reshape(E_, ISL, 128, 4, 512).transpose(0, 2, 3, 1, 4)
        .reshape(E_, 128, 4 * ISL * 512))))
    parts.append(("sg", _slabify(np.ascontiguousarray(ins["L0_ffn_sg_w"].T))))
    parts.append(("su", _slabify(np.ascontiguousarray(ins["L0_ffn_su_w"].T))))
    sdT = np.ascontiguousarray(ins["L0_ffn_sd_w"].T)
    parts.append(("sd4", np.ascontiguousarray(
        sdT.reshape(SISL, 128, 4, 512).transpose(1, 2, 0, 3)
        .reshape(128, 4 * SISL * 512))))
    d = dict(parts)
    groups = {}
    groups["A1"] = d["qkv"].astype(np.float32, copy=False).reshape(-1)
    ga = np.concatenate([d[n].astype(np.float32, copy=False).reshape(-1)
                         for n in ["woutT", "wq", "wk", "wv", "wo",
                                   "gate", "sg", "su", "sd4"]])
    pad = (-ga.size) % (8 * 2048)
    if pad:
        ga = np.concatenate([ga, np.zeros(pad, np.float32)])
    groups["A2"] = ga
    groups["G"] = d["eg"].astype(np.float32, copy=False).reshape(-1)
    groups["U"] = d["eu"].astype(np.float32, copy=False).reshape(-1)
    groups["D"] = d["ed4"].astype(np.float32, copy=False).reshape(-1)
    for k in ("A1", "G", "U", "D"):
        assert groups[k].size % (8 * 2048) == 0
    return groups

def blob_offsets():
    offs = {}
    offs["qkv"] = ("A1", 0, (48, 128, NCH * 128))
    cur = 0
    for name, shape in [
            ("woutT", (C, C)),
            ("wq", (16, 128, NCH * 128)), ("wk", (16, 128, NCH * 128)),
            ("wv", (16, 128, NCH * 128)), ("wo", (C, C)),
            ("gate", (128, NCH * 16)),
            ("sg", (SISL, 128, NCH * 128)), ("su", (SISL, 128, NCH * 128)),
            ("sd4", (128, 4 * SISL * 512))]:
        offs[name] = ("A2", cur, shape)
        cur += int(np.prod(shape))
    ga_elems = cur + ((-cur) % (8 * 2048))
    offs["eg"] = ("G", 0, (E_, ISL, 128, NCH * 128))
    offs["eu"] = ("U", 0, (E_, ISL, 128, NCH * 128))
    offs["ed4"] = ("D", 0, (E_, 128, 4 * ISL * 512))
    return offs, ga_elems

def host_percore(ins):
    x = ins["L0_attn_x"].astype(np.float32, copy=False)
    pe = _perm_half()
    half = HD // 2
    thetas = 10000.0 ** (-np.arange(half, dtype=np.float64) / half)
    ang = np.arange(T, dtype=np.float64)[:, None] * thetas[None, :]
    sinT = np.sin(ang).astype(np.float32)
    cosT = np.cos(ang).astype(np.float32)

    bq = ins["L0_attn_b_qkv"].astype(np.float32).reshape(3, H, HD).copy()
    bq[0] = bq[0][:, pe]
    bq[1] = bq[1][:, pe]
    bqkv_cols = np.ascontiguousarray(bq.reshape(48, 128).T)
    bout_row = ins["L0_attn_b_out"].astype(np.float32).reshape(1, C)
    ln1_tile = np.ascontiguousarray(np.broadcast_to(ins["L0_ffn_ln1_w"].astype(np.float32), (128, C)))
    ln2_tile = np.ascontiguousarray(np.broadcast_to(ins["L0_ffn_ln2_w"].astype(np.float32), (128, C)))

    cores = []
    for core in range(8):
        b, p = core // 2, core % 2
        rows = np.concatenate([np.arange(256 * i + 128 * p, 256 * i + 128 * p + 128)
                               for i in range(NQT)])
        cos_l = cosT[rows].T
        sin_l = sinT[rows].T
        cm = np.zeros((NQT, 128, 512), np.float32)
        for i in range(NQT):
            base = 512 * (i // 2)
            tg = 256 * i + 128 * p + np.arange(128)
            kt = base + np.arange(512)
            cm[i] = np.where(kt[None, :] <= tg[:, None], 0.0, MASKVAL)
        cores.append(dict(
            x=np.ascontiguousarray(x[b, rows]),
            ropeA=np.ascontiguousarray(np.concatenate([cos_l, sin_l], axis=0)),
            ropeB=np.ascontiguousarray(np.concatenate([sin_l, cos_l], axis=0)),
            cmask=np.ascontiguousarray(cm), rows=rows))
    shared = dict(bqkv_cols=bqkv_cols, bout_row=bout_row,
                  ln1_tile=ln1_tile, ln2_tile=ln2_tile)
    return cores, shared


class Ctr:
    def __init__(self, stack, nc, name):
        self.h = stack.enter_context(nc.semaphore(name))
        self.name = name

class Tk:
    def __init__(self, ctr, per=1):
        self.ctr = ctr
        self.map = {}
        self.next = 0
        self.per = per
    def idx(self, key):
        if key not in self.map:
            self.map[key] = self.next
            self.next += 1
        return self.map[key]
    def val(self, key):
        return (self.idx(key) + 1) * self.per
    def inc(self, instr, key):
        i = self.idx(key)
        instr.then_inc(self.ctr.h, self.per)
        return (i + 1) * self.per

class Pipe:
    def __init__(self, banks, fill: Tk, dones):
        self.banks, self.fill, self.dones = banks, fill, dones
        self.done = fill
    def prod(self, eng, key):
        i = self.fill.idx(key)
        n = len(self.banks)
        if i >= n:
            eng.wait_ge(self.dones[i % n].h, i // n)
        return self.banks[i % n]
    def prod_inc(self, instr, key):
        self.fill.inc(instr, key)
    def cons(self, eng, key):
        i = self.fill.idx(key)
        eng.wait_ge(self.fill.ctr.h, i + 1)
        return self.banks[i % len(self.banks)]
    def cons_inc(self, instr, key):
        i = self.fill.idx(key)
        instr.then_inc(self.dones[i % len(self.banks)].h, 1)
    def done_val(self, key):
        i = self.fill.idx(key)
        return (self.dones[i % len(self.banks)], i // len(self.banks) + 1)

class Stream:
    def __init__(self, bufs, lds, fr, ndma=1):
        self.bufs, self.lds, self.fr, self.ndma = bufs, lds, fr, ndma
        self.nfill = 0
        self.ncons = 0
        self.nfr = 0
    def load(self, eng, fill):
        s = self.nfill % len(self.bufs)
        if self.nfill >= len(self.bufs):
            eng.wait_ge(self.fr.h, self.nfill - len(self.bufs) + 1)
        fill(eng, self.bufs[s], self.lds[s])
        self.nfill += 1
    def buf(self, i):
        return self.bufs[i % len(self.bufs)]
    def wait(self, eng, i):
        n = len(self.bufs)
        eng.wait_ge(self.lds[i % n].h, 16 * self.ndma * (i // n + 1))
    def wait_all(self, eng, nfills):
        n = len(self.bufs)
        for s in range(n):
            cnt = len([i for i in range(nfills) if i % n == s])
            if cnt:
                eng.wait_ge(self.lds[s].h, 16 * self.ndma * cnt)
    def release_on(self, instr):
        instr.then_inc(self.fr.h, 1)
        self.nfr += 1


def build(upto="E", debug=()):
    nc = bass.Bass("TRN2", target_bir_lowering=False)
    upto_i = "ABCDE".index(upto)

    OFFS, GA_ELEMS = blob_offsets()
    GKEYS = ["A1", "A2", "G", "U", "D"]
    GSIZES = {"A1": 48 * 128 * NCH * 128, "A2": GA_ELEMS,
              "G": 46137344, "U": 46137344, "D": 46137344}
    SHARDS = {k: v // 8 for k, v in GSIZES.items()}

    x_in = nc.dram_tensor("x", [NT, C], F32, kind="ExternalInput").ap()
    wshard = {k: nc.dram_tensor(f"wshard{k}", [SHARDS[k]], F32, kind="ExternalInput").ap()
              for k in GKEYS}
    ropeA_in = nc.dram_tensor("ropeA", [128, NT], F32, kind="ExternalInput").ap()
    ropeB_in = nc.dram_tensor("ropeB", [128, NT], F32, kind="ExternalInput").ap()
    cm_in = nc.dram_tensor("cmask", [NQT, 128, 512], F32, kind="ExternalInput").ap()
    bqkv_in = nc.dram_tensor("bqkv_cols", [128, 48], F32, kind="ExternalInput").ap()
    bout_in = nc.dram_tensor("bout_row", [1, C], F32, kind="ExternalInput").ap()
    ln1_in = nc.dram_tensor("ln1_tile", [128, C], F32, kind="ExternalInput").ap()
    ln2_in = nc.dram_tensor("ln2_tile", [128, C], F32, kind="ExternalInput").ap()

    shard_bf = {k: nc.dram_tensor(f"shard_bf{k}", [SHARDS[k]], BF16, kind="Internal").ap()
                for k in GKEYS}
    blob_t = {k: nc.dram_tensor(f"blob{k}", [GSIZES[k] // 2048, 2048], BF16, kind="Internal")
              for k in GKEYS}
    blob = {k: blob_t[k].ap().rearrange("a b -> (a b)") for k in GKEYS}
    agin = nc.dram_tensor("agin", [2 * C, NT], BF16, kind="Internal").ap()
    agout = nc.dram_tensor("agout", [4 * C, NT], BF16, kind="Internal").ap()
    qbuf = nc.dram_tensor("qbuf", [C, NT], BF16, kind="Internal").ap()
    attv = nc.dram_tensor("attv", [C, NT], BF16, kind="Internal").ap()
    x1_hbm = nc.dram_tensor("x1_hbm", [NT, C], F32, kind="Internal").ap()
    xf_hbm = nc.dram_tensor("xf_hbm", [NT, C], F32, kind="Internal").ap()
    cw_dram = nc.dram_tensor("cw_dram", [E_, MTOK], F32, kind="Internal").ap()
    out_hbm = nc.dram_tensor("y", [NT, C], F32, kind="ExternalOutput").ap()

    dbg = {}
    for name, shape, dt in [("agin", [2 * C, NT], BF16), ("q", [C, NT], BF16),
                            ("attv", [C, NT], BF16), ("x1", [NT, C], F32),
                            ("xf", [NT, C], F32), ("agout", [4 * C, NT], BF16),
                            ("cw", [E_, MTOK], F32)]:
        if name in debug:
            dbg[name] = nc.dram_tensor("dbg_" + name, shape, dt, kind="ExternalOutput").ap()

    id_bf = nc.inline_tensor(np.eye(128, dtype=ml_dtypes.bfloat16), name="id_bf").ap()
    id_f = nc.inline_tensor(np.eye(128, dtype=np.float32), name="id_f").ap()
    ones_col = nc.inline_tensor(np.ones((128, 1), dtype=ml_dtypes.bfloat16), name="ones_col").ap()
    ones_row_f = nc.inline_tensor(np.ones((1, 128), dtype=np.float32), name="ones_row_f").ap()
    ones_row_bf = nc.inline_tensor(np.ones((1, 128), dtype=ml_dtypes.bfloat16), name="ones_row_bf").ap()
    eps_np = np.zeros((128, 2), np.float32)
    eps_np[:, 0] = 1e-5
    eps_np[:, 1] = 1e-6
    eps_in = nc.inline_tensor(eps_np, name="epsc").ap()

    st = contextlib.ExitStack()
    with st:
        SB = lambda n, s, d: st.enter_context(nc.sbuf_tensor(n, s, d))
        PS = lambda n, s, d: st.enter_context(nc.psum_tensor(n, s, d))
        id_bf_s = SB("id_bf_s", [128, 128], BF16)
        id_f_s = SB("id_f_s", [128, 128], F32)
        ones_col_s = SB("ones_col_s", [128, 1], BF16)
        ones_row_f_s = SB("ones_row_f_s", [1, 128], F32)
        ones_row_bf_s = SB("ones_row_bf_s", [1, 128], BF16)
        eps_s = SB("eps_s", [128, 2], F32)
        bqkv_s = SB("bqkv_s", [128, 48], F32)
        bout_bf = SB("bout_bf", [1, C], BF16)
        ln1_bf = SB("ln1_bf", [128, C], BF16)
        ln2_bf = SB("ln2_bf", [128, C], BF16)
        gate_sb = SB("gate_sb", [128, NCH * 16], BF16)
        stat = SB("stat", [128, 16], F32)
        stage = [SB(f"stage{i}", [128, C], BF16) for i in range(3)]

        psA = PS("psA", [128, 512], F32)
        psB = PS("psB", [128, 512], F32)
        psC = PS("psC", [128, 512], F32)
        psD = PS("psD", [128, 512], F32)
        psT1 = PS("psT1", [128, 1024], BF16)
        psT2 = PS("psT2", [128, 1024], BF16)
        psO = PS("psO", [128, 512], F32)
        psM = PS("psM", [128, 512], F32)

        Cm = lambda name: Ctr(st, nc, name)
        v_self = Cm("v_self"); v_n = [0]
        a_self = Cm("a_self"); a_n = [0]
        g_self = Cm("g_self"); g_n = [0]

        def selfinc(ctr, nref, instr):
            instr.then_inc(ctr.h, 1)
            nref[0] += 1
            return nref[0]

        s_castA1 = Cm("s_castA1")
        s_castR = Cm("s_castR")
        s_cast = {k: (s_castA1 if k == "A1" else s_castR) for k in GKEYS}
        def cast_done(g, gk):
            if gk == "A1":
                g.wait_ge(s_castA1.h, 16 * 8)
            else:
                g.wait_ge(s_castR.h, 16 * 32)
        s_ag = Cm("s_ag")
        s_gate = Cm("s_gate")
        s_consts = Cm("s_consts")
        s_cconst = Cm("s_cconst")
        st_stp = [Cm(f"st_stp{k}") for k in range(4)]
        def stw_all(eng, n):
            for k in range(4):
                c = (n - k + 3) // 4
                if c:
                    eng.wait_ge(st_stp[k].h, 16 * c)
        def stw_one(eng, s):
            eng.wait_ge(st_stp[s % 4].h, 16 * (s // 4 + 1))
        def stw_pre(eng, s):
            if s >= 4:
                eng.wait_ge(st_stp[s % 4].h, 16 * (s // 4))
        def stw_inc(d, s):
            d.then_inc(st_stp[s % 4].h, 16)
        def stw_pairs(n):
            return [(st_stp[k], 16 * ((n - k + 3) // 4)) for k in range(4)
                    if (n - k + 3) // 4 > 0]
        st_attv = Cm("st_attv")
        st_x1 = Cm("st_x1")
        st_xf = Cm("st_xf")
        st_out = Cm("st_out")
        st_cw = Cm("st_cw")
        st_dbg = Cm("st_dbg")

        main = Pipe([psA, psB], Tk(Cm("main_f")), [Cm("main_d0"), Cm("main_d1")])
        alt = Pipe([psC, psD], Tk(Cm("alt_f")), [Cm("alt_d0"), Cm("alt_d1")])
        misc = Pipe([psO, psM], Tk(Cm("misc_f")), [Cm("misc_d0"), Cm("misc_d1")])
        tpp = Pipe([psT1, psT2], Tk(Cm("tp_f")), [Cm("tp_d0"), Cm("tp_d1")])

        M = {}
        for nm in ["la_sq", "la_stat", "la_sd", "la_xhat", "la_fm", "la_fmT",
                   "at_sq", "at_sqrt", "at_knorm", "at_mrow", "at_mhat",
                   "at_smask", "at_exp", "at_linv", "at_prowsc", "at_pt",
                   "at_ptmm", "at_pvmm", "at_pvp", "at_normmm", "at_av",
                   "c_stage",
                   "e_acc", "e_that", "e_tfm", "e_glog", "e_gexp", "e_gcw",
                   "e_gT", "e_cwb", "e_sg", "e_hh", "e_dn", "e_dnmm", "e_fin"]:
            M[nm] = Tk(Cm(nm))

        def mkstream(name, nslots, ndma=1):
            return Stream([], [Cm(f"{name}_ld{i}") for i in range(nslots)],
                          Cm(name + "_fr"), ndma)

        xin = mkstream("xin", 2)
        slab = mkstream("slab", 3)
        Kst = mkstream("Kst", 2, ndma=2)
        Vst = mkstream("Vst", 2, ndma=2)
        Qst = mkstream("Qst", 2)
        avst = mkstream("avst", 2)
        wrow = mkstream("wrow", 3)
        egst = mkstream("egst", 2)
        eust = mkstream("eust", 2)
        edst = mkstream("edst", 2)
        cwrow = mkstream("cwrow", 2)

        block = st.enter_context(nc.Block())

        def bview(name, idx=()):
            grp, off, shape = OFFS[name]
            for k2, i2 in enumerate(idx):
                off += i2 * int(np.prod(shape[k2 + 1:]))
            sub = shape[len(idx):]
            assert len(sub) == 2
            r, c = sub
            return blob[grp][off: off + r * c].rearrange("(a b) -> a b", b=c)

        @block.gpsimd
        def _(g):
            for gk in GKEYS:
                chunk = SHARDS[gk] // 8
                for i in range(8):
                    g.dma_start(out=shard_bf[gk][i * chunk:(i + 1) * chunk],
                                in_=wshard[gk][i * chunk:(i + 1) * chunk]).then_inc(s_cast[gk].h, 16)
            for gi, gk in enumerate(["A1", "A2"]):
                cast_done(g, gk)
                g.collective_compute(
                    "AllGather", ALU.bypass,
                    replica_groups=[[0, 1, 2, 3, 4, 5, 6, 7]],
                    ins=[shard_bf[gk].rearrange("(a b) -> a b", b=2048)],
                    outs=[blob_t[gk].ap()]).then_inc(s_ag.h, 1)
            g.dma_start(out=bout_bf[:], in_=bout_in[:]).then_inc(s_cconst.h, 16)
            g.dma_start(out=ln1_bf[:], in_=ln1_in[:]).then_inc(s_cconst.h, 16)
            g.dma_start(out=ln2_bf[:], in_=ln2_in[:]).then_inc(s_cconst.h, 16)

        @block.sync
        def _(sy):
            for dst, src in [(id_bf_s, id_bf), (id_f_s, id_f), (ones_col_s, ones_col),
                             (ones_row_f_s, ones_row_f), (ones_row_bf_s, ones_row_bf),
                             (eps_s, eps_in), (bqkv_s, bqkv_in)]:
                sy.dma_start(out=dst[:], in_=src[:]).then_inc(s_consts.h, 16)
        CONSTS = 7 * 16

        stA = contextlib.ExitStack()
        SA = lambda n, s, d: stA.enter_context(nc.sbuf_tensor(n, s, d))
        fmbuf = SA("fmbufA", [128, NCH * NT], BF16)
        fmv = fmbuf[:].rearrange("p (c n) -> p c n", c=NCH)
        sq_s = [SA(f"sq_sA{i}", [128, C], F32) for i in range(2)]
        xhat_s = [SA(f"xhat_sA{i}", [128, C], BF16) for i in range(2)]
        ropeA_s = SA("ropeA_s", [128, NT], F32)
        ropeB_s = SA("ropeB_s", [128, NT], F32)
        rope_tmpA = SA("rope_tmpA", [128, 512], F32)
        rope_tmpB = SA("rope_tmpB", [128, 512], F32)
        xin.bufs = [SA(f"xinA{i}", [128, C], F32) for i in range(2)]
        slab.bufs = [SA(f"slabA{i}", [128, NCH * 128], BF16) for i in range(3)]

        A_ORDER0 = [*range(16, 48), *range(0, 16)]
        for tt in range(NQT):
            for nm in ["la_sq", "la_stat", "la_sd", "la_xhat", "la_fmT", "la_fm"]:
                M[nm].idx(f"A{tt}")
        for oi in range(48):
            M["c_stage"].idx(f"A{oi}")

        def a_store(sy, oi):
            o = A_ORDER0[oi]
            sel, head = divmod(o, 16)
            stg = stage[oi % 3]
            sy.wait_ge(M["c_stage"].ctr.h, M["c_stage"].val(f"A{oi}"))
            stw_pre(sy, oi)
            if sel == 0:
                d = sy.dma_start(out=qbuf[head * 128:(head + 1) * 128, :], in_=stg[:])
            elif sel == 1:
                d = sy.dma_start(out=agin[head * 128:(head + 1) * 128, :], in_=stg[:])
            else:
                d = sy.dma_start(out=agin[C + head * 128:C + (head + 1) * 128, :], in_=stg[:])
            stw_inc(d, oi)

        @block.sync
        def _(sy):
            sy.wait_ge(s_consts.h, CONSTS)
            sy.dma_start(out=ropeA_s[:], in_=ropeA_in[:]).then_inc(s_consts.h, 16)
            sy.dma_start(out=ropeB_s[:], in_=ropeB_in[:]).then_inc(s_consts.h, 16)
            for tt in range(NQT):
                def fill(eng, buf, ld, tt=tt):
                    eng.dma_start(out=buf[:], in_=x_in[tt * 128:(tt + 1) * 128, :]
                                  ).then_inc(ld.h, 16)
                xin.load(sy, fill)
            sy.wait_ge(s_ag.h, 1)
            for oi in range(48):
                o = A_ORDER0[oi]
                def fill(eng, buf, ld, o=o):
                    eng.dma_start(out=buf[:], in_=bview("qkv", (o,))).then_inc(ld.h, 16)
                slab.load(sy, fill)
                if oi >= 2:
                    a_store(sy, oi - 2)
            a_store(sy, 46)
            a_store(sy, 47)
        CONSTS_A = CONSTS + 32

        @block.scalar
        def _(s):
            s.wait_ge(s_consts.h, CONSTS)
            for tt in range(NQT):
                xin.wait(s, tt)
                if tt >= 2:
                    s.wait_ge(M["la_stat"].ctr.h, M["la_stat"].val(f"A{tt - 2}"))
                a = s.activation(sq_s[tt % 2][:], xin.buf(tt)[:], AF.Square)
                M["la_sq"].inc(a, f"A{tt}")
                s.wait_ge(M["la_stat"].ctr.h, M["la_stat"].val(f"A{tt}"))
                a = s.activation(stat[:, 6:7], stat[:, 5:6], AF.Sqrt, bias=eps_s[:, 0:1])
                M["la_sd"].inc(a, f"A{tt}")

        @block.vector
        def _(v):
            for tt in range(NQT):
                xin.wait(v, tt)
                v.wait_ge(M["la_sq"].ctr.h, M["la_sq"].val(f"A{tt}"))
                v.reduce_sum(stat[:, 0:1], xin.buf(tt)[:], axis=AX.X)
                r2 = v.reduce_sum(stat[:, 1:2], sq_s[tt % 2][:], axis=AX.X)
                n = selfinc(v_self, v_n, r2)
                v.wait_ge(v_self.h, n)
                v.tensor_scalar(stat[:, 2:3], stat[:, 0:1], 1.0 / C, None, ALU.mult)
                t2 = v.tensor_scalar(stat[:, 3:4], stat[:, 1:2], 1.0 / C, None, ALU.mult)
                n = selfinc(v_self, v_n, t2)
                v.wait_ge(v_self.h, n)
                t3 = v.tensor_mul(stat[:, 4:5], stat[:, 2:3], stat[:, 2:3])
                n = selfinc(v_self, v_n, t3)
                v.wait_ge(v_self.h, n)
                t5 = v.tensor_sub(stat[:, 5:6], stat[:, 3:4], stat[:, 4:5])
                M["la_stat"].inc(t5, f"A{tt}")
                v.wait_ge(M["la_sd"].ctr.h, M["la_sd"].val(f"A{tt}"))
                rc = v.reciprocal(stat[:, 7:8], stat[:, 6:7])
                n = selfinc(v_self, v_n, rc)
                v.wait_ge(v_self.h, n)
                if tt >= 2:
                    v.wait_ge(M["la_fmT"].ctr.h, M["la_fmT"].val(f"A{tt - 2}"))
                tsx = v.tensor_scalar(xhat_s[tt % 2][:], xin.buf(tt)[:],
                                      stat[:, 2:3], stat[:, 7:8], ALU.subtract, ALU.mult)
                M["la_xhat"].inc(tsx, f"A{tt}")
                v.wait_ge(M["la_xhat"].ctr.h, M["la_xhat"].val(f"A{tt}"))
                xin.release_on(v.nop())
                xin.ncons += 1
                for ett in ([tt - 1] if tt >= 1 else []) + ([tt] if tt == NQT - 1 else []):
                    for half in range(2):
                        key = f"Afm{ett}_{half}"
                        bank = tpp.cons(v, key)
                        cp = v.tensor_copy(
                            fmv[:, half * 8:(half + 1) * 8, ett * 128:(ett + 1) * 128],
                            bank[:].rearrange("p (j n) -> p j n", j=8))
                        tpp.cons_inc(cp, key)
                    dc, dv_ = tpp.done_val(key)
                    v.wait_ge(dc.h, dv_)
                    M["la_fm"].inc(v.nop(), f"A{ett}")

        @block.tensor
        def _(t):
            t.wait_ge(s_consts.h, CONSTS_A)
            for tt in range(NQT):
                t.wait_ge(M["la_xhat"].ctr.h, M["la_xhat"].val(f"A{tt}"))
                for half in range(2):
                    key = f"Afm{tt}_{half}"
                    bank = tpp.prod(t, key)
                    for j in range(8):
                        c = half * 8 + j
                        mm = t.transpose(bank[:, j * 128:(j + 1) * 128],
                                         xhat_s[tt % 2][:, c * 128:(c + 1) * 128], id_bf_s[:])
                    tpp.prod_inc(mm, key)
                t.wait_ge(tpp.fill.ctr.h, tpp.fill.val(key))
                M["la_fmT"].inc(t.nop(), f"A{tt}")

        A_ORDER = [*range(16, 48), *range(0, 16)]

        @block.tensor
        def _(t):
            t.wait_ge(M["la_fm"].ctr.h, M["la_fm"].val(f"A{NQT - 1}"))
            for oi in range(48):
                slab.wait(t, oi)
                sb = slab.buf(oi)
                for mt in range(4):
                    key = f"Aqkv{oi}_{mt}"
                    bank = main.prod(t, key)
                    for k in range(NCH):
                        mm = t.matmul(bank[:], sb[:, k * 128:(k + 1) * 128],
                                      fmv[:, k, mt * 512:(mt + 1) * 512],
                                      start=(k == 0), stop=(k == NCH - 1))
                    main.prod_inc(mm, key)
                t.wait_ge(main.fill.ctr.h, main.fill.val(key))
                slab.release_on(t.nop())
                slab.ncons += 1

        @block.vector
        def _(v):
            v.wait_ge(s_consts.h, CONSTS_A)
            for oi in range(48):
                o = A_ORDER[oi]
                sel = o // 16
                stg = stage[oi % 3]
                if oi >= 3:
                    stw_one(v, oi - 3)
                for mt in range(4):
                    key = f"Aqkv{oi}_{mt}"
                    bank = main.cons(v, key)
                    ms = stg[:, mt * 512:(mt + 1) * 512]
                    bcol = bqkv_s[:, o:o + 1]
                    if sel == 2:
                        ev = v.tensor_scalar(ms, bank[:], bcol, None, ALU.add)
                        n = selfinc(v_self, v_n, ev)
                        v.wait_ge(v_self.h, n)
                        main.cons_inc(v.nop(), key)
                    else:
                        TAm = ropeA_s[:, mt * 512:(mt + 1) * 512]
                        TBm = ropeB_s[:, mt * 512:(mt + 1) * 512]
                        p1 = v.scalar_tensor_tensor(rope_tmpA[0:64, :], bank[0:64, :],
                                                    bcol[0:64, :], TAm[0:64, :], ALU.add, ALU.mult)
                        selfinc(v_self, v_n, p1)
                        p2 = v.scalar_tensor_tensor(rope_tmpB[0:64, :], bank[64:128, :],
                                                    bcol[64:128, :], TAm[64:128, :], ALU.add, ALU.mult)
                        selfinc(v_self, v_n, p2)
                        p3 = v.scalar_tensor_tensor(rope_tmpA[64:128, :], bank[0:64, :],
                                                    bcol[0:64, :], TBm[0:64, :], ALU.add, ALU.mult)
                        selfinc(v_self, v_n, p3)
                        p4 = v.scalar_tensor_tensor(rope_tmpB[64:128, :], bank[64:128, :],
                                                    bcol[64:128, :], TBm[64:128, :], ALU.add, ALU.mult)
                        n = selfinc(v_self, v_n, p4)
                        v.wait_ge(v_self.h, n)
                        main.cons_inc(v.nop(), key)
                        r1 = v.tensor_sub(ms[0:64, :], rope_tmpA[0:64, :], rope_tmpB[0:64, :])
                        selfinc(v_self, v_n, r1)
                        r2 = v.tensor_add(ms[64:128, :], rope_tmpA[64:128, :], rope_tmpB[64:128, :])
                        n = selfinc(v_self, v_n, r2)
                        v.wait_ge(v_self.h, n)
                M["c_stage"].inc(v.nop(), f"A{oi}")

        A_KV_ST = 32
        A_ST_TOTAL = 48

        @block.gpsimd
        def _(g):
            stw_all(g, A_KV_ST)
            g.collective_compute(
                "AllGather", ALU.bypass,
                replica_groups=[[0, 1], [2, 3], [4, 5], [6, 7]],
                ins=[agin[:]], outs=[agout[:]]).then_inc(s_ag.h, 1)
            for gk in ["G", "U", "D"]:
                cast_done(g, gk)
                g.collective_compute(
                    "AllGather", ALU.bypass,
                    replica_groups=[[0, 1, 2, 3, 4, 5, 6, 7]],
                    ins=[shard_bf[gk].rearrange("(a b) -> a b", b=2048)],
                    outs=[blob_t[gk].ap()]).then_inc(s_ag.h, 1)

        if upto_i <= 1 and ("agin" in dbg or "q" in dbg):
            @block.sync
            def _(sy):
                stw_all(sy, A_ST_TOTAL)
                if "agin" in dbg:
                    sy.dma_start(out=dbg["agin"][:], in_=agin[:]).then_inc(st_dbg.h, 16)
                if "q" in dbg:
                    sy.dma_start(out=dbg["q"][:], in_=qbuf[:]).then_inc(st_dbg.h, 16)
        if "agout" in dbg:
            @block.sync
            def _(sy):
                sy.wait_ge(s_ag.h, 3)
                sy.dma_start(out=dbg["agout"][:], in_=agout[:]).then_inc(st_dbg.h, 16)

        if upto_i == 0:
            stA.close()

        def emit_attention(tag, ag_val, qst_store_base, av_dst, extra_waits,
                           prev_avld_fills):
            stX = contextlib.ExitStack()
            SX = lambda n, s, d: stX.enter_context(nc.sbuf_tensor(n, s, d))
            Kst.bufs = [SX(f"K{tag}{i}", [128, T], BF16) for i in range(2)]
            Vst.bufs = [SX(f"V{tag}{i}", [128, T], BF16) for i in range(2)]
            Qst.bufs = [SX(f"Q{tag}{i}", [128, NT], BF16) for i in range(2)]
            VT_s = SX(f"VT{tag}", [128, T], BF16)
            Ksq_s = SX(f"Ksq{tag}", [128, T], BF16)
            Qsq_s = SX(f"Qsq{tag}", [128, NT], BF16)
            kn2row = SX(f"kn2row{tag}", [1, T], F32)
            qn2row = SX(f"qn2row{tag}", [1, NT], F32)
            knmax = SX(f"knmax{tag}", [1, 1], F32)
            mrow = SX(f"mrow{tag}", [1, NT], F32)
            mhat_cols = SX(f"mhat{tag}", [128, NQT], F32)
            cm_s = SX(f"cm{tag}", [128, NQT * 512], F32)
            Prow = [SX(f"Prow{tag}{i}", [128, T], BF16) for i in range(3)]
            Smask = [SX(f"Smask{tag}{i}", [128, 512], F32) for i in range(3)]
            PT_s = SX(f"PT{tag}", [128, 2 * T], BF16)
            lparts = SX(f"lparts{tag}", [128, NQT * 8], F32)
            l_cols = SX(f"lcols{tag}", [128, NQT], F32)
            linv_cols = SX(f"linv{tag}", [128, NQT], F32)
            av_big = [SX(f"avbig{tag}{i}", [128, C], BF16) for i in range(2)]

            kb0, vb0, qb0 = Kst.nfill, Vst.nfill, Qst.nfill
            for h in range(H):
                for nm in ["at_sq", "at_sqrt", "at_knorm", "at_mrow", "at_mhat",
                           "at_normmm", "at_pvmm"]:
                    M[nm].idx(f"{tag}{h}")
                for i in range(NQT):
                    for nm in ["at_smask", "at_exp", "at_linv", "at_prowsc",
                               "at_pt", "at_ptmm"]:
                        M[nm].idx(f"{tag}{h}_{i}")
                for jp in range(8):
                    M["at_pvp"].idx(f"{tag}{h}_{jp}")
                    M["at_av"].idx(f"{tag}{h}_{jp}")
            g0 = g_n[0]
            atv0 = ATV_BASE[tag]

            def att_store(sy, h):
                sy.wait_ge(M["at_av"].ctr.h, M["at_av"].val(f"{tag}{h}_7"))
                sy.wait_ge(st_attv.h, 16 * (atv0 + h))
                sy.dma_start(out=av_dst[h * 128:(h + 1) * 128, :],
                             in_=av_big[h % 2][:]).then_inc(st_attv.h, 16)

            @block.sync
            def _(sy):
                sy.wait_ge(s_ag.h, ag_val)
                for ctr, val in extra_waits:
                    sy.wait_ge(ctr.h, val)
                if prev_avld_fills:
                    avst.wait_all(sy, prev_avld_fills)
                sy.dma_start(out=cm_s[:].rearrange("p (a c) -> p a c", a=NQT),
                             in_=cm_in.rearrange("a b c -> b a c")
                             ).then_inc(s_consts.h, 16)
                for h in range(H):
                    def fillK(eng, buf, ld, h=h):
                        for par in range(2):
                            eng.dma_start(
                                out=buf[:].rearrange("p (g two n) -> p g two n", two=2, n=128)[:, :, par, :],
                                in_=agout[par * 2 * C + h * 128: par * 2 * C + (h + 1) * 128, :]
                                .rearrange("p (g n) -> p g n", n=128)
                            ).then_inc(ld.h, 16)
                    Kst.load(sy, fillK)
                    def fillV(eng, buf, ld, h=h):
                        for par in range(2):
                            eng.dma_start(
                                out=buf[:].rearrange("p (g two n) -> p g two n", two=2, n=128)[:, :, par, :],
                                in_=agout[par * 2 * C + C + h * 128: par * 2 * C + C + (h + 1) * 128, :]
                                .rearrange("p (g n) -> p g n", n=128)
                            ).then_inc(ld.h, 16)
                    Vst.load(sy, fillV)
                    def fillQ(eng, buf, ld, h=h):
                        stw_one(eng, qst_store_base + h)
                        eng.dma_start(out=buf[:], in_=qbuf[h * 128:(h + 1) * 128, :]
                                      ).then_inc(ld.h, 16)
                    Qst.load(sy, fillQ)
                    if h >= 1:
                        att_store(sy, h - 1)

            @block.tensor
            def _(t):
                for h in range(H):
                    K_ = Kst.buf(kb0 + h)
                    V_ = Vst.buf(vb0 + h)
                    Q_ = Qst.buf(qb0 + h)
                    Vst.wait(t, vb0 + h)
                    if h > 0:
                        t.wait_ge(M["at_pvmm"].ctr.h, M["at_pvmm"].val(f"{tag}{h - 1}"))
                    for grp in range(4):
                        key = f"{tag}vt{h}_{grp}"
                        bank = tpp.prod(t, key)
                        for j in range(8):
                            jb = grp * 8 + j
                            mm = t.transpose(bank[:, j * 128:(j + 1) * 128],
                                             V_[:, jb * 128:(jb + 1) * 128], id_bf_s[:])
                        tpp.prod_inc(mm, key)
                    Kst.wait(t, kb0 + h)
                    Qst.wait(t, qb0 + h)
                    t.wait_ge(M["at_sq"].ctr.h, M["at_sq"].val(f"{tag}{h}"))
                    for kb in range(8):
                        key = f"{tag}kn{h}_{kb}"
                        bank = misc.prod(t, key)
                        mm = t.matmul(bank[0:1, :], ones_col_s[:],
                                      Ksq_s[:, kb * 512:(kb + 1) * 512], start=True, stop=True)
                        misc.prod_inc(mm, key)
                    for qb in range(4):
                        key = f"{tag}qn{h}_{qb}"
                        bank = misc.prod(t, key)
                        mm = t.matmul(bank[0:1, :], ones_col_s[:],
                                      Qsq_s[:, qb * 512:(qb + 1) * 512], start=True, stop=True)
                        misc.prod_inc(mm, key)
                    t.wait_ge(misc.fill.ctr.h, misc.fill.val(f"{tag}qn{h}_3"))
                    M["at_normmm"].inc(t.nop(), f"{tag}{h}")
                    t.wait_ge(M["at_mrow"].ctr.h, M["at_mrow"].val(f"{tag}{h}"))
                    key = f"{tag}mh{h}"
                    bank = misc.prod(t, key)
                    for i in range(NQT):
                        mm = t.transpose(bank[:, i:i + 1], mrow[:, i * 128:(i + 1) * 128],
                                         id_f_s[0:1, 0:1])
                    misc.prod_inc(mm, key)
                    t.wait_ge(M["at_mhat"].ctr.h, M["at_mhat"].val(f"{tag}{h}"))
                    for i in range(NQT + 2):
                        if i < NQT:
                            Pi = 512 * (i // 2 + 1)
                            for kb in range(Pi // 512):
                                key = f"{tag}S{h}_{i}_{kb}"
                                bank = main.prod(t, key)
                                mm = t.matmul(bank[:], Q_[:, i * 128:(i + 1) * 128],
                                              K_[:, kb * 512:(kb + 1) * 512],
                                              start=True, stop=True)
                                main.prod_inc(mm, key)
                        if i >= 2:
                            ip = i - 2
                            Pp = 512 * (ip // 2 + 1)
                            t.wait_ge(M["at_prowsc"].ctr.h,
                                      M["at_prowsc"].val(f"{tag}{h}_{ip}"))
                            nb128 = Pp // 128
                            pr = Prow[ip % 3]
                            for grp in range((nb128 + 7) // 8):
                                key = f"{tag}pt{h}_{ip}_{grp}"
                                bank = tpp.prod(t, key)
                                nj = min(8, nb128 - grp * 8)
                                for j in range(nj):
                                    jb = grp * 8 + j
                                    mm = t.transpose(bank[:, j * 128:(j + 1) * 128],
                                                     pr[:, jb * 128:(jb + 1) * 128], id_bf_s[:])
                                tpp.prod_inc(mm, key)
                            t.wait_ge(tpp.fill.ctr.h, tpp.fill.val(key))
                            M["at_ptmm"].inc(t.nop(), f"{tag}{h}_{ip}")
                            if ip % 2 == 1:
                                jp = ip // 2
                                t.wait_ge(M["at_pt"].ctr.h, M["at_pt"].val(f"{tag}{h}_{ip}"))
                                nkb = 4 * (jp + 1)
                                key = f"{tag}pv{h}_{jp}"
                                bank = alt.prod(t, key)
                                for jb in range(nkb):
                                    mm = t.matmul(bank[:, 0:256],
                                                  VT_s[:, jb * 128:(jb + 1) * 128],
                                                  PT_s[:, jb * 256:(jb + 1) * 256],
                                                  start=(jb == 0), stop=(jb == nkb - 1))
                                alt.prod_inc(mm, key)
                                t.wait_ge(alt.fill.ctr.h, alt.fill.val(key))
                                M["at_pvp"].inc(t.nop(), f"{tag}{h}_{jp}")
                    M["at_pvmm"].inc(t.nop(), f"{tag}{h}")
                    Kst.release_on(t.nop()); Kst.ncons += 1
                    Vst.release_on(t.nop()); Vst.ncons += 1
                    Qst.release_on(t.nop()); Qst.ncons += 1

            @block.scalar
            def _(s):
                s.wait_ge(s_consts.h, consts_after_att[tag])
                for h in range(H):
                    K_ = Kst.buf(kb0 + h)
                    Q_ = Qst.buf(qb0 + h)
                    Kst.wait(s, kb0 + h)
                    Qst.wait(s, qb0 + h)
                    if h > 0:
                        s.wait_ge(M["at_normmm"].ctr.h, M["at_normmm"].val(f"{tag}{h - 1}"))
                    s.activation(Ksq_s[:], K_[:], AF.Square)
                    qsq = s.activation(Qsq_s[:], Q_[:], AF.Square)
                    M["at_sq"].inc(qsq, f"{tag}{h}")
                    s.wait_ge(M["at_knorm"].ctr.h, M["at_knorm"].val(f"{tag}{h}"))
                    s.activation(knmax[:], knmax[:], AF.Sqrt)
                    sq2 = s.activation(qn2row[:], qn2row[:], AF.Sqrt)
                    M["at_sqrt"].inc(sq2, f"{tag}{h}")
                    s.wait_ge(M["at_mhat"].ctr.h, M["at_mhat"].val(f"{tag}{h}"))
                    for i in range(NQT):
                        Pi = 512 * (i // 2 + 1)
                        nb = Pi // 512
                        pr = Prow[i % 3]
                        if i >= 3:
                            s.wait_ge(M["at_ptmm"].ctr.h, M["at_ptmm"].val(f"{tag}{h}_{i - 3}"))
                        elif h > 0:
                            s.wait_ge(M["at_ptmm"].ctr.h,
                                      M["at_ptmm"].val(f"{tag}{h - 1}_{NQT - 3 + i}"))
                        for kb in range(nb):
                            key = f"{tag}S{h}_{i}_{kb}"
                            if kb == nb - 1:
                                s.wait_ge(M["at_smask"].ctr.h, M["at_smask"].val(f"{tag}{h}_{i}"))
                                ex = s.activation(pr[:, kb * 512:(kb + 1) * 512],
                                                  Smask[i % 3][:], AF.Exp,
                                                  bias=mhat_cols[:, i:i + 1], scale=SCALE,
                                                  accum_out=lparts[:, i * 8 + kb:i * 8 + kb + 1])
                            else:
                                bank = main.cons(s, key)
                                ex = s.activation(pr[:, kb * 512:(kb + 1) * 512],
                                                  bank[:], AF.Exp,
                                                  bias=mhat_cols[:, i:i + 1], scale=SCALE,
                                                  accum_out=lparts[:, i * 8 + kb:i * 8 + kb + 1])
                                main.cons_inc(ex, key)
                        M["at_exp"].inc(ex, f"{tag}{h}_{i}")

            @block.vector
            def _(v):
                v.wait_ge(s_consts.h, consts_after_att[tag])
                for h in range(H):
                    if h > 0:
                        v.wait_ge(M["at_pvmm"].ctr.h, M["at_pvmm"].val(f"{tag}{h - 1}"))
                    for grp in range(4):
                        key = f"{tag}vt{h}_{grp}"
                        bank = tpp.cons(v, key)
                        cp = v.tensor_copy(VT_s[:, grp * 1024:(grp + 1) * 1024], bank[:])
                        tpp.cons_inc(cp, key)
                    for kb in range(8):
                        key = f"{tag}kn{h}_{kb}"
                        bank = misc.cons(v, key)
                        cp = v.tensor_copy(kn2row[:, kb * 512:(kb + 1) * 512], bank[0:1, :])
                        misc.cons_inc(cp, key)
                    for qb in range(4):
                        key = f"{tag}qn{h}_{qb}"
                        bank = misc.cons(v, key)
                        cp = v.tensor_copy(qn2row[:, qb * 512:(qb + 1) * 512], bank[0:1, :])
                        misc.cons_inc(cp, key)
                    n = selfinc(v_self, v_n, v.nop())
                    v.wait_ge(v_self.h, n)
                    rm = v.reduce_max(knmax[:], kn2row[:], axis=AX.X)
                    M["at_knorm"].inc(rm, f"{tag}{h}")
                    v.wait_ge(M["at_sqrt"].ctr.h, M["at_sqrt"].val(f"{tag}{h}"))
                    mr = v.tensor_scalar(mrow[:], qn2row[:], knmax[0:1, 0:1],
                                         -SCALE * 1.02, ALU.mult, ALU.mult)
                    M["at_mrow"].inc(mr, f"{tag}{h}")
                    key = f"{tag}mh{h}"
                    bank = misc.cons(v, key)
                    if h > 0:
                        v.wait_ge(M["at_exp"].ctr.h, M["at_exp"].val(f"{tag}{h - 1}_{NQT - 1}"))
                    cp = v.tensor_copy(mhat_cols[:], bank[:, 0:NQT])
                    M["at_mhat"].inc(cp, f"{tag}{h}")
                    v.wait_ge(M["at_mhat"].ctr.h, M["at_mhat"].val(f"{tag}{h}"))
                    misc.cons_inc(v.nop(), key)
                    for i in range(NQT + 2):
                        if i < NQT:
                            Pi = 512 * (i // 2 + 1)
                            nb = Pi // 512
                            key = f"{tag}S{h}_{i}_{nb - 1}"
                            bank = main.cons(v, key)
                            if i >= 3:
                                v.wait_ge(M["at_exp"].ctr.h, M["at_exp"].val(f"{tag}{h}_{i - 3}"))
                            elif h > 0:
                                v.wait_ge(M["at_exp"].ctr.h,
                                          M["at_exp"].val(f"{tag}{h - 1}_{NQT - 3 + i}"))
                            stt = v.scalar_tensor_tensor(Smask[i % 3][:], bank[:], 1.0,
                                                         cm_s[:, i * 512:(i + 1) * 512],
                                                         ALU.mult, ALU.add)
                            M["at_smask"].inc(stt, f"{tag}{h}_{i}")
                            v.wait_ge(M["at_smask"].ctr.h, M["at_smask"].val(f"{tag}{h}_{i}"))
                            main.cons_inc(v.nop(), key)
                            v.wait_ge(M["at_exp"].ctr.h, M["at_exp"].val(f"{tag}{h}_{i}"))
                            if h > 0:
                                v.wait_ge(M["at_prowsc"].ctr.h,
                                          M["at_prowsc"].val(f"{tag}{h - 1}_{i}"))
                            rs = v.reduce_sum(l_cols[:, i:i + 1], lparts[:, i * 8:i * 8 + nb], axis=AX.X)
                            n = selfinc(v_self, v_n, rs)
                            v.wait_ge(v_self.h, n)
                            rc = v.reciprocal(linv_cols[:, i:i + 1], l_cols[:, i:i + 1])
                            M["at_linv"].inc(rc, f"{tag}{h}_{i}")
                            v.wait_ge(M["at_linv"].ctr.h, M["at_linv"].val(f"{tag}{h}_{i}"))
                            pr = Prow[i % 3]
                            rsc = v.tensor_scalar(pr[:, 0:Pi], pr[:, 0:Pi],
                                                  linv_cols[:, i:i + 1], None, ALU.mult)
                            M["at_prowsc"].inc(rsc, f"{tag}{h}_{i}")
                        if i >= 2:
                            iq = i - 2
                            Pq = 512 * (iq // 2 + 1)
                            nb128 = Pq // 128
                            if iq >= 2:
                                v.wait_ge(M["at_pvp"].ctr.h,
                                          M["at_pvp"].val(f"{tag}{h}_{iq // 2 - 1}"))
                            for grp in range((nb128 + 7) // 8):
                                key = f"{tag}pt{h}_{iq}_{grp}"
                                bank = tpp.cons(v, key)
                                nj = min(8, nb128 - grp * 8)
                                cp = v.tensor_copy(
                                    PT_s[:].rearrange("p (g two n) -> p g two n", two=2, n=128)[
                                        :, grp * 8:grp * 8 + nj, iq % 2, :],
                                    bank[:, 0:nj * 128].rearrange("p (j n) -> p j n", n=128))
                                tpp.cons_inc(cp, key)
                            dc, dv_ = tpp.done_val(key)
                            v.wait_ge(dc.h, dv_)
                            M["at_pt"].inc(v.nop(), f"{tag}{h}_{iq}")
                            if iq % 2 == 1:
                                jp = iq // 2
                                key = f"{tag}pv{h}_{jp}"
                                bank = alt.cons(v, key)
                                if jp == 0 and h >= 2:
                                    v.wait_ge(st_attv.h, 16 * (atv0 + h - 1))
                                ev = v.tensor_copy(av_big[h % 2][:, jp * 256:(jp + 1) * 256],
                                                   bank[:, 0:256])
                                M["at_av"].inc(ev, f"{tag}{h}_{jp}")
                                v.wait_ge(M["at_av"].ctr.h, M["at_av"].val(f"{tag}{h}_{jp}"))
                                alt.cons_inc(v.nop(), key)

            @block.sync
            def _(sy):
                att_store(sy, H - 1)
            return stX

        consts_after_att = {"B": CONSTS_A + 16, "D": CONSTS_A + 32}
        ATV_BASE = {"B": 0, "D": 16}

        ATV_TOT = 16
        if upto_i >= 1:
            stA.close()
            stB = emit_attention("B", 3, 32, attv, stw_pairs(A_ST_TOTAL), 0)
            if "attv" in dbg and upto_i <= 2:
                @block.sync
                def _(sy):
                    sy.wait_ge(st_attv.h, 16 * ATV_TOT)
                    sy.dma_start(out=dbg["attv"][:], in_=attv[:]).then_inc(st_dbg.h, 16)

        if upto_i >= 2:
            stB.close()
            stC = contextlib.ExitStack()
            SC = lambda n, s, d: stC.enter_context(nc.sbuf_tensor(n, s, d))
            fmbufC = SC("fmbufC", [128, NCH * NT], BF16)
            fmvC = fmbufC[:].rearrange("p (c n) -> p c n", c=NCH)
            sqC = [SC(f"sqC{i}", [128, C], F32) for i in range(2)]
            xhatC = [SC(f"xhatC{i}", [128, C], BF16) for i in range(2)]
            x1t = [SC(f"x1t{i}", [128, C], F32) for i in range(2)]
            xft = [SC(f"xft{i}", [128, C], F32) for i in range(2)]
            xin.bufs = [SC(f"xinC{i}", [128, C], F32) for i in range(2)]
            slab.bufs = [SC(f"slabC{i}", [128, NCH * 128], BF16) for i in range(3)]
            wrow.bufs = [SC(f"wrowC{i}", [128, C], BF16) for i in range(3)]
            avst.bufs = [SC(f"avstC{i}", [128, C], BF16) for i in range(2)]

            xb0, avb0, wrb0, slb0 = xin.nfill, avst.nfill, wrow.nfill, slab.nfill
            for tt in range(NQT):
                M["e_acc"].idx(f"C{tt}")
                M["la_sq"].idx(f"C{tt}")
                M["la_stat"].idx(f"C{tt}")
                M["la_sd"].idx(f"C{tt}")
                M["c_stage"].idx(f"Cxf{tt}")
                M["la_sq"].idx(f"C{tt}b")
                M["la_stat"].idx(f"C{tt}b")
                M["la_sd"].idx(f"C{tt}b")
                M["la_xhat"].idx(f"C{tt}")
                M["la_fmT"].idx(f"C{tt}")
                M["la_fm"].idx(f"C{tt}")
            for oi in range(48):
                M["c_stage"].idx(f"Cq{oi}")
            C_ATT_B_TOTAL = 16

            def c_xstore(sy, tt):
                sy.wait_ge(M["e_acc"].ctr.h, M["e_acc"].val(f"C{tt}"))
                sy.wait_ge(st_x1.h, 16 * tt)
                sy.dma_start(out=x1_hbm[tt * 128:(tt + 1) * 128, :],
                             in_=x1t[tt % 2][:]).then_inc(st_x1.h, 16)
                sy.wait_ge(M["c_stage"].ctr.h, M["c_stage"].val(f"Cxf{tt}"))
                sy.wait_ge(st_xf.h, 16 * tt)
                sy.dma_start(out=xf_hbm[tt * 128:(tt + 1) * 128, :],
                             in_=xft[tt % 2][:]).then_inc(st_xf.h, 16)

            def c_qstore(sy, oi):
                sel, head = divmod(oi, 16)
                stg = stage[oi % 3]
                sy.wait_ge(M["c_stage"].ctr.h, M["c_stage"].val(f"Cq{oi}"))
                stw_pre(sy, 48 + oi)
                if sel == 0:
                    d = sy.dma_start(out=agin[head * 128:(head + 1) * 128, :], in_=stg[:])
                elif sel == 1:
                    d = sy.dma_start(out=agin[C + head * 128:C + (head + 1) * 128, :], in_=stg[:])
                else:
                    d = sy.dma_start(out=qbuf[head * 128:(head + 1) * 128, :], in_=stg[:])
                stw_inc(d, 48 + oi)

            n_dbgA = (sum(1 for k in ("agin", "q", "agout") if k in dbg)
                      if upto_i <= 1 else
                      sum(1 for k in ("agout",) if k in dbg))

            @block.sync
            def _(sy):
                sy.wait_ge(st_attv.h, 16 * C_ATT_B_TOTAL)
                sy.wait_ge(s_ag.h, 2)
                if n_dbgA:
                    sy.wait_ge(st_dbg.h, 16 * n_dbgA)
                for tt in range(NQT):
                    def fillav(eng, buf, ld, tt=tt):
                        eng.dma_start(
                            out=buf[:].rearrange("p (c n) -> p c n", c=NCH),
                            in_=attv[:, tt * 128:(tt + 1) * 128]
                            .rearrange("(c p) n -> p c n", p=128)
                        ).then_inc(ld.h, 16)
                    avst.load(sy, fillav)
                    def fillx(eng, buf, ld, tt=tt):
                        eng.dma_start(out=buf[:], in_=x_in[tt * 128:(tt + 1) * 128, :]
                                      ).then_inc(ld.h, 16)
                    xin.load(sy, fillx)
                    for c in range(NCH):
                        def fillw(eng, buf, ld, c=c):
                            eng.dma_start(out=buf[:], in_=bview("woutT")[c * 128:(c + 1) * 128, :]
                                          ).then_inc(ld.h, 16)
                        wrow.load(sy, fillw)
                    if tt >= 1:
                        c_xstore(sy, tt - 1)
                c_xstore(sy, NQT - 1)
                sy.wait_ge(s_ag.h, 3)
                Qst.wait_all(sy, 16)
                for oi in range(48):
                    nm, o = [("wk", oi), ("wv", oi - 16), ("wq", oi - 32)][oi // 16]
                    def fill(eng, buf, ld, nm=nm, o=o):
                        eng.dma_start(out=buf[:], in_=bview(nm, (o,))).then_inc(ld.h, 16)
                    slab.load(sy, fill)
                    if oi >= 2:
                        c_qstore(sy, oi - 2)
                c_qstore(sy, 46)
                c_qstore(sy, 47)

            @block.tensor
            def _(t):
                t.wait_ge(s_cconst.h, 48)
                for tt in range(NQT):
                    avst.wait(t, avb0 + tt)
                    av = avst.buf(avb0 + tt)
                    avv = av[:].rearrange("p (c n) -> p c n", c=NCH)
                    keys = [f"Cop{tt}_{j}" for j in range(4)]
                    banks = [main.prod(t, keys[0]), main.prod(t, keys[1]),
                             alt.prod(t, keys[2]), alt.prod(t, keys[3])]
                    for c in range(NCH):
                        wrow.wait(t, wrb0 + tt * NCH + c)
                        wr = wrow.buf(wrb0 + tt * NCH + c)
                        for j in range(4):
                            mmw = t.matmul(banks[j][:], avv[:, c, :], wr[:, j * 512:(j + 1) * 512],
                                           start=(c == 0), stop=False)
                        wrow.release_on(mmw)
                        wrow.ncons += 1
                    for j in range(4):
                        mm = t.matmul(banks[j][:], ones_row_bf_s[:],
                                      bout_bf[0:1, j * 512:(j + 1) * 512],
                                      start=False, stop=True)
                        (main if j < 2 else alt).prod_inc(mm, keys[j])
                    t.wait_ge(alt.fill.ctr.h, alt.fill.val(keys[3]))
                    avst.release_on(t.nop())
                    avst.ncons += 1
                    t.wait_ge(M["la_xhat"].ctr.h, M["la_xhat"].val(f"C{tt}"))
                    for half in range(2):
                        key = f"Cfm{tt}_{half}"
                        bank = tpp.prod(t, key)
                        for j in range(8):
                            c = half * 8 + j
                            mm = t.transpose(bank[:, j * 128:(j + 1) * 128],
                                             xhatC[tt % 2][:, c * 128:(c + 1) * 128], id_bf_s[:])
                        tpp.prod_inc(mm, key)
                    t.wait_ge(tpp.fill.ctr.h, tpp.fill.val(key))
                    M["la_fmT"].inc(t.nop(), f"C{tt}")
                t.wait_ge(M["la_fm"].ctr.h, M["la_fm"].val(f"C{NQT - 1}"))
                for oi in range(48):
                    slab.wait(t, slb0 + oi)
                    sb = slab.buf(slb0 + oi)
                    for mt in range(4):
                        key = f"Cqkv{oi}_{mt}"
                        bank = main.prod(t, key)
                        for k in range(NCH):
                            mm = t.matmul(bank[:], sb[:, k * 128:(k + 1) * 128],
                                          fmvC[:, k, mt * 512:(mt + 1) * 512],
                                          start=(k == 0), stop=(k == NCH - 1))
                        main.prod_inc(mm, key)
                    t.wait_ge(main.fill.ctr.h, main.fill.val(key))
                    slab.release_on(t.nop())
                    slab.ncons += 1

            @block.scalar
            def _(s):
                for tt in range(NQT):
                    s.wait_ge(M["e_acc"].ctr.h, M["e_acc"].val(f"C{tt}"))
                    if tt >= 2:
                        s.wait_ge(M["la_stat"].ctr.h, M["la_stat"].val(f"C{tt - 2}b"))
                    a = s.activation(sqC[tt % 2][:], x1t[tt % 2][:], AF.Square)
                    M["la_sq"].inc(a, f"C{tt}")
                    s.wait_ge(M["la_stat"].ctr.h, M["la_stat"].val(f"C{tt}"))
                    a = s.activation(stat[:, 6:7], stat[:, 5:6], AF.Sqrt, bias=eps_s[:, 0:1])
                    M["la_sd"].inc(a, f"C{tt}")
                    s.wait_ge(M["c_stage"].ctr.h, M["c_stage"].val(f"Cxf{tt}"))
                    a = s.activation(sqC[tt % 2][:], xft[tt % 2][:], AF.Square)
                    M["la_sq"].inc(a, f"C{tt}b")
                    s.wait_ge(M["la_stat"].ctr.h, M["la_stat"].val(f"C{tt}b"))
                    a = s.activation(stat[:, 14:15], stat[:, 13:14], AF.Sqrt, bias=eps_s[:, 1:2])
                    M["la_sd"].inc(a, f"C{tt}b")

            @block.vector
            def _(v):
                v.wait_ge(s_cconst.h, 48)
                for tt in range(NQT):
                    xin.wait(v, xb0 + tt)
                    xtile = xin.buf(xb0 + tt)
                    if tt >= 2:
                        v.wait_ge(st_x1.h, 16 * (tt - 1))
                    keys = [f"Cop{tt}_{j}" for j in range(4)]
                    for j in range(4):
                        bank = (main if j < 2 else alt).cons(v, keys[j])
                        ev = v.scalar_tensor_tensor(x1t[tt % 2][:, j * 512:(j + 1) * 512],
                                                    bank[:], 1.0,
                                                    xtile[:, j * 512:(j + 1) * 512],
                                                    ALU.mult, ALU.add)
                        if j == 3:
                            M["e_acc"].inc(ev, f"C{tt}")
                            v.wait_ge(M["e_acc"].ctr.h, M["e_acc"].val(f"C{tt}"))
                            (main if j < 2 else alt).cons_inc(v.nop(), keys[j])
                            xin.release_on(v.nop())
                            xin.ncons += 1
                        else:
                            (main if j < 2 else alt).cons_inc(ev, keys[j])
                    v.wait_ge(M["la_sq"].ctr.h, M["la_sq"].val(f"C{tt}"))
                    v.reduce_sum(stat[:, 0:1], x1t[tt % 2][:], axis=AX.X)
                    r2 = v.reduce_sum(stat[:, 1:2], sqC[tt % 2][:], axis=AX.X)
                    n = selfinc(v_self, v_n, r2)
                    v.wait_ge(v_self.h, n)
                    v.tensor_scalar(stat[:, 2:3], stat[:, 0:1], 1.0 / C, None, ALU.mult)
                    t2 = v.tensor_scalar(stat[:, 3:4], stat[:, 1:2], 1.0 / C, None, ALU.mult)
                    n = selfinc(v_self, v_n, t2)
                    v.wait_ge(v_self.h, n)
                    t3 = v.tensor_mul(stat[:, 4:5], stat[:, 2:3], stat[:, 2:3])
                    n = selfinc(v_self, v_n, t3)
                    v.wait_ge(v_self.h, n)
                    t5 = v.tensor_sub(stat[:, 5:6], stat[:, 3:4], stat[:, 4:5])
                    M["la_stat"].inc(t5, f"C{tt}")
                    v.wait_ge(M["la_sd"].ctr.h, M["la_sd"].val(f"C{tt}"))
                    rc = v.reciprocal(stat[:, 7:8], stat[:, 6:7])
                    n = selfinc(v_self, v_n, rc)
                    v.wait_ge(v_self.h, n)
                    if tt >= 2:
                        v.wait_ge(st_xf.h, 16 * (tt - 1))
                    xfe = v.tensor_scalar(xft[tt % 2][:], x1t[tt % 2][:],
                                          stat[:, 2:3], stat[:, 7:8], ALU.subtract, ALU.mult)
                    M["c_stage"].inc(xfe, f"Cxf{tt}")
                    v.wait_ge(M["la_sq"].ctr.h, M["la_sq"].val(f"C{tt}b"))
                    r2 = v.reduce_sum(stat[:, 12:13], sqC[tt % 2][:], axis=AX.X)
                    n = selfinc(v_self, v_n, r2)
                    v.wait_ge(v_self.h, n)
                    t2 = v.tensor_scalar(stat[:, 13:14], stat[:, 12:13], 1.0 / C, None, ALU.mult)
                    M["la_stat"].inc(t2, f"C{tt}b")
                    v.wait_ge(M["la_sd"].ctr.h, M["la_sd"].val(f"C{tt}b"))
                    rc = v.reciprocal(stat[:, 15:16], stat[:, 14:15])
                    n = selfinc(v_self, v_n, rc)
                    v.wait_ge(v_self.h, n)
                    h2f = v.tensor_scalar(sqC[tt % 2][:], xft[tt % 2][:],
                                          stat[:, 15:16], None, ALU.mult)
                    n = selfinc(v_self, v_n, h2f)
                    v.wait_ge(v_self.h, n)
                    if tt >= 2:
                        v.wait_ge(M["la_fmT"].ctr.h, M["la_fmT"].val(f"C{tt - 2}"))
                    h2b = v.tensor_mul(xhatC[tt % 2][:], sqC[tt % 2][:], ln1_bf[:])
                    M["la_xhat"].inc(h2b, f"C{tt}")
                    for ett in ([tt - 1] if tt >= 1 else []) + ([tt] if tt == NQT - 1 else []):
                        for half in range(2):
                            key = f"Cfm{ett}_{half}"
                            bank = tpp.cons(v, key)
                            cp = v.tensor_copy(
                                fmvC[:, half * 8:(half + 1) * 8, ett * 128:(ett + 1) * 128],
                                bank[:].rearrange("p (j n) -> p j n", j=8))
                            tpp.cons_inc(cp, key)
                        dc, dv_ = tpp.done_val(key)
                        v.wait_ge(dc.h, dv_)
                        M["la_fm"].inc(v.nop(), f"C{ett}")
                for oi in range(48):
                    stg = stage[oi % 3]
                    if oi >= 3:
                        stw_one(v, 48 + oi - 3)
                    for mt in range(4):
                        key = f"Cqkv{oi}_{mt}"
                        bank = main.cons(v, key)
                        ev = v.tensor_copy(stg[:, mt * 512:(mt + 1) * 512], bank[:])
                        n = selfinc(v_self, v_n, ev)
                        v.wait_ge(v_self.h, n)
                        main.cons_inc(v.nop(), key)
                    M["c_stage"].inc(v.nop(), f"Cq{oi}")

            C_ST_TOTAL = 48 + 48

            @block.gpsimd
            def _(g):
                stw_all(g, 48 + 32)
                Kst.wait_all(g, 16)
                Vst.wait_all(g, 16)
                g.collective_compute(
                    "AllGather", ALU.bypass,
                    replica_groups=[[0, 1], [2, 3], [4, 5], [6, 7]],
                    ins=[agin[:]], outs=[agout[:]]).then_inc(s_ag.h, 1)

            if dbg:
                @block.sync
                def _(sy):
                    sy.wait_ge(st_x1.h, 16 * 16)
                    sy.wait_ge(st_xf.h, 16 * 16)
                    if "x1" in dbg:
                        sy.dma_start(out=dbg["x1"][:], in_=x1_hbm[:]).then_inc(st_dbg.h, 16)
                    if "xf" in dbg:
                        sy.dma_start(out=dbg["xf"][:], in_=xf_hbm[:]).then_inc(st_dbg.h, 16)
                    if upto_i == 2:
                        stw_all(sy, 96)
                        if "agin" in dbg:
                            sy.dma_start(out=dbg["agin"][:], in_=agin[:]).then_inc(st_dbg.h, 16)
                        if "q" in dbg:
                            sy.dma_start(out=dbg["q"][:], in_=qbuf[:]).then_inc(st_dbg.h, 16)

        if upto_i >= 3:
            ATV_TOT = 32
            stC.close()
            stD = emit_attention("D", 7, 80, attv,
                                 stw_pairs(C_ST_TOTAL)
                                 + [(st_x1, 16 * 16), (st_xf, 16 * 16)],
                                 avst.nfill)
            if "attv" in dbg:
                @block.sync
                def _(sy):
                    sy.wait_ge(st_attv.h, 16 * 32)
                    sy.dma_start(out=dbg["attv"][:], in_=attv[:]).then_inc(st_dbg.h, 16)

        if upto_i >= 4:
            stE = contextlib.ExitStack()
            SE = lambda n, s, d: stE.enter_context(nc.sbuf_tensor(n, s, d))
            stD.close()
            tfm = SE("tfm", [128, NCH * MTOK], BF16)
            tfv = tfm[:].rearrange("p (c n) -> p c n", c=NCH)
            accs = [SE(f"acc{i}", [128, C], F32) for i in range(4)]
            sqE = SE("sqE", [128, C], F32)
            thatE = SE("thatE", [128, C], BF16)
            hh_t = SE("hh_t", [128, 2 * ISL * 512], BF16)
            hh0 = [SE(f"hh0_{i}", [128, 512], BF16) for i in range(2)]
            sg_t = [SE(f"sg_t{i}", [128, 512], BF16) for i in range(2)]
            cwb = SE("cwb", [128, MTOK], F32)
            logits_s = SE("logits_s", [128, 16], F32)
            el_s = SE("el_s", [128, 16], F32)
            top8_s = SE("top8_s", [128, 8], F32)
            cw_s = SE("cw_s", [128, 16], F32)
            cwT_s = SE("cwT_s", [16, 128], F32)
            avst.bufs = [SE(f"avstE{i}", [128, 8 * MTOK], BF16) for i in range(2)]
            wrow.bufs = [SE(f"wrowE{i}", [128, C], BF16) for i in range(3)]
            xin.bufs = [SE(f"xinE{i}", [128, C], F32) for i in range(2)]
            egst.bufs = [SE(f"egstE{i}", [128, NCH * 128], BF16) for i in range(2)]
            eust.bufs = [SE(f"eustE{i}", [128, NCH * 128], BF16) for i in range(2)]
            edst.bufs = [SE(f"edstE{i}", [128, ISL * 512], BF16) for i in range(2)]
            cwrow.bufs = [SE(f"cwrowE{i}", [1, MTOK], F32) for i in range(2)]

            D_ATT_TOTAL = 256
            EXPERTS = [*range(E_), "sh"]
            GUSEQ = []
            for m_ in range(NMACRO):
                for ei_, e_ in enumerate(EXPERTS):
                    for isl_ in range(ISL if e_ != "sh" else SISL):
                        GUSEQ.append((m_, ei_, isl_))
            GUPOS = {k: i for i, k in enumerate(GUSEQ)}

            def av_loads(sy, mm):
                for k in range(2):
                    def fillav(eng, buf, ld, k=k, mm=mm):
                        eng.dma_start(
                            out=buf[:].rearrange("p (c n) -> p c n", c=8),
                            in_=attv[k * 8 * 128:(k + 1) * 8 * 128,
                                     mm * MTOK:(mm + 1) * MTOK]
                            .rearrange("(c p) n -> p c n", p=128)
                        ).then_inc(ld.h, 16)
                    avst.load(sy, fillav)

            def wo_load(sy, c):
                def fillw(eng, buf, ld, c=c):
                    eng.dma_start(out=buf[:], in_=bview("wo")[c * 128:(c + 1) * 128, :]
                                  ).then_inc(ld.h, 16)
                wrow.load(sy, fillw)

            E_AV0, E_WR0 = avst.nfill, wrow.nfill
            for m in range(NMACRO):
                avb0, xb0, wrb0 = E_AV0 + 2 * m, xin.nfill, E_WR0 + 64 * m
                for tt in range(4):
                    for nm in ["e_acc", "la_sq", "la_stat", "la_sd"]:
                        M[nm].idx(f"E{m}_{tt}" if nm != "e_acc" else f"E{m}_{tt}")
                    for nm in ["e_that", "e_tfm", "e_glog", "e_gexp", "e_gcw", "e_gT"]:
                        M[nm].idx(f"{m}_{tt}")
                for ei, e in enumerate(EXPERTS):
                    nisl = ISL if e != "sh" else SISL
                    if e != "sh":
                        M["e_cwb"].idx(f"{m}_{e}")
                    for isl in range(nisl):
                        M["e_sg"].idx(f"{m}_{ei}_{isl}")
                        M["e_hh"].idx(f"{m}_{ei}_{isl}")
                    for dg in range(4):
                        for tt in range(4):
                            M["e_dn"].idx(f"{m}_{ei}_{dg}_{tt}")
                    M["e_dnmm"].idx(f"{m}_{ei}")
                for tt in range(4):
                    M["e_fin"].idx(f"{m}_{tt}")
                egb0, eub0, edb0, cwb0 = egst.nfill, eust.nfill, edst.nfill, cwrow.nfill

                @block.sync
                def _(sy):
                    if m == 0:
                        sy.wait_ge(st_attv.h, 16 * 32)
                        sy.wait_ge(s_ag.h, 6)
                        sy.dma_start(out=gate_sb[:], in_=bview("gate")
                                     ).then_inc(s_gate.h, 16)
                    def cw_store(tt):
                        sy.wait_ge(M["e_gT"].ctr.h, M["e_gT"].val(f"{m}_{tt}"))
                        if m > 0 and tt == 0:
                            cwrow.wait_all(sy, 16 * m)
                        sy.wait_ge(st_cw.h, 16 * (m * 4 + tt))
                        sy.dma_start(out=cw_dram[:, tt * 128:(tt + 1) * 128],
                                     in_=cwT_s[:]).then_inc(st_cw.h, 16)

                    if m == 0:
                        av_loads(sy, 0)
                    for tt in range(4):
                        gt = m * 4 + tt
                        def fillx(eng, buf, ld, gt=gt):
                            eng.wait_ge(st_xf.h, 16 * (gt + 1))
                            eng.dma_start(out=buf[:], in_=xf_hbm[gt * 128:(gt + 1) * 128, :]
                                          ).then_inc(ld.h, 16)
                        xin.load(sy, fillx)
                        for c in range(NCH):
                            if m >= 1 and tt == 0 and c < 3:
                                continue
                            wo_load(sy, c)
                        if tt >= 1:
                            cw_store(tt - 1)
                    cw_store(3)
                    for ei, e in enumerate(EXPERTS):
                        nisl = ISL if e != "sh" else SISL
                        if e != "sh":
                            def fillcw(eng, buf, ld, e=e):
                                eng.wait_ge(st_cw.h, 16 * 4 * (m + 1))
                                eng.dma_start(out=buf[:], in_=cw_dram[e:e + 1, :]
                                              ).then_inc(ld.h, 16)
                            cwrow.load(sy, fillcw)
                        for isl in range(nisl):
                            nm_g, nm_u = ("eg", "eu") if e != "sh" else ("sg", "su")
                            eidx = (e, isl) if e != "sh" else (isl,)
                            def fillg(eng, buf, ld, nm_g=nm_g, eidx=eidx):
                                eng.dma_start(out=buf[:], in_=bview(nm_g, eidx)
                                              ).then_inc(ld.h, 16)
                            egst.load(sy, fillg)
                            def fillu(eng, buf, ld, nm_u=nm_u, eidx=eidx):
                                eng.dma_start(out=buf[:], in_=bview(nm_u, eidx)
                                              ).then_inc(ld.h, 16)
                            eust.load(sy, fillu)
                        for dg in range(4):
                            nsets = 1 if e != "sh" else 2
                            for half in range(nsets):
                                def filld(eng, buf, ld, e=e, dg=dg, half=half):
                                    if e != "sh":
                                        src = bview("ed4", (e,))
                                        c0 = dg * ISL * 512
                                    else:
                                        src = bview("sd4")
                                        c0 = (dg * SISL + half * ISL) * 512
                                    eng.dma_start(
                                        out=buf[:],
                                        in_=src[:, c0:c0 + ISL * 512]
                                    ).then_inc(ld.h, 16)
                                edst.load(sy, filld)
                    for tt in range(4):
                        gt = m * 4 + tt
                        def fillx1(eng, buf, ld, gt=gt):
                            eng.wait_ge(st_x1.h, 16 * (gt + 1))
                            eng.dma_start(out=buf[:], in_=x1_hbm[gt * 128:(gt + 1) * 128, :]
                                          ).then_inc(ld.h, 16)
                        xin.load(sy, fillx1)
                    if m < NMACRO - 1:
                        av_loads(sy, m + 1)
                        for c in range(3):
                            wo_load(sy, c)
                    for tt in range(4):
                        gt = m * 4 + tt
                        sy.wait_ge(M["e_fin"].ctr.h, M["e_fin"].val(f"{m}_{tt}"))
                        sy.wait_ge(st_out.h, 16 * gt)
                        sy.dma_start(out=out_hbm[gt * 128:(gt + 1) * 128, :],
                                     in_=accs[tt][:]).then_inc(st_out.h, 16)

                @block.tensor
                def _(t):
                    for tt in range(4):
                        keys = [f"Eop{m}_{tt}_{j}" for j in range(4)]
                        banks = [main.prod(t, keys[0]), main.prod(t, keys[1]),
                                 alt.prod(t, keys[2]), alt.prod(t, keys[3])]
                        for c in range(NCH):
                            if tt == 0 and c % 8 == 0:
                                avst.wait(t, avb0 + c // 8)
                            av = avst.buf(avb0 + c // 8)
                            lhsT = av[:, (c % 8) * 512 + tt * 128:
                                      (c % 8) * 512 + tt * 128 + 128]
                            wrow.wait(t, wrb0 + tt * NCH + c)
                            wr = wrow.buf(wrb0 + tt * NCH + c)
                            for j in range(4):
                                mm = t.matmul(banks[j][:], lhsT,
                                              wr[:, j * 512:(j + 1) * 512],
                                              start=(c == 0), stop=(c == NCH - 1))
                            if c < NCH - 1:
                                wrow.release_on(mm)
                                wrow.ncons += 1
                        alt.prod_inc(mm, keys[2])
                        t.wait_ge(alt.fill.ctr.h, alt.fill.val(keys[2]))
                        main.prod_inc(t.nop(), keys[0])
                        main.prod_inc(t.nop(), keys[1])
                        alt.prod_inc(t.nop(), keys[3])
                        wrow.release_on(t.nop())
                        wrow.ncons += 1
                        if tt == 3:
                            avst.release_on(t.nop())
                            avst.release_on(t.nop())
                            avst.ncons += 2
                        t.wait_ge(M["e_that"].ctr.h, M["e_that"].val(f"{m}_{tt}"))
                        for half in range(2):
                            key = f"Efm{m}_{tt}_{half}"
                            bank = tpp.prod(t, key)
                            for j in range(8):
                                c = half * 8 + j
                                mm = t.transpose(bank[:, j * 128:(j + 1) * 128],
                                                 thatE[:, c * 128:(c + 1) * 128], id_bf_s[:])
                            tpp.prod_inc(mm, key)
                        M["la_fmT"].inc(t.nop(), f"E{m}_{tt}")
                        if m == 0 and tt == 0:
                            t.wait_ge(s_gate.h, 16)
                        t.wait_ge(M["e_tfm"].ctr.h, M["e_tfm"].val(f"{m}_{tt}"))
                        key = f"Eg{m}_{tt}"
                        bank = misc.prod(t, key)
                        for k in range(NCH):
                            mm = t.matmul(bank[:, 0:16], tfv[:, k, tt * 128:(tt + 1) * 128],
                                          gate_sb[:, k * 16:(k + 1) * 16],
                                          start=(k == 0), stop=(k == NCH - 1))
                        misc.prod_inc(mm, key)
                        t.wait_ge(M["e_gcw"].ctr.h, M["e_gcw"].val(f"{m}_{tt}"))
                        key = f"EgT{m}_{tt}"
                        bank = misc.prod(t, key)
                        mm = t.transpose(bank[0:16, 0:128], cw_s[:], id_f_s[:])
                        misc.prod_inc(mm, key)
                    for ei, e in enumerate(EXPERTS):
                        nisl = ISL if e != "sh" else SISL
                        if e != "sh":
                            key = f"Ecw{m}_{e}"
                            bank = misc.prod(t, key)
                            cwrow.wait(t, cwb0 + ei)
                            mm = t.matmul(bank[:], ones_row_f_s[:],
                                          cwrow.buf(cwb0 + ei)[:], start=True, stop=True)
                            misc.prod_inc(mm, key)
                            t.wait_ge(misc.fill.ctr.h, misc.fill.val(key))
                            cwrow.release_on(t.nop())
                            cwrow.ncons += 1
                        for isl in range(nisl):
                            gidx = egb0 + sum(ISL if x != "sh" else SISL for x in EXPERTS[:ei]) + isl
                            egst.wait(t, gidx)
                            eust.wait(t, gidx)
                            eg_ = egst.buf(gidx)
                            eu_ = eust.buf(gidx)
                            hhcol = (isl % (2 * ISL)) * 512 if e == "sh" else ((ei % 2) * ISL + isl) * 512
                            if ei >= 2 and isl == 0:
                                t.wait_ge(M["e_dnmm"].ctr.h, M["e_dnmm"].val(f"{m}_{ei - 2}"))
                            elif e == "sh" and isl == ISL and ei >= 1:
                                t.wait_ge(M["e_dnmm"].ctr.h, M["e_dnmm"].val(f"{m}_{ei - 1}"))
                            kg = f"Egu{m}_{ei}_{isl}g"
                            bank = main.prod(t, kg)
                            for k in range(NCH):
                                mm = t.matmul(bank[:], eg_[:, k * 128:(k + 1) * 128],
                                              tfv[:, k, :], start=(k == 0), stop=(k == NCH - 1))
                            main.prod_inc(mm, kg)
                            ku = f"Egu{m}_{ei}_{isl}u"
                            bank = alt.prod(t, ku)
                            for k in range(NCH):
                                mm = t.matmul(bank[:], eu_[:, k * 128:(k + 1) * 128],
                                              tfv[:, k, :], start=(k == 0), stop=(k == NCH - 1))
                            alt.prod_inc(mm, ku)
                            t.wait_ge(alt.fill.ctr.h, alt.fill.val(ku))
                            egst.release_on(t.nop()); egst.ncons += 1
                            eust.release_on(t.nop()); eust.ncons += 1
                        for dg in range(4):
                            nsets = 1 if e != "sh" else 2
                            dsets = []
                            for half in range(nsets):
                                didx = edb0 + sum((4 if x != "sh" else 8) for x in EXPERTS[:ei]) + dg * nsets + half
                                edst.wait(t, didx)
                                dsets.append(edst.buf(didx))
                            for tt in range(4):
                                key = f"Edn{m}_{ei}_{dg}_{tt}"
                                bank = misc.prod(t, key)
                                for isl in range(nisl):
                                    hcol = (isl * 512 if e == "sh"
                                            else ((ei % 2) * ISL + isl) * 512)
                                    t.wait_ge(M["e_hh"].ctr.h,
                                              M["e_hh"].val(f"{m}_{ei}_{isl}"))
                                    ds = dsets[isl // ISL] if e == "sh" else dsets[0]
                                    mm = t.matmul(bank[:],
                                                  hh_t[:, hcol + tt * 128: hcol + tt * 128 + 128],
                                                  ds[:, (isl % ISL) * 512:(isl % ISL + 1) * 512],
                                                  start=(isl == 0), stop=(isl == nisl - 1))
                                misc.prod_inc(mm, key)
                            t.wait_ge(misc.fill.ctr.h, misc.fill.val(key))
                            for half in range(nsets):
                                edst.release_on(t.nop()); edst.ncons += 1
                        M["e_dnmm"].inc(t.nop(), f"{m}_{ei}")

                @block.scalar
                def _(s):
                    for tt in range(4):
                        s.wait_ge(M["e_acc"].ctr.h, M["e_acc"].val(f"E{m}_{tt}"))
                        a = s.activation(sqE[:], accs[tt][:], AF.Square)
                        M["la_sq"].inc(a, f"E{m}_{tt}")
                        s.wait_ge(M["la_stat"].ctr.h, M["la_stat"].val(f"E{m}_{tt}"))
                        a = s.activation(stat[:, 6:7], stat[:, 5:6], AF.Sqrt, bias=eps_s[:, 1:2])
                        M["la_sd"].inc(a, f"E{m}_{tt}")
                        s.wait_ge(M["e_glog"].ctr.h, M["e_glog"].val(f"{m}_{tt}"))
                        a = s.activation(el_s[:], logits_s[:], AF.Exp, bias=stat[:, 8:9])
                        M["e_gexp"].inc(a, f"{m}_{tt}")
                    for ei, e in enumerate(EXPERTS):
                        nisl = ISL if e != "sh" else SISL
                        for isl in range(nisl):
                            kg = f"Egu{m}_{ei}_{isl}g"
                            bank = main.cons(s, kg)
                            if (ei * 100 + isl) >= 2:
                                pass
                            pos = GUPOS[(m, ei, isl)]
                            if pos >= 2:
                                pm, pei, pisl = GUSEQ[pos - 2]
                                s.wait_ge(M["e_hh"].ctr.h,
                                          M["e_hh"].val(f"{pm}_{pei}_{pisl}"))
                            sl = s.activation(sg_t[pos % 2][:], bank[:], AF.Silu)
                            M["e_sg"].inc(sl, f"{m}_{ei}_{isl}")
                            s.wait_ge(M["e_sg"].ctr.h, M["e_sg"].val(f"{m}_{ei}_{isl}"))
                            main.cons_inc(s.nop(), kg)

                @block.vector
                def _(v):
                    v.wait_ge(s_cconst.h, 48)
                    for tt in range(4):
                        gt = m * 4 + tt
                        xin.wait(v, xb0 + tt)
                        xtile = xin.buf(xb0 + tt)
                        if m > 0:
                            v.wait_ge(st_out.h, 16 * ((m - 1) * 4 + tt + 1))
                        keys = [f"Eop{m}_{tt}_{j}" for j in range(4)]
                        for j in range(4):
                            bank = (main if j < 2 else alt).cons(v, keys[j])
                            ev = v.scalar_tensor_tensor(accs[tt][:, j * 512:(j + 1) * 512],
                                                        bank[:], 1.0,
                                                        xtile[:, j * 512:(j + 1) * 512],
                                                        ALU.mult, ALU.add)
                            if j == 3:
                                M["e_acc"].inc(ev, f"E{m}_{tt}")
                                v.wait_ge(M["e_acc"].ctr.h, M["e_acc"].val(f"E{m}_{tt}"))
                                (main if j < 2 else alt).cons_inc(v.nop(), keys[j])
                                xin.release_on(v.nop())
                                xin.ncons += 1
                            else:
                                (main if j < 2 else alt).cons_inc(ev, keys[j])
                        v.wait_ge(M["la_sq"].ctr.h, M["la_sq"].val(f"E{m}_{tt}"))
                        r2 = v.reduce_sum(stat[:, 12:13], sqE[:], axis=AX.X)
                        n = selfinc(v_self, v_n, r2)
                        v.wait_ge(v_self.h, n)
                        t2 = v.tensor_scalar(stat[:, 5:6], stat[:, 12:13], 1.0 / C, None, ALU.mult)
                        M["la_stat"].inc(t2, f"E{m}_{tt}")
                        v.wait_ge(M["la_sd"].ctr.h, M["la_sd"].val(f"E{m}_{tt}"))
                        rc = v.reciprocal(stat[:, 7:8], stat[:, 6:7])
                        n = selfinc(v_self, v_n, rc)
                        v.wait_ge(v_self.h, n)
                        h2f = v.tensor_scalar(sqE[:], accs[tt][:], stat[:, 7:8], None, ALU.mult)
                        n = selfinc(v_self, v_n, h2f)
                        v.wait_ge(v_self.h, n)
                        thb = v.tensor_mul(thatE[:], sqE[:], ln2_bf[:])
                        M["e_that"].inc(thb, f"{m}_{tt}")
                        for half in range(2):
                            key = f"Efm{m}_{tt}_{half}"
                            bank = tpp.cons(v, key)
                            cp = v.tensor_copy(
                                tfv[:, half * 8:(half + 1) * 8, tt * 128:(tt + 1) * 128],
                                bank[:].rearrange("p (j n) -> p j n", j=8))
                            tpp.cons_inc(cp, key)
                        dc, dv_ = tpp.done_val(key)
                        v.wait_ge(dc.h, dv_)
                        M["e_tfm"].inc(v.nop(), f"{m}_{tt}")
                        key = f"Eg{m}_{tt}"
                        bank = misc.cons(v, key)
                        cp = v.tensor_copy(logits_s[:], bank[:, 0:16])
                        n = selfinc(v_self, v_n, cp)
                        v.wait_ge(v_self.h, n)
                        misc.cons_inc(v.nop(), key)
                        rm = v.reduce_max(stat[:, 9:10], logits_s[:], axis=AX.X)
                        n = selfinc(v_self, v_n, rm)
                        v.wait_ge(v_self.h, n)
                        ng = v.tensor_scalar(stat[:, 8:9], stat[:, 9:10], -1.0, None, ALU.mult)
                        M["e_glog"].inc(ng, f"{m}_{tt}")
                        v.wait_ge(M["e_gexp"].ctr.h, M["e_gexp"].val(f"{m}_{tt}"))
                        mx = v.max(top8_s[:], el_s[:])
                        n = selfinc(v_self, v_n, mx)
                        v.wait_ge(v_self.h, n)
                        rs = v.reduce_sum(stat[:, 10:11], top8_s[:, 0:KTOP], axis=AX.X)
                        n = selfinc(v_self, v_n, rs)
                        v.wait_ge(v_self.h, n)
                        rc = v.reciprocal(stat[:, 11:12], stat[:, 10:11])
                        ge = v.tensor_scalar(cw_s[:], el_s[:], top8_s[:, KTOP - 1:KTOP],
                                             None, ALU.is_ge)
                        n = selfinc(v_self, v_n, ge)
                        v.wait_ge(v_self.h, n)
                        cwf = v.scalar_tensor_tensor(cw_s[:], cw_s[:], stat[:, 11:12],
                                                     el_s[:], ALU.mult, ALU.mult)
                        M["e_gcw"].inc(cwf, f"{m}_{tt}")
                        key = f"EgT{m}_{tt}"
                        bank = misc.cons(v, key)
                        if m > 0 or tt > 0:
                            v.wait_ge(st_cw.h, 16 * (m * 4 + tt))
                        cp = v.tensor_copy(cwT_s[:], bank[0:16, 0:128])
                        M["e_gT"].inc(cp, f"{m}_{tt}")
                        v.wait_ge(M["e_gT"].ctr.h, M["e_gT"].val(f"{m}_{tt}"))
                        misc.cons_inc(v.nop(), key)
                    for ei, e in enumerate(EXPERTS):
                        nisl = ISL if e != "sh" else SISL
                        if e != "sh":
                            key = f"Ecw{m}_{e}"
                            bank = misc.cons(v, key)
                            cp = v.tensor_copy(cwb[:], bank[:])
                            M["e_cwb"].inc(cp, f"{m}_{e}")
                            v.wait_ge(M["e_cwb"].ctr.h, M["e_cwb"].val(f"{m}_{e}"))
                            misc.cons_inc(v.nop(), key)
                        for isl in range(nisl):
                            ku = f"Egu{m}_{ei}_{isl}u"
                            bank = alt.cons(v, ku)
                            v.wait_ge(M["e_sg"].ctr.h, M["e_sg"].val(f"{m}_{ei}_{isl}"))
                            pos = GUPOS[(m, ei, isl)]
                            hhx = v.scalar_tensor_tensor(hh0[pos % 2][:], bank[:], 1.0,
                                                         sg_t[pos % 2][:], ALU.mult, ALU.mult)
                            n = selfinc(v_self, v_n, hhx)
                            v.wait_ge(v_self.h, n)
                            alt.cons_inc(v.nop(), ku)
                            hcol = (isl * 512 if e == "sh"
                                    else ((ei % 2) * ISL + isl) * 512)
                            if e != "sh":
                                v.wait_ge(M["e_cwb"].ctr.h, M["e_cwb"].val(f"{m}_{e}"))
                                hw = v.tensor_mul(hh_t[:, hcol:hcol + 512], hh0[pos % 2][:],
                                                  cwb[:])
                            else:
                                hw = v.tensor_copy(hh_t[:, hcol:hcol + 512], hh0[pos % 2][:])
                            M["e_hh"].inc(hw, f"{m}_{ei}_{isl}")
                        for dg in range(4):
                            for tt in range(4):
                                key = f"Edn{m}_{ei}_{dg}_{tt}"
                                bank = misc.cons(v, key)
                                if ei == 0:
                                    v.wait_ge(M["e_that"].ctr.h, M["e_that"].val(f"{m}_{tt}"))
                                dn = v.tensor_add(accs[tt][:, dg * 512:(dg + 1) * 512],
                                                  accs[tt][:, dg * 512:(dg + 1) * 512],
                                                  bank[:])
                                M["e_dn"].inc(dn, f"{m}_{ei}_{dg}_{tt}")
                                v.wait_ge(M["e_dn"].ctr.h, M["e_dn"].val(f"{m}_{ei}_{dg}_{tt}"))
                                misc.cons_inc(v.nop(), key)
                    for tt in range(4):
                        v.wait_ge(M["e_dn"].ctr.h, M["e_dn"].val(f"{m}_{len(EXPERTS) - 1}_3_3"))
                        xin.wait(v, xb0 + 4 + tt)
                        ad = v.tensor_add(accs[tt][:], accs[tt][:], xin.buf(xb0 + 4 + tt)[:])
                        M["e_fin"].inc(ad, f"{m}_{tt}")
                        xin.ncons += 1
                        v.wait_ge(M["e_fin"].ctr.h, M["e_fin"].val(f"{m}_{tt}"))
                        xin.release_on(v.nop())

        @block.sync
        def _(sy):
            if upto_i == 0:
                stw_all(sy, 48)
            elif upto_i == 1:
                sy.wait_ge(st_attv.h, 16 * 16)
            elif upto_i == 2:
                stw_all(sy, 96)
                sy.wait_ge(st_x1.h, 16 * 16)
                sy.wait_ge(st_xf.h, 16 * 16)
            elif upto_i == 3:
                sy.wait_ge(st_attv.h, 16 * 32)
            else:
                sy.wait_ge(st_out.h, 16 * 16)
            if dbg:
                sy.wait_ge(st_dbg.h, 16 * len(dbg))
            if upto_i < 4:
                sy.dma_start(out=out_hbm[0:128, :], in_=x_in[0:128, :]).then_inc(st_out.h, 16)
                sy.wait_ge(st_out.h, 16)

        if upto_i == 0:
            pass
        elif upto_i == 1:
            stB.close()
        elif upto_i == 2:
            stC.close()
        elif upto_i == 3:
            stD.close()
        else:
            stE.close()

    return nc, dbg


def make_inmaps(ins):
    blobs = host_blobs(ins)
    cores, shared = host_percore(ins)
    in_maps = []
    for c in range(8):
        m = dict(
            x=cores[c]["x"],
            ropeA=cores[c]["ropeA"], ropeB=cores[c]["ropeB"],
            cmask=cores[c]["cmask"],
            bqkv_cols=shared["bqkv_cols"], bout_row=shared["bout_row"],
            ln1_tile=shared["ln1_tile"], ln2_tile=shared["ln2_tile"])
        for gk in ("A1", "A2", "G", "U", "D"):
            sh = blobs[gk].size // 8
            m[f"wshard{gk}"] = np.ascontiguousarray(blobs[gk][c * sh:(c + 1) * sh])
        in_maps.append(m)
    return in_maps, cores


_CACHE = {}

def kernel(**inputs):
    if "nc" not in _CACHE:
        _CACHE["nc"] = build(upto="E")[0]
    nc = _CACHE["nc"]
    in_maps, cores = make_inmaps(inputs)
    res = run_bass_kernel_spmd(nc, in_maps, core_ids=list(range(8)))
    full = np.zeros((B, T, C), np.float32)
    for core in range(8):
        full[core // 2][cores[core]["rows"]] = res.results[core]["y"]
    return full


# version 09
# speedup vs baseline: 1.0636x; 1.0103x over previous
import math
import contextlib
import numpy as np
import ml_dtypes
import concourse.bass as bass
import concourse.mybir as mybir
from concourse.bass_utils import run_bass_kernel_spmd

F32 = mybir.dt.float32
BF16 = mybir.dt.bfloat16
AF = mybir.ActivationFunctionType
ALU = mybir.AluOpType
AX = mybir.AxisListType

B, T, C = 4, 4096, 2048
H, HD = 16, 128
NT = 2048
NQT = 16
E_, KTOP, I_ = 16, 6, 1408
ISL = I_ // 128
SI = 2816
SISL = SI // 128
NCH = C // 128
SCALE = 1.0 / math.sqrt(HD)
MASKVAL = -1e9
MTOK = 512
NMACRO = NT // MTOK
BLOB_ELEMS = 189300736


def _perm_half():
    return np.concatenate([np.arange(0, HD, 2), np.arange(1, HD, 2)])

def _slabify(w):
    ci, no = w.shape
    k, o = ci // 128, no // 128
    return np.ascontiguousarray(
        w.reshape(k, 128, o, 128).transpose(2, 1, 0, 3).reshape(o, 128, k * 128))

def host_blobs(ins):
    pe = _perm_half()
    W1 = np.ascontiguousarray(ins["L0_attn_w_qkv"].T).reshape(C, 3, H, HD)
    W1q = W1.copy()
    W1q[:, 0] = W1[:, 0][:, :, pe]
    W1q[:, 1] = W1[:, 1][:, :, pe]
    W1 = W1q.reshape(C, 3 * C)
    parts = []
    parts.append(("qkv", _slabify(W1)))
    parts.append(("woutT", np.ascontiguousarray(ins["L0_attn_w_out"].T)))
    parts.append(("wq", _slabify(ins["L0_ffn_wq"])))
    parts.append(("wk", _slabify(ins["L0_ffn_wk"])))
    parts.append(("wv", _slabify(ins["L0_ffn_wv"])))
    parts.append(("wo", np.ascontiguousarray(ins["L0_ffn_wo"])))
    gt = np.ascontiguousarray(ins["L0_ffn_gate_w"].T)
    parts.append(("gate", np.ascontiguousarray(
        gt.reshape(NCH, 128, 16).transpose(1, 0, 2).reshape(128, NCH * 16))))
    parts.append(("eg", np.ascontiguousarray(
        ins["L0_ffn_eg_w"].reshape(E_, ISL, 128, NCH, 128)
        .transpose(0, 1, 4, 3, 2).reshape(E_, ISL, 128, NCH * 128))))
    parts.append(("eu", np.ascontiguousarray(
        ins["L0_ffn_eu_w"].reshape(E_, ISL, 128, NCH, 128)
        .transpose(0, 1, 4, 3, 2).reshape(E_, ISL, 128, NCH * 128))))
    edT = np.ascontiguousarray(ins["L0_ffn_ed_w"].transpose(0, 2, 1))
    parts.append(("ed4", np.ascontiguousarray(
        edT.reshape(E_, ISL, 128, 4, 512).transpose(0, 2, 3, 1, 4)
        .reshape(E_, 128, 4 * ISL * 512))))
    parts.append(("sg", _slabify(np.ascontiguousarray(ins["L0_ffn_sg_w"].T))))
    parts.append(("su", _slabify(np.ascontiguousarray(ins["L0_ffn_su_w"].T))))
    sdT = np.ascontiguousarray(ins["L0_ffn_sd_w"].T)
    parts.append(("sd4", np.ascontiguousarray(
        sdT.reshape(SISL, 128, 4, 512).transpose(1, 2, 0, 3)
        .reshape(128, 4 * SISL * 512))))
    d = dict(parts)
    groups = {}
    groups["A1"] = d["qkv"].astype(np.float32, copy=False).reshape(-1)
    ga = np.concatenate([d[n].astype(np.float32, copy=False).reshape(-1)
                         for n in ["woutT", "wq", "wk", "wv", "wo",
                                   "gate", "sg", "su", "sd4"]])
    pad = (-ga.size) % (8 * 2048)
    if pad:
        ga = np.concatenate([ga, np.zeros(pad, np.float32)])
    groups["A2"] = ga
    groups["G"] = d["eg"].astype(np.float32, copy=False).reshape(-1)
    groups["U"] = d["eu"].astype(np.float32, copy=False).reshape(-1)
    groups["D"] = d["ed4"].astype(np.float32, copy=False).reshape(-1)
    for k in ("A1", "G", "U", "D"):
        assert groups[k].size % (8 * 2048) == 0
    return groups

def blob_offsets():
    offs = {}
    offs["qkv"] = ("A1", 0, (48, 128, NCH * 128))
    cur = 0
    for name, shape in [
            ("woutT", (C, C)),
            ("wq", (16, 128, NCH * 128)), ("wk", (16, 128, NCH * 128)),
            ("wv", (16, 128, NCH * 128)), ("wo", (C, C)),
            ("gate", (128, NCH * 16)),
            ("sg", (SISL, 128, NCH * 128)), ("su", (SISL, 128, NCH * 128)),
            ("sd4", (128, 4 * SISL * 512))]:
        offs[name] = ("A2", cur, shape)
        cur += int(np.prod(shape))
    ga_elems = cur + ((-cur) % (8 * 2048))
    offs["eg"] = ("G", 0, (E_, ISL, 128, NCH * 128))
    offs["eu"] = ("U", 0, (E_, ISL, 128, NCH * 128))
    offs["ed4"] = ("D", 0, (E_, 128, 4 * ISL * 512))
    return offs, ga_elems

def host_percore(ins):
    x = ins["L0_attn_x"].astype(np.float32, copy=False)
    pe = _perm_half()
    half = HD // 2
    thetas = 10000.0 ** (-np.arange(half, dtype=np.float64) / half)
    ang = np.arange(T, dtype=np.float64)[:, None] * thetas[None, :]
    sinT = np.sin(ang).astype(np.float32)
    cosT = np.cos(ang).astype(np.float32)

    bq = ins["L0_attn_b_qkv"].astype(np.float32).reshape(3, H, HD).copy()
    bq[0] = bq[0][:, pe]
    bq[1] = bq[1][:, pe]
    bqkv_cols = np.ascontiguousarray(bq.reshape(48, 128).T)
    bout_row = ins["L0_attn_b_out"].astype(np.float32).reshape(1, C)
    ln1_tile = np.ascontiguousarray(np.broadcast_to(ins["L0_ffn_ln1_w"].astype(np.float32), (128, C)))
    ln2_tile = np.ascontiguousarray(np.broadcast_to(ins["L0_ffn_ln2_w"].astype(np.float32), (128, C)))

    cores = []
    for core in range(8):
        b, p = core // 2, core % 2
        rows = np.concatenate([np.arange(256 * i + 128 * p, 256 * i + 128 * p + 128)
                               for i in range(NQT)])
        cos_l = cosT[rows].T
        sin_l = sinT[rows].T
        cm = np.zeros((NQT, 128, 512), np.float32)
        for i in range(NQT):
            base = 512 * (i // 2)
            tg = 256 * i + 128 * p + np.arange(128)
            kt = base + np.arange(512)
            cm[i] = np.where(kt[None, :] <= tg[:, None], 0.0, MASKVAL)
        cores.append(dict(
            x=np.ascontiguousarray(x[b, rows]),
            ropeA=np.ascontiguousarray(np.concatenate([cos_l, sin_l], axis=0)),
            ropeB=np.ascontiguousarray(np.concatenate([sin_l, cos_l], axis=0)),
            cmask=np.ascontiguousarray(cm), rows=rows))
    shared = dict(bqkv_cols=bqkv_cols, bout_row=bout_row,
                  ln1_tile=ln1_tile, ln2_tile=ln2_tile)
    return cores, shared


class Ctr:
    def __init__(self, stack, nc, name):
        self.h = stack.enter_context(nc.semaphore(name))
        self.name = name

class Tk:
    def __init__(self, ctr, per=1):
        self.ctr = ctr
        self.map = {}
        self.next = 0
        self.per = per
    def idx(self, key):
        if key not in self.map:
            self.map[key] = self.next
            self.next += 1
        return self.map[key]
    def val(self, key):
        return (self.idx(key) + 1) * self.per
    def inc(self, instr, key):
        i = self.idx(key)
        instr.then_inc(self.ctr.h, self.per)
        return (i + 1) * self.per

class Pipe:
    def __init__(self, banks, fill: Tk, dones):
        self.banks, self.fill, self.dones = banks, fill, dones
        self.done = fill
    def prod(self, eng, key):
        i = self.fill.idx(key)
        n = len(self.banks)
        if i >= n:
            eng.wait_ge(self.dones[i % n].h, i // n)
        return self.banks[i % n]
    def prod_inc(self, instr, key):
        self.fill.inc(instr, key)
    def cons(self, eng, key):
        i = self.fill.idx(key)
        eng.wait_ge(self.fill.ctr.h, i + 1)
        return self.banks[i % len(self.banks)]
    def cons_inc(self, instr, key):
        i = self.fill.idx(key)
        instr.then_inc(self.dones[i % len(self.banks)].h, 1)
    def done_val(self, key):
        i = self.fill.idx(key)
        return (self.dones[i % len(self.banks)], i // len(self.banks) + 1)

class Stream:
    def __init__(self, bufs, lds, fr, ndma=1):
        self.bufs, self.lds, self.fr, self.ndma = bufs, lds, fr, ndma
        self.nfill = 0
        self.ncons = 0
        self.nfr = 0
    def load(self, eng, fill):
        s = self.nfill % len(self.bufs)
        if self.nfill >= len(self.bufs):
            eng.wait_ge(self.fr.h, self.nfill - len(self.bufs) + 1)
        fill(eng, self.bufs[s], self.lds[s])
        self.nfill += 1
    def buf(self, i):
        return self.bufs[i % len(self.bufs)]
    def wait(self, eng, i):
        n = len(self.bufs)
        eng.wait_ge(self.lds[i % n].h, 16 * self.ndma * (i // n + 1))
    def wait_all(self, eng, nfills):
        n = len(self.bufs)
        for s in range(n):
            cnt = len([i for i in range(nfills) if i % n == s])
            if cnt:
                eng.wait_ge(self.lds[s].h, 16 * self.ndma * cnt)
    def release_on(self, instr):
        instr.then_inc(self.fr.h, 1)
        self.nfr += 1


def build(upto="E", debug=()):
    nc = bass.Bass("TRN2", target_bir_lowering=False)
    upto_i = "ABCDE".index(upto)

    OFFS, GA_ELEMS = blob_offsets()
    GKEYS = ["A1", "A2", "G", "U", "D"]
    GSIZES = {"A1": 48 * 128 * NCH * 128, "A2": GA_ELEMS,
              "G": 46137344, "U": 46137344, "D": 46137344}
    SHARDS = {k: v // 8 for k, v in GSIZES.items()}

    x_in = nc.dram_tensor("x", [NT, C], F32, kind="ExternalInput").ap()
    wshard = {k: nc.dram_tensor(f"wshard{k}", [SHARDS[k]], F32, kind="ExternalInput").ap()
              for k in GKEYS}
    ropeA_in = nc.dram_tensor("ropeA", [128, NT], F32, kind="ExternalInput").ap()
    ropeB_in = nc.dram_tensor("ropeB", [128, NT], F32, kind="ExternalInput").ap()
    cm_in = nc.dram_tensor("cmask", [NQT, 128, 512], F32, kind="ExternalInput").ap()
    bqkv_in = nc.dram_tensor("bqkv_cols", [128, 48], F32, kind="ExternalInput").ap()
    bout_in = nc.dram_tensor("bout_row", [1, C], F32, kind="ExternalInput").ap()
    ln1_in = nc.dram_tensor("ln1_tile", [128, C], F32, kind="ExternalInput").ap()
    ln2_in = nc.dram_tensor("ln2_tile", [128, C], F32, kind="ExternalInput").ap()

    shard_bf = {k: nc.dram_tensor(f"shard_bf{k}", [SHARDS[k]], BF16, kind="Internal").ap()
                for k in GKEYS}
    blob_t = {k: nc.dram_tensor(f"blob{k}", [GSIZES[k] // 2048, 2048], BF16, kind="Internal")
              for k in GKEYS}
    blob = {k: blob_t[k].ap().rearrange("a b -> (a b)") for k in GKEYS}
    agin = nc.dram_tensor("agin", [2 * C, NT], BF16, kind="Internal").ap()
    agoutK = nc.dram_tensor("agoutK", [2 * C, NT], BF16, kind="Internal").ap()
    agoutV = nc.dram_tensor("agoutV", [2 * C, NT], BF16, kind="Internal").ap()
    qbuf = nc.dram_tensor("qbuf", [C, NT], BF16, kind="Internal").ap()
    attv = nc.dram_tensor("attv", [C, NT], BF16, kind="Internal").ap()
    x1_hbm = nc.dram_tensor("x1_hbm", [NT, C], F32, kind="Internal").ap()
    xf_hbm = nc.dram_tensor("xf_hbm", [NT, C], F32, kind="Internal").ap()
    cw_dram = nc.dram_tensor("cw_dram", [E_, MTOK], F32, kind="Internal").ap()
    out_hbm = nc.dram_tensor("y", [NT, C], F32, kind="ExternalOutput").ap()

    dbg = {}
    for name, shape, dt in [("agin", [2 * C, NT], BF16), ("q", [C, NT], BF16),
                            ("attv", [C, NT], BF16), ("x1", [NT, C], F32),
                            ("xf", [NT, C], F32), ("agout", [4 * C, NT], BF16),
                            ("cw", [E_, MTOK], F32)]:
        if name in debug:
            dbg[name] = nc.dram_tensor("dbg_" + name, shape, dt, kind="ExternalOutput").ap()

    id_bf = nc.inline_tensor(np.eye(128, dtype=ml_dtypes.bfloat16), name="id_bf").ap()
    id_f = nc.inline_tensor(np.eye(128, dtype=np.float32), name="id_f").ap()
    ones_col = nc.inline_tensor(np.ones((128, 1), dtype=ml_dtypes.bfloat16), name="ones_col").ap()
    ones_row_f = nc.inline_tensor(np.ones((1, 128), dtype=np.float32), name="ones_row_f").ap()
    ones_row_bf = nc.inline_tensor(np.ones((1, 128), dtype=ml_dtypes.bfloat16), name="ones_row_bf").ap()
    eps_np = np.zeros((128, 2), np.float32)
    eps_np[:, 0] = 1e-5
    eps_np[:, 1] = 1e-6
    eps_in = nc.inline_tensor(eps_np, name="epsc").ap()

    st = contextlib.ExitStack()
    with st:
        SB = lambda n, s, d: st.enter_context(nc.sbuf_tensor(n, s, d))
        PS = lambda n, s, d: st.enter_context(nc.psum_tensor(n, s, d))
        id_bf_s = SB("id_bf_s", [128, 128], BF16)
        id_f_s = SB("id_f_s", [128, 128], F32)
        ones_col_s = SB("ones_col_s", [128, 1], BF16)
        ones_row_f_s = SB("ones_row_f_s", [1, 128], F32)
        ones_row_bf_s = SB("ones_row_bf_s", [1, 128], BF16)
        eps_s = SB("eps_s", [128, 2], F32)
        bqkv_s = SB("bqkv_s", [128, 48], F32)
        bout_bf = SB("bout_bf", [1, C], BF16)
        ln1_bf = SB("ln1_bf", [128, C], BF16)
        ln2_bf = SB("ln2_bf", [128, C], BF16)
        gate_sb = SB("gate_sb", [128, NCH * 16], BF16)
        stat = SB("stat", [128, 16], F32)
        stage = [SB(f"stage{i}", [128, C], BF16) for i in range(3)]

        psA = PS("psA", [128, 512], F32)
        psB = PS("psB", [128, 512], F32)
        psC = PS("psC", [128, 512], F32)
        psD = PS("psD", [128, 512], F32)
        psT1 = PS("psT1", [128, 1024], BF16)
        psT2 = PS("psT2", [128, 1024], BF16)
        psO = PS("psO", [128, 512], F32)
        psM = PS("psM", [128, 512], F32)

        Cm = lambda name: Ctr(st, nc, name)
        v_self = Cm("v_self"); v_n = [0]
        a_self = Cm("a_self"); a_n = [0]
        g_self = Cm("g_self"); g_n = [0]

        def selfinc(ctr, nref, instr):
            instr.then_inc(ctr.h, 1)
            nref[0] += 1
            return nref[0]

        s_castA1 = Cm("s_castA1")
        s_castR = Cm("s_castR")
        s_cast = {k: (s_castA1 if k == "A1" else s_castR) for k in GKEYS}
        def cast_done(g, gk):
            if gk == "A1":
                g.wait_ge(s_castA1.h, 16 * 8)
            else:
                g.wait_ge(s_castR.h, 16 * 32)
        s_ag = Cm("s_ag")
        s_gate = Cm("s_gate")
        s_consts = Cm("s_consts")
        s_cconst = Cm("s_cconst")
        st_stp = [Cm(f"st_stp{k}") for k in range(4)]
        def stw_all(eng, n):
            for k in range(4):
                c = (n - k + 3) // 4
                if c:
                    eng.wait_ge(st_stp[k].h, 16 * c)
        def stw_one(eng, s):
            eng.wait_ge(st_stp[s % 4].h, 16 * (s // 4 + 1))
        def stw_pre(eng, s):
            if s >= 4:
                eng.wait_ge(st_stp[s % 4].h, 16 * (s // 4))
        def stw_inc(d, s):
            d.then_inc(st_stp[s % 4].h, 16)
        def stw_pairs(n):
            return [(st_stp[k], 16 * ((n - k + 3) // 4)) for k in range(4)
                    if (n - k + 3) // 4 > 0]
        st_attv = Cm("st_attv")
        st_x1 = Cm("st_x1")
        st_xf = Cm("st_xf")
        st_out = Cm("st_out")
        st_cw = Cm("st_cw")
        st_dbg = Cm("st_dbg")

        main = Pipe([psA, psB], Tk(Cm("main_f")), [Cm("main_d0"), Cm("main_d1")])
        alt = Pipe([psC, psD], Tk(Cm("alt_f")), [Cm("alt_d0"), Cm("alt_d1")])
        misc = Pipe([psO, psM], Tk(Cm("misc_f")), [Cm("misc_d0"), Cm("misc_d1")])
        tpp = Pipe([psT1, psT2], Tk(Cm("tp_f")), [Cm("tp_d0"), Cm("tp_d1")])

        M = {}
        for nm in ["la_sq", "la_stat", "la_sd", "la_xhat", "la_fm", "la_fmT",
                   "at_sq", "at_sqrt", "at_knorm", "at_mrow", "at_mhat",
                   "at_smask", "at_exp", "at_linv", "at_prowsc", "at_pt",
                   "at_ptmm", "at_pvmm", "at_pvp", "at_normmm", "at_av",
                   "c_stage",
                   "e_acc", "e_that", "e_tfm", "e_glog", "e_gexp", "e_gcw",
                   "e_gT", "e_cwb", "e_sg", "e_hh", "e_dn", "e_dnmm", "e_fin"]:
            M[nm] = Tk(Cm(nm))

        def mkstream(name, nslots, ndma=1):
            return Stream([], [Cm(f"{name}_ld{i}") for i in range(nslots)],
                          Cm(name + "_fr"), ndma)

        xin = mkstream("xin", 2)
        slab = mkstream("slab", 3)
        Kst = mkstream("Kst", 2, ndma=2)
        Vst = mkstream("Vst", 2, ndma=2)
        Qst = mkstream("Qst", 2)
        avst = mkstream("avst", 2)
        wrow = mkstream("wrow", 3)
        egst = mkstream("egst", 2)
        eust = mkstream("eust", 2)
        edst = mkstream("edst", 2)
        cwrow = mkstream("cwrow", 2)

        block = st.enter_context(nc.Block())

        def bview(name, idx=()):
            grp, off, shape = OFFS[name]
            for k2, i2 in enumerate(idx):
                off += i2 * int(np.prod(shape[k2 + 1:]))
            sub = shape[len(idx):]
            assert len(sub) == 2
            r, c = sub
            return blob[grp][off: off + r * c].rearrange("(a b) -> a b", b=c)

        @block.gpsimd
        def _(g):
            for gk in GKEYS:
                chunk = SHARDS[gk] // 8
                for i in range(8):
                    g.dma_start(out=shard_bf[gk][i * chunk:(i + 1) * chunk],
                                in_=wshard[gk][i * chunk:(i + 1) * chunk]).then_inc(s_cast[gk].h, 16)
            for gi, gk in enumerate(["A1", "A2"]):
                cast_done(g, gk)
                g.collective_compute(
                    "AllGather", ALU.bypass,
                    replica_groups=[[0, 1, 2, 3, 4, 5, 6, 7]],
                    ins=[shard_bf[gk].rearrange("(a b) -> a b", b=2048)],
                    outs=[blob_t[gk].ap()]).then_inc(s_ag.h, 1)
            g.dma_start(out=bout_bf[:], in_=bout_in[:]).then_inc(s_cconst.h, 16)
            g.dma_start(out=ln1_bf[:], in_=ln1_in[:]).then_inc(s_cconst.h, 16)
            g.dma_start(out=ln2_bf[:], in_=ln2_in[:]).then_inc(s_cconst.h, 16)

        @block.sync
        def _(sy):
            for dst, src in [(id_bf_s, id_bf), (id_f_s, id_f), (ones_col_s, ones_col),
                             (ones_row_f_s, ones_row_f), (ones_row_bf_s, ones_row_bf),
                             (eps_s, eps_in), (bqkv_s, bqkv_in)]:
                sy.dma_start(out=dst[:], in_=src[:]).then_inc(s_consts.h, 16)
        CONSTS = 7 * 16

        stA = contextlib.ExitStack()
        SA = lambda n, s, d: stA.enter_context(nc.sbuf_tensor(n, s, d))
        fmbuf = SA("fmbufA", [128, NCH * NT], BF16)
        fmv = fmbuf[:].rearrange("p (c n) -> p c n", c=NCH)
        sq_s = [SA(f"sq_sA{i}", [128, C], F32) for i in range(2)]
        xhat_s = [SA(f"xhat_sA{i}", [128, C], BF16) for i in range(2)]
        ropeA_s = SA("ropeA_s", [128, NT], F32)
        ropeB_s = SA("ropeB_s", [128, NT], F32)
        rope_tmpA = SA("rope_tmpA", [128, 512], F32)
        rope_tmpB = SA("rope_tmpB", [128, 512], F32)
        xin.bufs = [SA(f"xinA{i}", [128, C], F32) for i in range(2)]
        slab.bufs = [SA(f"slabA{i}", [128, NCH * 128], BF16) for i in range(3)]

        A_ORDER0 = [*range(16, 48), *range(0, 16)]
        for tt in range(NQT):
            for nm in ["la_sq", "la_stat", "la_sd", "la_xhat", "la_fmT", "la_fm"]:
                M[nm].idx(f"A{tt}")
        for oi in range(48):
            M["c_stage"].idx(f"A{oi}")

        def a_store(sy, oi):
            o = A_ORDER0[oi]
            sel, head = divmod(o, 16)
            stg = stage[oi % 3]
            sy.wait_ge(M["c_stage"].ctr.h, M["c_stage"].val(f"A{oi}"))
            stw_pre(sy, oi)
            if sel == 0:
                d = sy.dma_start(out=qbuf[head * 128:(head + 1) * 128, :], in_=stg[:])
            elif sel == 1:
                d = sy.dma_start(out=agin[head * 128:(head + 1) * 128, :], in_=stg[:])
            else:
                d = sy.dma_start(out=agin[C + head * 128:C + (head + 1) * 128, :], in_=stg[:])
            stw_inc(d, oi)

        @block.sync
        def _(sy):
            sy.wait_ge(s_consts.h, CONSTS)
            sy.dma_start(out=ropeA_s[:], in_=ropeA_in[:]).then_inc(s_consts.h, 16)
            sy.dma_start(out=ropeB_s[:], in_=ropeB_in[:]).then_inc(s_consts.h, 16)
            for tt in range(NQT):
                def fill(eng, buf, ld, tt=tt):
                    eng.dma_start(out=buf[:], in_=x_in[tt * 128:(tt + 1) * 128, :]
                                  ).then_inc(ld.h, 16)
                xin.load(sy, fill)
            sy.wait_ge(s_ag.h, 1)
            for oi in range(48):
                o = A_ORDER0[oi]
                def fill(eng, buf, ld, o=o):
                    eng.dma_start(out=buf[:], in_=bview("qkv", (o,))).then_inc(ld.h, 16)
                slab.load(sy, fill)
                if oi >= 2:
                    a_store(sy, oi - 2)
            a_store(sy, 46)
            a_store(sy, 47)
        CONSTS_A = CONSTS + 32

        @block.scalar
        def _(s):
            s.wait_ge(s_consts.h, CONSTS)
            for tt in range(NQT):
                xin.wait(s, tt)
                if tt >= 2:
                    s.wait_ge(M["la_stat"].ctr.h, M["la_stat"].val(f"A{tt - 2}"))
                a = s.activation(sq_s[tt % 2][:], xin.buf(tt)[:], AF.Square)
                M["la_sq"].inc(a, f"A{tt}")
                s.wait_ge(M["la_stat"].ctr.h, M["la_stat"].val(f"A{tt}"))
                a = s.activation(stat[:, 6:7], stat[:, 5:6], AF.Sqrt, bias=eps_s[:, 0:1])
                M["la_sd"].inc(a, f"A{tt}")

        @block.vector
        def _(v):
            for tt in range(NQT):
                xin.wait(v, tt)
                v.wait_ge(M["la_sq"].ctr.h, M["la_sq"].val(f"A{tt}"))
                v.reduce_sum(stat[:, 0:1], xin.buf(tt)[:], axis=AX.X)
                r2 = v.reduce_sum(stat[:, 1:2], sq_s[tt % 2][:], axis=AX.X)
                n = selfinc(v_self, v_n, r2)
                v.wait_ge(v_self.h, n)
                v.tensor_scalar(stat[:, 2:3], stat[:, 0:1], 1.0 / C, None, ALU.mult)
                t2 = v.tensor_scalar(stat[:, 3:4], stat[:, 1:2], 1.0 / C, None, ALU.mult)
                n = selfinc(v_self, v_n, t2)
                v.wait_ge(v_self.h, n)
                t3 = v.tensor_mul(stat[:, 4:5], stat[:, 2:3], stat[:, 2:3])
                n = selfinc(v_self, v_n, t3)
                v.wait_ge(v_self.h, n)
                t5 = v.tensor_sub(stat[:, 5:6], stat[:, 3:4], stat[:, 4:5])
                M["la_stat"].inc(t5, f"A{tt}")
                v.wait_ge(M["la_sd"].ctr.h, M["la_sd"].val(f"A{tt}"))
                rc = v.reciprocal(stat[:, 7:8], stat[:, 6:7])
                n = selfinc(v_self, v_n, rc)
                v.wait_ge(v_self.h, n)
                if tt >= 2:
                    v.wait_ge(M["la_fmT"].ctr.h, M["la_fmT"].val(f"A{tt - 2}"))
                tsx = v.tensor_scalar(xhat_s[tt % 2][:], xin.buf(tt)[:],
                                      stat[:, 2:3], stat[:, 7:8], ALU.subtract, ALU.mult)
                M["la_xhat"].inc(tsx, f"A{tt}")
                v.wait_ge(M["la_xhat"].ctr.h, M["la_xhat"].val(f"A{tt}"))
                xin.release_on(v.nop())
                xin.ncons += 1
                for ett in ([tt - 1] if tt >= 1 else []) + ([tt] if tt == NQT - 1 else []):
                    for half in range(2):
                        key = f"Afm{ett}_{half}"
                        bank = tpp.cons(v, key)
                        cp = v.tensor_copy(
                            fmv[:, half * 8:(half + 1) * 8, ett * 128:(ett + 1) * 128],
                            bank[:].rearrange("p (j n) -> p j n", j=8))
                        tpp.cons_inc(cp, key)
                    dc, dv_ = tpp.done_val(key)
                    v.wait_ge(dc.h, dv_)
                    M["la_fm"].inc(v.nop(), f"A{ett}")

        @block.tensor
        def _(t):
            t.wait_ge(s_consts.h, CONSTS_A)
            for tt in range(NQT):
                t.wait_ge(M["la_xhat"].ctr.h, M["la_xhat"].val(f"A{tt}"))
                for half in range(2):
                    key = f"Afm{tt}_{half}"
                    bank = tpp.prod(t, key)
                    for j in range(8):
                        c = half * 8 + j
                        mm = t.transpose(bank[:, j * 128:(j + 1) * 128],
                                         xhat_s[tt % 2][:, c * 128:(c + 1) * 128], id_bf_s[:])
                    tpp.prod_inc(mm, key)
                t.wait_ge(tpp.fill.ctr.h, tpp.fill.val(key))
                M["la_fmT"].inc(t.nop(), f"A{tt}")

        A_ORDER = [*range(16, 48), *range(0, 16)]

        @block.tensor
        def _(t):
            t.wait_ge(M["la_fm"].ctr.h, M["la_fm"].val(f"A{NQT - 1}"))
            for oi in range(48):
                slab.wait(t, oi)
                sb = slab.buf(oi)
                for mt in range(4):
                    key = f"Aqkv{oi}_{mt}"
                    bank = main.prod(t, key)
                    for k in range(NCH):
                        mm = t.matmul(bank[:], sb[:, k * 128:(k + 1) * 128],
                                      fmv[:, k, mt * 512:(mt + 1) * 512],
                                      start=(k == 0), stop=(k == NCH - 1))
                    main.prod_inc(mm, key)
                t.wait_ge(main.fill.ctr.h, main.fill.val(key))
                slab.release_on(t.nop())
                slab.ncons += 1

        @block.vector
        def _(v):
            v.wait_ge(s_consts.h, CONSTS_A)
            for oi in range(48):
                o = A_ORDER[oi]
                sel = o // 16
                stg = stage[oi % 3]
                if oi >= 3:
                    stw_one(v, oi - 3)
                for mt in range(4):
                    key = f"Aqkv{oi}_{mt}"
                    bank = main.cons(v, key)
                    ms = stg[:, mt * 512:(mt + 1) * 512]
                    bcol = bqkv_s[:, o:o + 1]
                    if sel == 2:
                        ev = v.tensor_scalar(ms, bank[:], bcol, None, ALU.add)
                        n = selfinc(v_self, v_n, ev)
                        v.wait_ge(v_self.h, n)
                        main.cons_inc(v.nop(), key)
                    else:
                        TAm = ropeA_s[:, mt * 512:(mt + 1) * 512]
                        TBm = ropeB_s[:, mt * 512:(mt + 1) * 512]
                        p1 = v.scalar_tensor_tensor(rope_tmpA[0:64, :], bank[0:64, :],
                                                    bcol[0:64, :], TAm[0:64, :], ALU.add, ALU.mult)
                        selfinc(v_self, v_n, p1)
                        p2 = v.scalar_tensor_tensor(rope_tmpB[0:64, :], bank[64:128, :],
                                                    bcol[64:128, :], TAm[64:128, :], ALU.add, ALU.mult)
                        selfinc(v_self, v_n, p2)
                        p3 = v.scalar_tensor_tensor(rope_tmpA[64:128, :], bank[0:64, :],
                                                    bcol[0:64, :], TBm[0:64, :], ALU.add, ALU.mult)
                        selfinc(v_self, v_n, p3)
                        p4 = v.scalar_tensor_tensor(rope_tmpB[64:128, :], bank[64:128, :],
                                                    bcol[64:128, :], TBm[64:128, :], ALU.add, ALU.mult)
                        n = selfinc(v_self, v_n, p4)
                        v.wait_ge(v_self.h, n)
                        main.cons_inc(v.nop(), key)
                        r1 = v.tensor_sub(ms[0:64, :], rope_tmpA[0:64, :], rope_tmpB[0:64, :])
                        selfinc(v_self, v_n, r1)
                        r2 = v.tensor_add(ms[64:128, :], rope_tmpA[64:128, :], rope_tmpB[64:128, :])
                        n = selfinc(v_self, v_n, r2)
                        v.wait_ge(v_self.h, n)
                M["c_stage"].inc(v.nop(), f"A{oi}")

        A_KV_ST = 32
        A_ST_TOTAL = 48

        @block.gpsimd
        def _(g):
            stw_all(g, 16)
            g.collective_compute(
                "AllGather", ALU.bypass,
                replica_groups=[[0, 1], [2, 3], [4, 5], [6, 7]],
                ins=[agin[0:C, :]], outs=[agoutK[:]]).then_inc(s_ag.h, 1)
            stw_all(g, 32)
            g.collective_compute(
                "AllGather", ALU.bypass,
                replica_groups=[[0, 1], [2, 3], [4, 5], [6, 7]],
                ins=[agin[C:2 * C, :]], outs=[agoutV[:]]).then_inc(s_ag.h, 1)
            for gk in ["G", "U", "D"]:
                cast_done(g, gk)
                g.collective_compute(
                    "AllGather", ALU.bypass,
                    replica_groups=[[0, 1, 2, 3, 4, 5, 6, 7]],
                    ins=[shard_bf[gk].rearrange("(a b) -> a b", b=2048)],
                    outs=[blob_t[gk].ap()]).then_inc(s_ag.h, 1)

        if upto_i <= 1 and ("agin" in dbg or "q" in dbg):
            @block.sync
            def _(sy):
                stw_all(sy, A_ST_TOTAL)
                if "agin" in dbg:
                    sy.dma_start(out=dbg["agin"][:], in_=agin[:]).then_inc(st_dbg.h, 16)
                if "q" in dbg:
                    sy.dma_start(out=dbg["q"][:], in_=qbuf[:]).then_inc(st_dbg.h, 16)
        if "agout" in dbg:
            @block.sync
            def _(sy):
                sy.wait_ge(s_ag.h, 4)
                sy.dma_start(out=dbg["agout"][0:2 * C, :], in_=agoutK[:])
                sy.dma_start(out=dbg["agout"][2 * C:4 * C, :], in_=agoutV[:]).then_inc(st_dbg.h, 16)

        if upto_i == 0:
            stA.close()

        def emit_attention(tag, ag_val, ag_val_v, qst_store_base, av_dst, extra_waits,
                           prev_avld_fills):
            stX = contextlib.ExitStack()
            SX = lambda n, s, d: stX.enter_context(nc.sbuf_tensor(n, s, d))
            Kst.bufs = [SX(f"K{tag}{i}", [128, T], BF16) for i in range(2)]
            Vst.bufs = [SX(f"V{tag}{i}", [128, T], BF16) for i in range(2)]
            Qst.bufs = [SX(f"Q{tag}{i}", [128, NT], BF16) for i in range(2)]
            VT_s = SX(f"VT{tag}", [128, T], BF16)
            Ksq_s = SX(f"Ksq{tag}", [128, T], BF16)
            Qsq_s = SX(f"Qsq{tag}", [128, NT], BF16)
            kn2row = SX(f"kn2row{tag}", [1, T], F32)
            qn2row = SX(f"qn2row{tag}", [1, NT], F32)
            knmax = SX(f"knmax{tag}", [1, 1], F32)
            mrow = SX(f"mrow{tag}", [1, NT], F32)
            mhat_cols = SX(f"mhat{tag}", [128, NQT], F32)
            cm_s = SX(f"cm{tag}", [128, NQT * 512], F32)
            Prow = [SX(f"Prow{tag}{i}", [128, T], BF16) for i in range(3)]
            Smask = [SX(f"Smask{tag}{i}", [128, 512], F32) for i in range(3)]
            PT_s = SX(f"PT{tag}", [128, 2 * T], BF16)
            lparts = SX(f"lparts{tag}", [128, NQT * 8], F32)
            l_cols = SX(f"lcols{tag}", [128, NQT], F32)
            linv_cols = SX(f"linv{tag}", [128, NQT], F32)
            av_big = [SX(f"avbig{tag}{i}", [128, C], BF16) for i in range(2)]

            kb0, vb0, qb0 = Kst.nfill, Vst.nfill, Qst.nfill
            for h in range(H):
                for nm in ["at_sq", "at_sqrt", "at_knorm", "at_mrow", "at_mhat",
                           "at_normmm", "at_pvmm"]:
                    M[nm].idx(f"{tag}{h}")
                for i in range(NQT):
                    for nm in ["at_smask", "at_exp", "at_linv", "at_prowsc",
                               "at_pt", "at_ptmm"]:
                        M[nm].idx(f"{tag}{h}_{i}")
                for jp in range(8):
                    M["at_pvp"].idx(f"{tag}{h}_{jp}")
                    M["at_av"].idx(f"{tag}{h}_{jp}")
            g0 = g_n[0]
            atv0 = ATV_BASE[tag]

            def att_store(sy, h):
                sy.wait_ge(M["at_av"].ctr.h, M["at_av"].val(f"{tag}{h}_7"))
                sy.wait_ge(st_attv.h, 16 * (atv0 + h))
                sy.dma_start(out=av_dst[h * 128:(h + 1) * 128, :],
                             in_=av_big[h % 2][:]).then_inc(st_attv.h, 16)

            @block.sync
            def _(sy):
                sy.wait_ge(s_ag.h, ag_val)
                for ctr, val in extra_waits:
                    sy.wait_ge(ctr.h, val)
                if prev_avld_fills:
                    avst.wait_all(sy, prev_avld_fills)
                sy.dma_start(out=cm_s[:].rearrange("p (a c) -> p a c", a=NQT),
                             in_=cm_in.rearrange("a b c -> b a c")
                             ).then_inc(s_consts.h, 16)
                for h in range(H):
                    def fillK(eng, buf, ld, h=h):
                        for par in range(2):
                            eng.dma_start(
                                out=buf[:].rearrange("p (g two n) -> p g two n", two=2, n=128)[:, :, par, :],
                                in_=agoutK[par * C + h * 128: par * C + (h + 1) * 128, :]
                                .rearrange("p (g n) -> p g n", n=128)
                            ).then_inc(ld.h, 16)
                    Kst.load(sy, fillK)
                    if h == 0:
                        sy.wait_ge(s_ag.h, ag_val_v)
                    def fillV(eng, buf, ld, h=h):
                        for par in range(2):
                            eng.dma_start(
                                out=buf[:].rearrange("p (g two n) -> p g two n", two=2, n=128)[:, :, par, :],
                                in_=agoutV[par * C + h * 128: par * C + (h + 1) * 128, :]
                                .rearrange("p (g n) -> p g n", n=128)
                            ).then_inc(ld.h, 16)
                    Vst.load(sy, fillV)
                    def fillQ(eng, buf, ld, h=h):
                        stw_one(eng, qst_store_base + h)
                        eng.dma_start(out=buf[:], in_=qbuf[h * 128:(h + 1) * 128, :]
                                      ).then_inc(ld.h, 16)
                    Qst.load(sy, fillQ)
                    if h >= 1:
                        att_store(sy, h - 1)

            @block.tensor
            def _(t):
                for h in range(H):
                    K_ = Kst.buf(kb0 + h)
                    V_ = Vst.buf(vb0 + h)
                    Q_ = Qst.buf(qb0 + h)
                    Vst.wait(t, vb0 + h)
                    if h > 0:
                        t.wait_ge(M["at_pvmm"].ctr.h, M["at_pvmm"].val(f"{tag}{h - 1}"))
                    for grp in range(4):
                        key = f"{tag}vt{h}_{grp}"
                        bank = tpp.prod(t, key)
                        for j in range(8):
                            jb = grp * 8 + j
                            mm = t.transpose(bank[:, j * 128:(j + 1) * 128],
                                             V_[:, jb * 128:(jb + 1) * 128], id_bf_s[:])
                        tpp.prod_inc(mm, key)
                    Kst.wait(t, kb0 + h)
                    Qst.wait(t, qb0 + h)
                    t.wait_ge(M["at_sq"].ctr.h, M["at_sq"].val(f"{tag}{h}"))
                    for kb in range(8):
                        key = f"{tag}kn{h}_{kb}"
                        bank = misc.prod(t, key)
                        mm = t.matmul(bank[0:1, :], ones_col_s[:],
                                      Ksq_s[:, kb * 512:(kb + 1) * 512], start=True, stop=True)
                        misc.prod_inc(mm, key)
                    for qb in range(4):
                        key = f"{tag}qn{h}_{qb}"
                        bank = misc.prod(t, key)
                        mm = t.matmul(bank[0:1, :], ones_col_s[:],
                                      Qsq_s[:, qb * 512:(qb + 1) * 512], start=True, stop=True)
                        misc.prod_inc(mm, key)
                    t.wait_ge(misc.fill.ctr.h, misc.fill.val(f"{tag}qn{h}_3"))
                    M["at_normmm"].inc(t.nop(), f"{tag}{h}")
                    t.wait_ge(M["at_mrow"].ctr.h, M["at_mrow"].val(f"{tag}{h}"))
                    key = f"{tag}mh{h}"
                    bank = misc.prod(t, key)
                    for i in range(NQT):
                        mm = t.transpose(bank[:, i:i + 1], mrow[:, i * 128:(i + 1) * 128],
                                         id_f_s[0:1, 0:1])
                    misc.prod_inc(mm, key)
                    t.wait_ge(M["at_mhat"].ctr.h, M["at_mhat"].val(f"{tag}{h}"))
                    for i in range(NQT + 2):
                        if i < NQT:
                            Pi = 512 * (i // 2 + 1)
                            for kb in range(Pi // 512):
                                key = f"{tag}S{h}_{i}_{kb}"
                                bank = main.prod(t, key)
                                mm = t.matmul(bank[:], Q_[:, i * 128:(i + 1) * 128],
                                              K_[:, kb * 512:(kb + 1) * 512],
                                              start=True, stop=True)
                                main.prod_inc(mm, key)
                        if i >= 2:
                            ip = i - 2
                            Pp = 512 * (ip // 2 + 1)
                            t.wait_ge(M["at_prowsc"].ctr.h,
                                      M["at_prowsc"].val(f"{tag}{h}_{ip}"))
                            nb128 = Pp // 128
                            pr = Prow[ip % 3]
                            for grp in range((nb128 + 7) // 8):
                                key = f"{tag}pt{h}_{ip}_{grp}"
                                bank = tpp.prod(t, key)
                                nj = min(8, nb128 - grp * 8)
                                for j in range(nj):
                                    jb = grp * 8 + j
                                    mm = t.transpose(bank[:, j * 128:(j + 1) * 128],
                                                     pr[:, jb * 128:(jb + 1) * 128], id_bf_s[:])
                                tpp.prod_inc(mm, key)
                            t.wait_ge(tpp.fill.ctr.h, tpp.fill.val(key))
                            M["at_ptmm"].inc(t.nop(), f"{tag}{h}_{ip}")
                            if ip % 2 == 1:
                                jp = ip // 2
                                t.wait_ge(M["at_pt"].ctr.h, M["at_pt"].val(f"{tag}{h}_{ip}"))
                                nkb = 4 * (jp + 1)
                                key = f"{tag}pv{h}_{jp}"
                                bank = alt.prod(t, key)
                                for jb in range(nkb):
                                    mm = t.matmul(bank[:, 0:256],
                                                  VT_s[:, jb * 128:(jb + 1) * 128],
                                                  PT_s[:, jb * 256:(jb + 1) * 256],
                                                  start=(jb == 0), stop=(jb == nkb - 1))
                                alt.prod_inc(mm, key)
                                t.wait_ge(alt.fill.ctr.h, alt.fill.val(key))
                                M["at_pvp"].inc(t.nop(), f"{tag}{h}_{jp}")
                    M["at_pvmm"].inc(t.nop(), f"{tag}{h}")
                    Kst.release_on(t.nop()); Kst.ncons += 1
                    Vst.release_on(t.nop()); Vst.ncons += 1
                    Qst.release_on(t.nop()); Qst.ncons += 1

            @block.scalar
            def _(s):
                s.wait_ge(s_consts.h, consts_after_att[tag])
                for h in range(H):
                    K_ = Kst.buf(kb0 + h)
                    Q_ = Qst.buf(qb0 + h)
                    Kst.wait(s, kb0 + h)
                    Qst.wait(s, qb0 + h)
                    if h > 0:
                        s.wait_ge(M["at_normmm"].ctr.h, M["at_normmm"].val(f"{tag}{h - 1}"))
                    s.activation(Ksq_s[:], K_[:], AF.Square)
                    qsq = s.activation(Qsq_s[:], Q_[:], AF.Square)
                    M["at_sq"].inc(qsq, f"{tag}{h}")
                    s.wait_ge(M["at_knorm"].ctr.h, M["at_knorm"].val(f"{tag}{h}"))
                    s.activation(knmax[:], knmax[:], AF.Sqrt)
                    sq2 = s.activation(qn2row[:], qn2row[:], AF.Sqrt)
                    M["at_sqrt"].inc(sq2, f"{tag}{h}")
                    s.wait_ge(M["at_mhat"].ctr.h, M["at_mhat"].val(f"{tag}{h}"))
                    for i in range(NQT):
                        Pi = 512 * (i // 2 + 1)
                        nb = Pi // 512
                        pr = Prow[i % 3]
                        if i >= 3:
                            s.wait_ge(M["at_ptmm"].ctr.h, M["at_ptmm"].val(f"{tag}{h}_{i - 3}"))
                        elif h > 0:
                            s.wait_ge(M["at_ptmm"].ctr.h,
                                      M["at_ptmm"].val(f"{tag}{h - 1}_{NQT - 3 + i}"))
                        for kb in range(nb):
                            key = f"{tag}S{h}_{i}_{kb}"
                            if kb == nb - 1:
                                s.wait_ge(M["at_smask"].ctr.h, M["at_smask"].val(f"{tag}{h}_{i}"))
                                ex = s.activation(pr[:, kb * 512:(kb + 1) * 512],
                                                  Smask[i % 3][:], AF.Exp,
                                                  bias=mhat_cols[:, i:i + 1], scale=SCALE,
                                                  accum_out=lparts[:, i * 8 + kb:i * 8 + kb + 1])
                            else:
                                bank = main.cons(s, key)
                                ex = s.activation(pr[:, kb * 512:(kb + 1) * 512],
                                                  bank[:], AF.Exp,
                                                  bias=mhat_cols[:, i:i + 1], scale=SCALE,
                                                  accum_out=lparts[:, i * 8 + kb:i * 8 + kb + 1])
                                main.cons_inc(ex, key)
                        M["at_exp"].inc(ex, f"{tag}{h}_{i}")

            @block.vector
            def _(v):
                v.wait_ge(s_consts.h, consts_after_att[tag])
                for h in range(H):
                    if h > 0:
                        v.wait_ge(M["at_pvmm"].ctr.h, M["at_pvmm"].val(f"{tag}{h - 1}"))
                    for grp in range(4):
                        key = f"{tag}vt{h}_{grp}"
                        bank = tpp.cons(v, key)
                        cp = v.tensor_copy(VT_s[:, grp * 1024:(grp + 1) * 1024], bank[:])
                        tpp.cons_inc(cp, key)
                    for kb in range(8):
                        key = f"{tag}kn{h}_{kb}"
                        bank = misc.cons(v, key)
                        cp = v.tensor_copy(kn2row[:, kb * 512:(kb + 1) * 512], bank[0:1, :])
                        misc.cons_inc(cp, key)
                    for qb in range(4):
                        key = f"{tag}qn{h}_{qb}"
                        bank = misc.cons(v, key)
                        cp = v.tensor_copy(qn2row[:, qb * 512:(qb + 1) * 512], bank[0:1, :])
                        misc.cons_inc(cp, key)
                    n = selfinc(v_self, v_n, v.nop())
                    v.wait_ge(v_self.h, n)
                    rm = v.reduce_max(knmax[:], kn2row[:], axis=AX.X)
                    M["at_knorm"].inc(rm, f"{tag}{h}")
                    v.wait_ge(M["at_sqrt"].ctr.h, M["at_sqrt"].val(f"{tag}{h}"))
                    mr = v.tensor_scalar(mrow[:], qn2row[:], knmax[0:1, 0:1],
                                         -SCALE * 1.02, ALU.mult, ALU.mult)
                    M["at_mrow"].inc(mr, f"{tag}{h}")
                    key = f"{tag}mh{h}"
                    bank = misc.cons(v, key)
                    if h > 0:
                        v.wait_ge(M["at_exp"].ctr.h, M["at_exp"].val(f"{tag}{h - 1}_{NQT - 1}"))
                    cp = v.tensor_copy(mhat_cols[:], bank[:, 0:NQT])
                    M["at_mhat"].inc(cp, f"{tag}{h}")
                    v.wait_ge(M["at_mhat"].ctr.h, M["at_mhat"].val(f"{tag}{h}"))
                    misc.cons_inc(v.nop(), key)
                    for i in range(NQT + 2):
                        if i < NQT:
                            Pi = 512 * (i // 2 + 1)
                            nb = Pi // 512
                            key = f"{tag}S{h}_{i}_{nb - 1}"
                            bank = main.cons(v, key)
                            if i >= 3:
                                v.wait_ge(M["at_exp"].ctr.h, M["at_exp"].val(f"{tag}{h}_{i - 3}"))
                            elif h > 0:
                                v.wait_ge(M["at_exp"].ctr.h,
                                          M["at_exp"].val(f"{tag}{h - 1}_{NQT - 3 + i}"))
                            stt = v.scalar_tensor_tensor(Smask[i % 3][:], bank[:], 1.0,
                                                         cm_s[:, i * 512:(i + 1) * 512],
                                                         ALU.mult, ALU.add)
                            M["at_smask"].inc(stt, f"{tag}{h}_{i}")
                            v.wait_ge(M["at_smask"].ctr.h, M["at_smask"].val(f"{tag}{h}_{i}"))
                            main.cons_inc(v.nop(), key)
                            v.wait_ge(M["at_exp"].ctr.h, M["at_exp"].val(f"{tag}{h}_{i}"))
                            if h > 0:
                                v.wait_ge(M["at_prowsc"].ctr.h,
                                          M["at_prowsc"].val(f"{tag}{h - 1}_{i}"))
                            rs = v.reduce_sum(l_cols[:, i:i + 1], lparts[:, i * 8:i * 8 + nb], axis=AX.X)
                            n = selfinc(v_self, v_n, rs)
                            v.wait_ge(v_self.h, n)
                            rc = v.reciprocal(linv_cols[:, i:i + 1], l_cols[:, i:i + 1])
                            M["at_linv"].inc(rc, f"{tag}{h}_{i}")
                            v.wait_ge(M["at_linv"].ctr.h, M["at_linv"].val(f"{tag}{h}_{i}"))
                            pr = Prow[i % 3]
                            rsc = v.tensor_scalar(pr[:, 0:Pi], pr[:, 0:Pi],
                                                  linv_cols[:, i:i + 1], None, ALU.mult)
                            M["at_prowsc"].inc(rsc, f"{tag}{h}_{i}")
                        if i >= 2:
                            iq = i - 2
                            Pq = 512 * (iq // 2 + 1)
                            nb128 = Pq // 128
                            if iq >= 2:
                                v.wait_ge(M["at_pvp"].ctr.h,
                                          M["at_pvp"].val(f"{tag}{h}_{iq // 2 - 1}"))
                            for grp in range((nb128 + 7) // 8):
                                key = f"{tag}pt{h}_{iq}_{grp}"
                                bank = tpp.cons(v, key)
                                nj = min(8, nb128 - grp * 8)
                                cp = v.tensor_copy(
                                    PT_s[:].rearrange("p (g two n) -> p g two n", two=2, n=128)[
                                        :, grp * 8:grp * 8 + nj, iq % 2, :],
                                    bank[:, 0:nj * 128].rearrange("p (j n) -> p j n", n=128))
                                tpp.cons_inc(cp, key)
                            dc, dv_ = tpp.done_val(key)
                            v.wait_ge(dc.h, dv_)
                            M["at_pt"].inc(v.nop(), f"{tag}{h}_{iq}")
                            if iq % 2 == 1:
                                jp = iq // 2
                                key = f"{tag}pv{h}_{jp}"
                                bank = alt.cons(v, key)
                                if jp == 0 and h >= 2:
                                    v.wait_ge(st_attv.h, 16 * (atv0 + h - 1))
                                ev = v.tensor_copy(av_big[h % 2][:, jp * 256:(jp + 1) * 256],
                                                   bank[:, 0:256])
                                M["at_av"].inc(ev, f"{tag}{h}_{jp}")
                                v.wait_ge(M["at_av"].ctr.h, M["at_av"].val(f"{tag}{h}_{jp}"))
                                alt.cons_inc(v.nop(), key)

            @block.sync
            def _(sy):
                att_store(sy, H - 1)
            return stX

        consts_after_att = {"B": CONSTS_A + 16, "D": CONSTS_A + 32}
        ATV_BASE = {"B": 0, "D": 16}

        ATV_TOT = 16
        if upto_i >= 1:
            stA.close()
            stB = emit_attention("B", 3, 4, 32, attv, stw_pairs(A_ST_TOTAL), 0)
            if "attv" in dbg and upto_i <= 2:
                @block.sync
                def _(sy):
                    sy.wait_ge(st_attv.h, 16 * ATV_TOT)
                    sy.dma_start(out=dbg["attv"][:], in_=attv[:]).then_inc(st_dbg.h, 16)

        if upto_i >= 2:
            stB.close()
            stC = contextlib.ExitStack()
            SC = lambda n, s, d: stC.enter_context(nc.sbuf_tensor(n, s, d))
            fmbufC = SC("fmbufC", [128, NCH * NT], BF16)
            fmvC = fmbufC[:].rearrange("p (c n) -> p c n", c=NCH)
            sqC = [SC(f"sqC{i}", [128, C], F32) for i in range(2)]
            xhatC = [SC(f"xhatC{i}", [128, C], BF16) for i in range(2)]
            x1t = [SC(f"x1t{i}", [128, C], F32) for i in range(2)]
            xft = [SC(f"xft{i}", [128, C], F32) for i in range(2)]
            xin.bufs = [SC(f"xinC{i}", [128, C], F32) for i in range(2)]
            slab.bufs = [SC(f"slabC{i}", [128, NCH * 128], BF16) for i in range(3)]
            wrow.bufs = [SC(f"wrowC{i}", [128, C], BF16) for i in range(3)]
            avst.bufs = [SC(f"avstC{i}", [128, C], BF16) for i in range(2)]

            xb0, avb0, wrb0, slb0 = xin.nfill, avst.nfill, wrow.nfill, slab.nfill
            for tt in range(NQT):
                M["e_acc"].idx(f"C{tt}")
                M["la_sq"].idx(f"C{tt}")
                M["la_stat"].idx(f"C{tt}")
                M["la_sd"].idx(f"C{tt}")
                M["c_stage"].idx(f"Cxf{tt}")
                M["la_sq"].idx(f"C{tt}b")
                M["la_stat"].idx(f"C{tt}b")
                M["la_sd"].idx(f"C{tt}b")
                M["la_xhat"].idx(f"C{tt}")
                M["la_fmT"].idx(f"C{tt}")
                M["la_fm"].idx(f"C{tt}")
            for oi in range(48):
                M["c_stage"].idx(f"Cq{oi}")
            C_ATT_B_TOTAL = 16

            def c_xstore(sy, tt):
                sy.wait_ge(M["e_acc"].ctr.h, M["e_acc"].val(f"C{tt}"))
                sy.wait_ge(st_x1.h, 16 * tt)
                sy.dma_start(out=x1_hbm[tt * 128:(tt + 1) * 128, :],
                             in_=x1t[tt % 2][:]).then_inc(st_x1.h, 16)
                sy.wait_ge(M["c_stage"].ctr.h, M["c_stage"].val(f"Cxf{tt}"))
                sy.wait_ge(st_xf.h, 16 * tt)
                sy.dma_start(out=xf_hbm[tt * 128:(tt + 1) * 128, :],
                             in_=xft[tt % 2][:]).then_inc(st_xf.h, 16)

            def c_qstore(sy, oi):
                sel, head = divmod(oi, 16)
                stg = stage[oi % 3]
                sy.wait_ge(M["c_stage"].ctr.h, M["c_stage"].val(f"Cq{oi}"))
                stw_pre(sy, 48 + oi)
                if sel == 0:
                    d = sy.dma_start(out=agin[head * 128:(head + 1) * 128, :], in_=stg[:])
                elif sel == 1:
                    d = sy.dma_start(out=agin[C + head * 128:C + (head + 1) * 128, :], in_=stg[:])
                else:
                    d = sy.dma_start(out=qbuf[head * 128:(head + 1) * 128, :], in_=stg[:])
                stw_inc(d, 48 + oi)

            n_dbgA = (sum(1 for k in ("agin", "q", "agout") if k in dbg)
                      if upto_i <= 1 else
                      sum(1 for k in ("agout",) if k in dbg))

            @block.sync
            def _(sy):
                sy.wait_ge(st_attv.h, 16 * C_ATT_B_TOTAL)
                sy.wait_ge(s_ag.h, 2)
                if n_dbgA:
                    sy.wait_ge(st_dbg.h, 16 * n_dbgA)
                for tt in range(NQT):
                    def fillav(eng, buf, ld, tt=tt):
                        eng.dma_start(
                            out=buf[:].rearrange("p (c n) -> p c n", c=NCH),
                            in_=attv[:, tt * 128:(tt + 1) * 128]
                            .rearrange("(c p) n -> p c n", p=128)
                        ).then_inc(ld.h, 16)
                    avst.load(sy, fillav)
                    def fillx(eng, buf, ld, tt=tt):
                        eng.dma_start(out=buf[:], in_=x_in[tt * 128:(tt + 1) * 128, :]
                                      ).then_inc(ld.h, 16)
                    xin.load(sy, fillx)
                    for c in range(NCH):
                        def fillw(eng, buf, ld, c=c):
                            eng.dma_start(out=buf[:], in_=bview("woutT")[c * 128:(c + 1) * 128, :]
                                          ).then_inc(ld.h, 16)
                        wrow.load(sy, fillw)
                    if tt >= 1:
                        c_xstore(sy, tt - 1)
                c_xstore(sy, NQT - 1)
                sy.wait_ge(s_ag.h, 4)
                Qst.wait_all(sy, 16)
                for oi in range(48):
                    nm, o = [("wk", oi), ("wv", oi - 16), ("wq", oi - 32)][oi // 16]
                    def fill(eng, buf, ld, nm=nm, o=o):
                        eng.dma_start(out=buf[:], in_=bview(nm, (o,))).then_inc(ld.h, 16)
                    slab.load(sy, fill)
                    if oi >= 2:
                        c_qstore(sy, oi - 2)
                c_qstore(sy, 46)
                c_qstore(sy, 47)

            @block.tensor
            def _(t):
                t.wait_ge(s_cconst.h, 48)
                for tt in range(NQT):
                    avst.wait(t, avb0 + tt)
                    av = avst.buf(avb0 + tt)
                    avv = av[:].rearrange("p (c n) -> p c n", c=NCH)
                    keys = [f"Cop{tt}_{j}" for j in range(4)]
                    banks = [main.prod(t, keys[0]), main.prod(t, keys[1]),
                             alt.prod(t, keys[2]), alt.prod(t, keys[3])]
                    for c in range(NCH):
                        wrow.wait(t, wrb0 + tt * NCH + c)
                        wr = wrow.buf(wrb0 + tt * NCH + c)
                        for j in range(4):
                            mmw = t.matmul(banks[j][:], avv[:, c, :], wr[:, j * 512:(j + 1) * 512],
                                           start=(c == 0), stop=False)
                        wrow.release_on(mmw)
                        wrow.ncons += 1
                    for j in range(4):
                        mm = t.matmul(banks[j][:], ones_row_bf_s[:],
                                      bout_bf[0:1, j * 512:(j + 1) * 512],
                                      start=False, stop=True)
                        (main if j < 2 else alt).prod_inc(mm, keys[j])
                    t.wait_ge(alt.fill.ctr.h, alt.fill.val(keys[3]))
                    avst.release_on(t.nop())
                    avst.ncons += 1
                    t.wait_ge(M["la_xhat"].ctr.h, M["la_xhat"].val(f"C{tt}"))
                    for half in range(2):
                        key = f"Cfm{tt}_{half}"
                        bank = tpp.prod(t, key)
                        for j in range(8):
                            c = half * 8 + j
                            mm = t.transpose(bank[:, j * 128:(j + 1) * 128],
                                             xhatC[tt % 2][:, c * 128:(c + 1) * 128], id_bf_s[:])
                        tpp.prod_inc(mm, key)
                    t.wait_ge(tpp.fill.ctr.h, tpp.fill.val(key))
                    M["la_fmT"].inc(t.nop(), f"C{tt}")
                t.wait_ge(M["la_fm"].ctr.h, M["la_fm"].val(f"C{NQT - 1}"))
                for oi in range(48):
                    slab.wait(t, slb0 + oi)
                    sb = slab.buf(slb0 + oi)
                    for mt in range(4):
                        key = f"Cqkv{oi}_{mt}"
                        bank = main.prod(t, key)
                        for k in range(NCH):
                            mm = t.matmul(bank[:], sb[:, k * 128:(k + 1) * 128],
                                          fmvC[:, k, mt * 512:(mt + 1) * 512],
                                          start=(k == 0), stop=(k == NCH - 1))
                        main.prod_inc(mm, key)
                    t.wait_ge(main.fill.ctr.h, main.fill.val(key))
                    slab.release_on(t.nop())
                    slab.ncons += 1

            @block.scalar
            def _(s):
                for tt in range(NQT):
                    s.wait_ge(M["e_acc"].ctr.h, M["e_acc"].val(f"C{tt}"))
                    if tt >= 2:
                        s.wait_ge(M["la_stat"].ctr.h, M["la_stat"].val(f"C{tt - 2}b"))
                    a = s.activation(sqC[tt % 2][:], x1t[tt % 2][:], AF.Square)
                    M["la_sq"].inc(a, f"C{tt}")
                    s.wait_ge(M["la_stat"].ctr.h, M["la_stat"].val(f"C{tt}"))
                    a = s.activation(stat[:, 6:7], stat[:, 5:6], AF.Sqrt, bias=eps_s[:, 0:1])
                    M["la_sd"].inc(a, f"C{tt}")
                    s.wait_ge(M["c_stage"].ctr.h, M["c_stage"].val(f"Cxf{tt}"))
                    a = s.activation(sqC[tt % 2][:], xft[tt % 2][:], AF.Square)
                    M["la_sq"].inc(a, f"C{tt}b")
                    s.wait_ge(M["la_stat"].ctr.h, M["la_stat"].val(f"C{tt}b"))
                    a = s.activation(stat[:, 14:15], stat[:, 13:14], AF.Sqrt, bias=eps_s[:, 1:2])
                    M["la_sd"].inc(a, f"C{tt}b")

            @block.vector
            def _(v):
                v.wait_ge(s_cconst.h, 48)
                for tt in range(NQT):
                    xin.wait(v, xb0 + tt)
                    xtile = xin.buf(xb0 + tt)
                    if tt >= 2:
                        v.wait_ge(st_x1.h, 16 * (tt - 1))
                    keys = [f"Cop{tt}_{j}" for j in range(4)]
                    for j in range(4):
                        bank = (main if j < 2 else alt).cons(v, keys[j])
                        ev = v.scalar_tensor_tensor(x1t[tt % 2][:, j * 512:(j + 1) * 512],
                                                    bank[:], 1.0,
                                                    xtile[:, j * 512:(j + 1) * 512],
                                                    ALU.mult, ALU.add)
                        if j == 3:
                            M["e_acc"].inc(ev, f"C{tt}")
                            v.wait_ge(M["e_acc"].ctr.h, M["e_acc"].val(f"C{tt}"))
                            (main if j < 2 else alt).cons_inc(v.nop(), keys[j])
                            xin.release_on(v.nop())
                            xin.ncons += 1
                        else:
                            (main if j < 2 else alt).cons_inc(ev, keys[j])
                    v.wait_ge(M["la_sq"].ctr.h, M["la_sq"].val(f"C{tt}"))
                    v.reduce_sum(stat[:, 0:1], x1t[tt % 2][:], axis=AX.X)
                    r2 = v.reduce_sum(stat[:, 1:2], sqC[tt % 2][:], axis=AX.X)
                    n = selfinc(v_self, v_n, r2)
                    v.wait_ge(v_self.h, n)
                    v.tensor_scalar(stat[:, 2:3], stat[:, 0:1], 1.0 / C, None, ALU.mult)
                    t2 = v.tensor_scalar(stat[:, 3:4], stat[:, 1:2], 1.0 / C, None, ALU.mult)
                    n = selfinc(v_self, v_n, t2)
                    v.wait_ge(v_self.h, n)
                    t3 = v.tensor_mul(stat[:, 4:5], stat[:, 2:3], stat[:, 2:3])
                    n = selfinc(v_self, v_n, t3)
                    v.wait_ge(v_self.h, n)
                    t5 = v.tensor_sub(stat[:, 5:6], stat[:, 3:4], stat[:, 4:5])
                    M["la_stat"].inc(t5, f"C{tt}")
                    v.wait_ge(M["la_sd"].ctr.h, M["la_sd"].val(f"C{tt}"))
                    rc = v.reciprocal(stat[:, 7:8], stat[:, 6:7])
                    n = selfinc(v_self, v_n, rc)
                    v.wait_ge(v_self.h, n)
                    if tt >= 2:
                        v.wait_ge(st_xf.h, 16 * (tt - 1))
                    xfe = v.tensor_scalar(xft[tt % 2][:], x1t[tt % 2][:],
                                          stat[:, 2:3], stat[:, 7:8], ALU.subtract, ALU.mult)
                    M["c_stage"].inc(xfe, f"Cxf{tt}")
                    v.wait_ge(M["la_sq"].ctr.h, M["la_sq"].val(f"C{tt}b"))
                    r2 = v.reduce_sum(stat[:, 12:13], sqC[tt % 2][:], axis=AX.X)
                    n = selfinc(v_self, v_n, r2)
                    v.wait_ge(v_self.h, n)
                    t2 = v.tensor_scalar(stat[:, 13:14], stat[:, 12:13], 1.0 / C, None, ALU.mult)
                    M["la_stat"].inc(t2, f"C{tt}b")
                    v.wait_ge(M["la_sd"].ctr.h, M["la_sd"].val(f"C{tt}b"))
                    rc = v.reciprocal(stat[:, 15:16], stat[:, 14:15])
                    n = selfinc(v_self, v_n, rc)
                    v.wait_ge(v_self.h, n)
                    h2f = v.tensor_scalar(sqC[tt % 2][:], xft[tt % 2][:],
                                          stat[:, 15:16], None, ALU.mult)
                    n = selfinc(v_self, v_n, h2f)
                    v.wait_ge(v_self.h, n)
                    if tt >= 2:
                        v.wait_ge(M["la_fmT"].ctr.h, M["la_fmT"].val(f"C{tt - 2}"))
                    h2b = v.tensor_mul(xhatC[tt % 2][:], sqC[tt % 2][:], ln1_bf[:])
                    M["la_xhat"].inc(h2b, f"C{tt}")
                    for ett in ([tt - 1] if tt >= 1 else []) + ([tt] if tt == NQT - 1 else []):
                        for half in range(2):
                            key = f"Cfm{ett}_{half}"
                            bank = tpp.cons(v, key)
                            cp = v.tensor_copy(
                                fmvC[:, half * 8:(half + 1) * 8, ett * 128:(ett + 1) * 128],
                                bank[:].rearrange("p (j n) -> p j n", j=8))
                            tpp.cons_inc(cp, key)
                        dc, dv_ = tpp.done_val(key)
                        v.wait_ge(dc.h, dv_)
                        M["la_fm"].inc(v.nop(), f"C{ett}")
                for oi in range(48):
                    stg = stage[oi % 3]
                    if oi >= 3:
                        stw_one(v, 48 + oi - 3)
                    for mt in range(4):
                        key = f"Cqkv{oi}_{mt}"
                        bank = main.cons(v, key)
                        ev = v.tensor_copy(stg[:, mt * 512:(mt + 1) * 512], bank[:])
                        n = selfinc(v_self, v_n, ev)
                        v.wait_ge(v_self.h, n)
                        main.cons_inc(v.nop(), key)
                    M["c_stage"].inc(v.nop(), f"Cq{oi}")

            C_ST_TOTAL = 48 + 48

            @block.gpsimd
            def _(g):
                stw_all(g, 48 + 16)
                Kst.wait_all(g, 16)
                g.collective_compute(
                    "AllGather", ALU.bypass,
                    replica_groups=[[0, 1], [2, 3], [4, 5], [6, 7]],
                    ins=[agin[0:C, :]], outs=[agoutK[:]]).then_inc(s_ag.h, 1)
                stw_all(g, 48 + 32)
                Vst.wait_all(g, 16)
                g.collective_compute(
                    "AllGather", ALU.bypass,
                    replica_groups=[[0, 1], [2, 3], [4, 5], [6, 7]],
                    ins=[agin[C:2 * C, :]], outs=[agoutV[:]]).then_inc(s_ag.h, 1)

            if dbg:
                @block.sync
                def _(sy):
                    sy.wait_ge(st_x1.h, 16 * 16)
                    sy.wait_ge(st_xf.h, 16 * 16)
                    if "x1" in dbg:
                        sy.dma_start(out=dbg["x1"][:], in_=x1_hbm[:]).then_inc(st_dbg.h, 16)
                    if "xf" in dbg:
                        sy.dma_start(out=dbg["xf"][:], in_=xf_hbm[:]).then_inc(st_dbg.h, 16)
                    if upto_i == 2:
                        stw_all(sy, 96)
                        if "agin" in dbg:
                            sy.dma_start(out=dbg["agin"][:], in_=agin[:]).then_inc(st_dbg.h, 16)
                        if "q" in dbg:
                            sy.dma_start(out=dbg["q"][:], in_=qbuf[:]).then_inc(st_dbg.h, 16)

        if upto_i >= 3:
            ATV_TOT = 32
            stC.close()
            stD = emit_attention("D", 8, 9, 80, attv,
                                 stw_pairs(C_ST_TOTAL)
                                 + [(st_x1, 16 * 16), (st_xf, 16 * 16)],
                                 avst.nfill)
            if "attv" in dbg:
                @block.sync
                def _(sy):
                    sy.wait_ge(st_attv.h, 16 * 32)
                    sy.dma_start(out=dbg["attv"][:], in_=attv[:]).then_inc(st_dbg.h, 16)

        if upto_i >= 4:
            stE = contextlib.ExitStack()
            SE = lambda n, s, d: stE.enter_context(nc.sbuf_tensor(n, s, d))
            stD.close()
            tfm = SE("tfm", [128, NCH * MTOK], BF16)
            tfv = tfm[:].rearrange("p (c n) -> p c n", c=NCH)
            accs = [SE(f"acc{i}", [128, C], F32) for i in range(4)]
            sqE = SE("sqE", [128, C], F32)
            thatE = SE("thatE", [128, C], BF16)
            hh_t = SE("hh_t", [128, 2 * ISL * 512], BF16)
            hh0 = [SE(f"hh0_{i}", [128, 512], BF16) for i in range(2)]
            sg_t = [SE(f"sg_t{i}", [128, 512], BF16) for i in range(2)]
            cwb = SE("cwb", [128, MTOK], F32)
            logits_s = SE("logits_s", [128, 16], F32)
            el_s = SE("el_s", [128, 16], F32)
            top8_s = SE("top8_s", [128, 8], F32)
            cw_s = SE("cw_s", [128, 16], F32)
            cwT_s = SE("cwT_s", [16, 128], F32)
            avst.bufs = [SE(f"avstE{i}", [128, 8 * MTOK], BF16) for i in range(2)]
            wrow.bufs = [SE(f"wrowE{i}", [128, C], BF16) for i in range(3)]
            xin.bufs = [SE(f"xinE{i}", [128, C], F32) for i in range(2)]
            egst.bufs = [SE(f"egstE{i}", [128, NCH * 128], BF16) for i in range(2)]
            eust.bufs = [SE(f"eustE{i}", [128, NCH * 128], BF16) for i in range(2)]
            edst.bufs = [SE(f"edstE{i}", [128, ISL * 512], BF16) for i in range(2)]
            cwrow.bufs = [SE(f"cwrowE{i}", [1, MTOK], F32) for i in range(2)]

            D_ATT_TOTAL = 256
            EXPERTS = [*range(E_), "sh"]
            GUSEQ = []
            for m_ in range(NMACRO):
                for ei_, e_ in enumerate(EXPERTS):
                    for isl_ in range(ISL if e_ != "sh" else SISL):
                        GUSEQ.append((m_, ei_, isl_))
            GUPOS = {k: i for i, k in enumerate(GUSEQ)}

            def av_loads(sy, mm):
                for k in range(2):
                    def fillav(eng, buf, ld, k=k, mm=mm):
                        eng.dma_start(
                            out=buf[:].rearrange("p (c n) -> p c n", c=8),
                            in_=attv[k * 8 * 128:(k + 1) * 8 * 128,
                                     mm * MTOK:(mm + 1) * MTOK]
                            .rearrange("(c p) n -> p c n", p=128)
                        ).then_inc(ld.h, 16)
                    avst.load(sy, fillav)

            def wo_load(sy, c):
                def fillw(eng, buf, ld, c=c):
                    eng.dma_start(out=buf[:], in_=bview("wo")[c * 128:(c + 1) * 128, :]
                                  ).then_inc(ld.h, 16)
                wrow.load(sy, fillw)

            E_AV0, E_WR0 = avst.nfill, wrow.nfill
            for m in range(NMACRO):
                avb0, xb0, wrb0 = E_AV0 + 2 * m, xin.nfill, E_WR0 + 64 * m
                for tt in range(4):
                    for nm in ["e_acc", "la_sq", "la_stat", "la_sd"]:
                        M[nm].idx(f"E{m}_{tt}" if nm != "e_acc" else f"E{m}_{tt}")
                    for nm in ["e_that", "e_tfm", "e_glog", "e_gexp", "e_gcw", "e_gT"]:
                        M[nm].idx(f"{m}_{tt}")
                for ei, e in enumerate(EXPERTS):
                    nisl = ISL if e != "sh" else SISL
                    if e != "sh":
                        M["e_cwb"].idx(f"{m}_{e}")
                    for isl in range(nisl):
                        M["e_sg"].idx(f"{m}_{ei}_{isl}")
                        M["e_hh"].idx(f"{m}_{ei}_{isl}")
                    for dg in range(4):
                        for tt in range(4):
                            M["e_dn"].idx(f"{m}_{ei}_{dg}_{tt}")
                    M["e_dnmm"].idx(f"{m}_{ei}")
                for tt in range(4):
                    M["e_fin"].idx(f"{m}_{tt}")
                egb0, eub0, edb0, cwb0 = egst.nfill, eust.nfill, edst.nfill, cwrow.nfill

                @block.sync
                def _(sy):
                    if m == 0:
                        sy.wait_ge(st_attv.h, 16 * 32)
                        sy.wait_ge(s_ag.h, 7)
                        sy.dma_start(out=gate_sb[:], in_=bview("gate")
                                     ).then_inc(s_gate.h, 16)
                    def cw_store(tt):
                        sy.wait_ge(M["e_gT"].ctr.h, M["e_gT"].val(f"{m}_{tt}"))
                        if m > 0 and tt == 0:
                            cwrow.wait_all(sy, 16 * m)
                        sy.wait_ge(st_cw.h, 16 * (m * 4 + tt))
                        sy.dma_start(out=cw_dram[:, tt * 128:(tt + 1) * 128],
                                     in_=cwT_s[:]).then_inc(st_cw.h, 16)

                    if m == 0:
                        av_loads(sy, 0)
                    for tt in range(4):
                        gt = m * 4 + tt
                        def fillx(eng, buf, ld, gt=gt):
                            eng.wait_ge(st_xf.h, 16 * (gt + 1))
                            eng.dma_start(out=buf[:], in_=xf_hbm[gt * 128:(gt + 1) * 128, :]
                                          ).then_inc(ld.h, 16)
                        xin.load(sy, fillx)
                        for c in range(NCH):
                            if m >= 1 and tt == 0 and c < 3:
                                continue
                            wo_load(sy, c)
                        if tt >= 1:
                            cw_store(tt - 1)
                    cw_store(3)
                    for ei, e in enumerate(EXPERTS):
                        nisl = ISL if e != "sh" else SISL
                        if e != "sh":
                            def fillcw(eng, buf, ld, e=e):
                                eng.wait_ge(st_cw.h, 16 * 4 * (m + 1))
                                eng.dma_start(out=buf[:], in_=cw_dram[e:e + 1, :]
                                              ).then_inc(ld.h, 16)
                            cwrow.load(sy, fillcw)
                        for isl in range(nisl):
                            nm_g, nm_u = ("eg", "eu") if e != "sh" else ("sg", "su")
                            eidx = (e, isl) if e != "sh" else (isl,)
                            def fillg(eng, buf, ld, nm_g=nm_g, eidx=eidx):
                                eng.dma_start(out=buf[:], in_=bview(nm_g, eidx)
                                              ).then_inc(ld.h, 16)
                            egst.load(sy, fillg)
                            def fillu(eng, buf, ld, nm_u=nm_u, eidx=eidx):
                                eng.dma_start(out=buf[:], in_=bview(nm_u, eidx)
                                              ).then_inc(ld.h, 16)
                            eust.load(sy, fillu)
                        for dg in range(4):
                            nsets = 1 if e != "sh" else 2
                            for half in range(nsets):
                                def filld(eng, buf, ld, e=e, dg=dg, half=half):
                                    if e != "sh":
                                        src = bview("ed4", (e,))
                                        c0 = dg * ISL * 512
                                    else:
                                        src = bview("sd4")
                                        c0 = (dg * SISL + half * ISL) * 512
                                    eng.dma_start(
                                        out=buf[:],
                                        in_=src[:, c0:c0 + ISL * 512]
                                    ).then_inc(ld.h, 16)
                                edst.load(sy, filld)
                    for tt in range(4):
                        gt = m * 4 + tt
                        def fillx1(eng, buf, ld, gt=gt):
                            eng.wait_ge(st_x1.h, 16 * (gt + 1))
                            eng.dma_start(out=buf[:], in_=x1_hbm[gt * 128:(gt + 1) * 128, :]
                                          ).then_inc(ld.h, 16)
                        xin.load(sy, fillx1)
                    if m < NMACRO - 1:
                        av_loads(sy, m + 1)
                        for c in range(3):
                            wo_load(sy, c)
                    for tt in range(4):
                        gt = m * 4 + tt
                        sy.wait_ge(M["e_fin"].ctr.h, M["e_fin"].val(f"{m}_{tt}"))
                        sy.wait_ge(st_out.h, 16 * gt)
                        sy.dma_start(out=out_hbm[gt * 128:(gt + 1) * 128, :],
                                     in_=accs[tt][:]).then_inc(st_out.h, 16)

                @block.tensor
                def _(t):
                    for tt in range(4):
                        keys = [f"Eop{m}_{tt}_{j}" for j in range(4)]
                        banks = [main.prod(t, keys[0]), main.prod(t, keys[1]),
                                 alt.prod(t, keys[2]), alt.prod(t, keys[3])]
                        for c in range(NCH):
                            if tt == 0 and c % 8 == 0:
                                avst.wait(t, avb0 + c // 8)
                            av = avst.buf(avb0 + c // 8)
                            lhsT = av[:, (c % 8) * 512 + tt * 128:
                                      (c % 8) * 512 + tt * 128 + 128]
                            wrow.wait(t, wrb0 + tt * NCH + c)
                            wr = wrow.buf(wrb0 + tt * NCH + c)
                            for j in range(4):
                                mm = t.matmul(banks[j][:], lhsT,
                                              wr[:, j * 512:(j + 1) * 512],
                                              start=(c == 0), stop=(c == NCH - 1))
                            if c < NCH - 1:
                                wrow.release_on(mm)
                                wrow.ncons += 1
                        alt.prod_inc(mm, keys[2])
                        t.wait_ge(alt.fill.ctr.h, alt.fill.val(keys[2]))
                        main.prod_inc(t.nop(), keys[0])
                        main.prod_inc(t.nop(), keys[1])
                        alt.prod_inc(t.nop(), keys[3])
                        wrow.release_on(t.nop())
                        wrow.ncons += 1
                        if tt == 3:
                            avst.release_on(t.nop())
                            avst.release_on(t.nop())
                            avst.ncons += 2
                        t.wait_ge(M["e_that"].ctr.h, M["e_that"].val(f"{m}_{tt}"))
                        for half in range(2):
                            key = f"Efm{m}_{tt}_{half}"
                            bank = tpp.prod(t, key)
                            for j in range(8):
                                c = half * 8 + j
                                mm = t.transpose(bank[:, j * 128:(j + 1) * 128],
                                                 thatE[:, c * 128:(c + 1) * 128], id_bf_s[:])
                            tpp.prod_inc(mm, key)
                        M["la_fmT"].inc(t.nop(), f"E{m}_{tt}")
                        if m == 0 and tt == 0:
                            t.wait_ge(s_gate.h, 16)
                        t.wait_ge(M["e_tfm"].ctr.h, M["e_tfm"].val(f"{m}_{tt}"))
                        key = f"Eg{m}_{tt}"
                        bank = misc.prod(t, key)
                        for k in range(NCH):
                            mm = t.matmul(bank[:, 0:16], tfv[:, k, tt * 128:(tt + 1) * 128],
                                          gate_sb[:, k * 16:(k + 1) * 16],
                                          start=(k == 0), stop=(k == NCH - 1))
                        misc.prod_inc(mm, key)
                        t.wait_ge(M["e_gcw"].ctr.h, M["e_gcw"].val(f"{m}_{tt}"))
                        key = f"EgT{m}_{tt}"
                        bank = misc.prod(t, key)
                        mm = t.transpose(bank[0:16, 0:128], cw_s[:], id_f_s[:])
                        misc.prod_inc(mm, key)
                    for ei, e in enumerate(EXPERTS):
                        nisl = ISL if e != "sh" else SISL
                        if e != "sh":
                            key = f"Ecw{m}_{e}"
                            bank = misc.prod(t, key)
                            cwrow.wait(t, cwb0 + ei)
                            mm = t.matmul(bank[:], ones_row_f_s[:],
                                          cwrow.buf(cwb0 + ei)[:], start=True, stop=True)
                            misc.prod_inc(mm, key)
                            t.wait_ge(misc.fill.ctr.h, misc.fill.val(key))
                            cwrow.release_on(t.nop())
                            cwrow.ncons += 1
                        for isl in range(nisl):
                            gidx = egb0 + sum(ISL if x != "sh" else SISL for x in EXPERTS[:ei]) + isl
                            egst.wait(t, gidx)
                            eust.wait(t, gidx)
                            eg_ = egst.buf(gidx)
                            eu_ = eust.buf(gidx)
                            hhcol = (isl % (2 * ISL)) * 512 if e == "sh" else ((ei % 2) * ISL + isl) * 512
                            if ei >= 2 and isl == 0:
                                t.wait_ge(M["e_dnmm"].ctr.h, M["e_dnmm"].val(f"{m}_{ei - 2}"))
                            elif e == "sh" and isl == ISL and ei >= 1:
                                t.wait_ge(M["e_dnmm"].ctr.h, M["e_dnmm"].val(f"{m}_{ei - 1}"))
                            kg = f"Egu{m}_{ei}_{isl}g"
                            bank = main.prod(t, kg)
                            for k in range(NCH):
                                mm = t.matmul(bank[:], eg_[:, k * 128:(k + 1) * 128],
                                              tfv[:, k, :], start=(k == 0), stop=(k == NCH - 1))
                            main.prod_inc(mm, kg)
                            ku = f"Egu{m}_{ei}_{isl}u"
                            bank = alt.prod(t, ku)
                            for k in range(NCH):
                                mm = t.matmul(bank[:], eu_[:, k * 128:(k + 1) * 128],
                                              tfv[:, k, :], start=(k == 0), stop=(k == NCH - 1))
                            alt.prod_inc(mm, ku)
                            t.wait_ge(alt.fill.ctr.h, alt.fill.val(ku))
                            egst.release_on(t.nop()); egst.ncons += 1
                            eust.release_on(t.nop()); eust.ncons += 1
                        for dg in range(4):
                            nsets = 1 if e != "sh" else 2
                            dsets = []
                            for half in range(nsets):
                                didx = edb0 + sum((4 if x != "sh" else 8) for x in EXPERTS[:ei]) + dg * nsets + half
                                edst.wait(t, didx)
                                dsets.append(edst.buf(didx))
                            for tt in range(4):
                                key = f"Edn{m}_{ei}_{dg}_{tt}"
                                bank = misc.prod(t, key)
                                for isl in range(nisl):
                                    hcol = (isl * 512 if e == "sh"
                                            else ((ei % 2) * ISL + isl) * 512)
                                    t.wait_ge(M["e_hh"].ctr.h,
                                              M["e_hh"].val(f"{m}_{ei}_{isl}"))
                                    ds = dsets[isl // ISL] if e == "sh" else dsets[0]
                                    mm = t.matmul(bank[:],
                                                  hh_t[:, hcol + tt * 128: hcol + tt * 128 + 128],
                                                  ds[:, (isl % ISL) * 512:(isl % ISL + 1) * 512],
                                                  start=(isl == 0), stop=(isl == nisl - 1))
                                misc.prod_inc(mm, key)
                            t.wait_ge(misc.fill.ctr.h, misc.fill.val(key))
                            for half in range(nsets):
                                edst.release_on(t.nop()); edst.ncons += 1
                        M["e_dnmm"].inc(t.nop(), f"{m}_{ei}")

                @block.scalar
                def _(s):
                    for tt in range(4):
                        s.wait_ge(M["e_acc"].ctr.h, M["e_acc"].val(f"E{m}_{tt}"))
                        a = s.activation(sqE[:], accs[tt][:], AF.Square)
                        M["la_sq"].inc(a, f"E{m}_{tt}")
                        s.wait_ge(M["la_stat"].ctr.h, M["la_stat"].val(f"E{m}_{tt}"))
                        a = s.activation(stat[:, 6:7], stat[:, 5:6], AF.Sqrt, bias=eps_s[:, 1:2])
                        M["la_sd"].inc(a, f"E{m}_{tt}")
                        s.wait_ge(M["e_glog"].ctr.h, M["e_glog"].val(f"{m}_{tt}"))
                        a = s.activation(el_s[:], logits_s[:], AF.Exp, bias=stat[:, 8:9])
                        M["e_gexp"].inc(a, f"{m}_{tt}")
                    for ei, e in enumerate(EXPERTS):
                        nisl = ISL if e != "sh" else SISL
                        for isl in range(nisl):
                            kg = f"Egu{m}_{ei}_{isl}g"
                            bank = main.cons(s, kg)
                            if (ei * 100 + isl) >= 2:
                                pass
                            pos = GUPOS[(m, ei, isl)]
                            if pos >= 2:
                                pm, pei, pisl = GUSEQ[pos - 2]
                                s.wait_ge(M["e_hh"].ctr.h,
                                          M["e_hh"].val(f"{pm}_{pei}_{pisl}"))
                            sl = s.activation(sg_t[pos % 2][:], bank[:], AF.Silu)
                            M["e_sg"].inc(sl, f"{m}_{ei}_{isl}")
                            s.wait_ge(M["e_sg"].ctr.h, M["e_sg"].val(f"{m}_{ei}_{isl}"))
                            main.cons_inc(s.nop(), kg)

                @block.vector
                def _(v):
                    v.wait_ge(s_cconst.h, 48)
                    for tt in range(4):
                        gt = m * 4 + tt
                        xin.wait(v, xb0 + tt)
                        xtile = xin.buf(xb0 + tt)
                        if m > 0:
                            v.wait_ge(st_out.h, 16 * ((m - 1) * 4 + tt + 1))
                        keys = [f"Eop{m}_{tt}_{j}" for j in range(4)]
                        for j in range(4):
                            bank = (main if j < 2 else alt).cons(v, keys[j])
                            ev = v.scalar_tensor_tensor(accs[tt][:, j * 512:(j + 1) * 512],
                                                        bank[:], 1.0,
                                                        xtile[:, j * 512:(j + 1) * 512],
                                                        ALU.mult, ALU.add)
                            if j == 3:
                                M["e_acc"].inc(ev, f"E{m}_{tt}")
                                v.wait_ge(M["e_acc"].ctr.h, M["e_acc"].val(f"E{m}_{tt}"))
                                (main if j < 2 else alt).cons_inc(v.nop(), keys[j])
                                xin.release_on(v.nop())
                                xin.ncons += 1
                            else:
                                (main if j < 2 else alt).cons_inc(ev, keys[j])
                        v.wait_ge(M["la_sq"].ctr.h, M["la_sq"].val(f"E{m}_{tt}"))
                        r2 = v.reduce_sum(stat[:, 12:13], sqE[:], axis=AX.X)
                        n = selfinc(v_self, v_n, r2)
                        v.wait_ge(v_self.h, n)
                        t2 = v.tensor_scalar(stat[:, 5:6], stat[:, 12:13], 1.0 / C, None, ALU.mult)
                        M["la_stat"].inc(t2, f"E{m}_{tt}")
                        v.wait_ge(M["la_sd"].ctr.h, M["la_sd"].val(f"E{m}_{tt}"))
                        rc = v.reciprocal(stat[:, 7:8], stat[:, 6:7])
                        n = selfinc(v_self, v_n, rc)
                        v.wait_ge(v_self.h, n)
                        h2f = v.tensor_scalar(sqE[:], accs[tt][:], stat[:, 7:8], None, ALU.mult)
                        n = selfinc(v_self, v_n, h2f)
                        v.wait_ge(v_self.h, n)
                        thb = v.tensor_mul(thatE[:], sqE[:], ln2_bf[:])
                        M["e_that"].inc(thb, f"{m}_{tt}")
                        for half in range(2):
                            key = f"Efm{m}_{tt}_{half}"
                            bank = tpp.cons(v, key)
                            cp = v.tensor_copy(
                                tfv[:, half * 8:(half + 1) * 8, tt * 128:(tt + 1) * 128],
                                bank[:].rearrange("p (j n) -> p j n", j=8))
                            tpp.cons_inc(cp, key)
                        dc, dv_ = tpp.done_val(key)
                        v.wait_ge(dc.h, dv_)
                        M["e_tfm"].inc(v.nop(), f"{m}_{tt}")
                        key = f"Eg{m}_{tt}"
                        bank = misc.cons(v, key)
                        cp = v.tensor_copy(logits_s[:], bank[:, 0:16])
                        n = selfinc(v_self, v_n, cp)
                        v.wait_ge(v_self.h, n)
                        misc.cons_inc(v.nop(), key)
                        rm = v.reduce_max(stat[:, 9:10], logits_s[:], axis=AX.X)
                        n = selfinc(v_self, v_n, rm)
                        v.wait_ge(v_self.h, n)
                        ng = v.tensor_scalar(stat[:, 8:9], stat[:, 9:10], -1.0, None, ALU.mult)
                        M["e_glog"].inc(ng, f"{m}_{tt}")
                        v.wait_ge(M["e_gexp"].ctr.h, M["e_gexp"].val(f"{m}_{tt}"))
                        mx = v.max(top8_s[:], el_s[:])
                        n = selfinc(v_self, v_n, mx)
                        v.wait_ge(v_self.h, n)
                        rs = v.reduce_sum(stat[:, 10:11], top8_s[:, 0:KTOP], axis=AX.X)
                        n = selfinc(v_self, v_n, rs)
                        v.wait_ge(v_self.h, n)
                        rc = v.reciprocal(stat[:, 11:12], stat[:, 10:11])
                        ge = v.tensor_scalar(cw_s[:], el_s[:], top8_s[:, KTOP - 1:KTOP],
                                             None, ALU.is_ge)
                        n = selfinc(v_self, v_n, ge)
                        v.wait_ge(v_self.h, n)
                        cwf = v.scalar_tensor_tensor(cw_s[:], cw_s[:], stat[:, 11:12],
                                                     el_s[:], ALU.mult, ALU.mult)
                        M["e_gcw"].inc(cwf, f"{m}_{tt}")
                        key = f"EgT{m}_{tt}"
                        bank = misc.cons(v, key)
                        if m > 0 or tt > 0:
                            v.wait_ge(st_cw.h, 16 * (m * 4 + tt))
                        cp = v.tensor_copy(cwT_s[:], bank[0:16, 0:128])
                        M["e_gT"].inc(cp, f"{m}_{tt}")
                        v.wait_ge(M["e_gT"].ctr.h, M["e_gT"].val(f"{m}_{tt}"))
                        misc.cons_inc(v.nop(), key)
                    for ei, e in enumerate(EXPERTS):
                        nisl = ISL if e != "sh" else SISL
                        if e != "sh":
                            key = f"Ecw{m}_{e}"
                            bank = misc.cons(v, key)
                            cp = v.tensor_copy(cwb[:], bank[:])
                            M["e_cwb"].inc(cp, f"{m}_{e}")
                            v.wait_ge(M["e_cwb"].ctr.h, M["e_cwb"].val(f"{m}_{e}"))
                            misc.cons_inc(v.nop(), key)
                        for isl in range(nisl):
                            ku = f"Egu{m}_{ei}_{isl}u"
                            bank = alt.cons(v, ku)
                            v.wait_ge(M["e_sg"].ctr.h, M["e_sg"].val(f"{m}_{ei}_{isl}"))
                            pos = GUPOS[(m, ei, isl)]
                            hhx = v.scalar_tensor_tensor(hh0[pos % 2][:], bank[:], 1.0,
                                                         sg_t[pos % 2][:], ALU.mult, ALU.mult)
                            n = selfinc(v_self, v_n, hhx)
                            v.wait_ge(v_self.h, n)
                            alt.cons_inc(v.nop(), ku)
                            hcol = (isl * 512 if e == "sh"
                                    else ((ei % 2) * ISL + isl) * 512)
                            if e != "sh":
                                v.wait_ge(M["e_cwb"].ctr.h, M["e_cwb"].val(f"{m}_{e}"))
                                hw = v.tensor_mul(hh_t[:, hcol:hcol + 512], hh0[pos % 2][:],
                                                  cwb[:])
                            else:
                                hw = v.tensor_copy(hh_t[:, hcol:hcol + 512], hh0[pos % 2][:])
                            M["e_hh"].inc(hw, f"{m}_{ei}_{isl}")
                        for dg in range(4):
                            for tt in range(4):
                                key = f"Edn{m}_{ei}_{dg}_{tt}"
                                bank = misc.cons(v, key)
                                if ei == 0:
                                    v.wait_ge(M["e_that"].ctr.h, M["e_that"].val(f"{m}_{tt}"))
                                dn = v.tensor_add(accs[tt][:, dg * 512:(dg + 1) * 512],
                                                  accs[tt][:, dg * 512:(dg + 1) * 512],
                                                  bank[:])
                                M["e_dn"].inc(dn, f"{m}_{ei}_{dg}_{tt}")
                                v.wait_ge(M["e_dn"].ctr.h, M["e_dn"].val(f"{m}_{ei}_{dg}_{tt}"))
                                misc.cons_inc(v.nop(), key)
                    for tt in range(4):
                        v.wait_ge(M["e_dn"].ctr.h, M["e_dn"].val(f"{m}_{len(EXPERTS) - 1}_3_3"))
                        xin.wait(v, xb0 + 4 + tt)
                        ad = v.tensor_add(accs[tt][:], accs[tt][:], xin.buf(xb0 + 4 + tt)[:])
                        M["e_fin"].inc(ad, f"{m}_{tt}")
                        xin.ncons += 1
                        v.wait_ge(M["e_fin"].ctr.h, M["e_fin"].val(f"{m}_{tt}"))
                        xin.release_on(v.nop())

        @block.sync
        def _(sy):
            if upto_i == 0:
                stw_all(sy, 48)
            elif upto_i == 1:
                sy.wait_ge(st_attv.h, 16 * 16)
            elif upto_i == 2:
                stw_all(sy, 96)
                sy.wait_ge(st_x1.h, 16 * 16)
                sy.wait_ge(st_xf.h, 16 * 16)
            elif upto_i == 3:
                sy.wait_ge(st_attv.h, 16 * 32)
            else:
                sy.wait_ge(st_out.h, 16 * 16)
            if dbg:
                sy.wait_ge(st_dbg.h, 16 * len(dbg))
            if upto_i < 4:
                sy.dma_start(out=out_hbm[0:128, :], in_=x_in[0:128, :]).then_inc(st_out.h, 16)
                sy.wait_ge(st_out.h, 16)

        if upto_i == 0:
            pass
        elif upto_i == 1:
            stB.close()
        elif upto_i == 2:
            stC.close()
        elif upto_i == 3:
            stD.close()
        else:
            stE.close()

    return nc, dbg


def make_inmaps(ins):
    blobs = host_blobs(ins)
    cores, shared = host_percore(ins)
    in_maps = []
    for c in range(8):
        m = dict(
            x=cores[c]["x"],
            ropeA=cores[c]["ropeA"], ropeB=cores[c]["ropeB"],
            cmask=cores[c]["cmask"],
            bqkv_cols=shared["bqkv_cols"], bout_row=shared["bout_row"],
            ln1_tile=shared["ln1_tile"], ln2_tile=shared["ln2_tile"])
        for gk in ("A1", "A2", "G", "U", "D"):
            sh = blobs[gk].size // 8
            m[f"wshard{gk}"] = np.ascontiguousarray(blobs[gk][c * sh:(c + 1) * sh])
        in_maps.append(m)
    return in_maps, cores


_CACHE = {}

def kernel(**inputs):
    if "nc" not in _CACHE:
        _CACHE["nc"] = build(upto="E")[0]
    nc = _CACHE["nc"]
    in_maps, cores = make_inmaps(inputs)
    res = run_bass_kernel_spmd(nc, in_maps, core_ids=list(range(8)))
    full = np.zeros((B, T, C), np.float32)
    for core in range(8):
        full[core // 2][cores[core]["rows"]] = res.results[core]["y"]
    return full
